# version 08
# baseline (speedup 1.0000x reference)
import contextlib
import numpy as np
import ml_dtypes

import concourse.bass as bass
import concourse.mybir as mybir
from concourse.bass_utils import run_bass_kernel_spmd

F32 = mybir.dt.float32
BF16 = mybir.dt.bfloat16
ALU = mybir.AluOpType
AF = mybir.ActivationFunctionType
AX = mybir.AxisListType.X

NCORES = 8
SEQS = 2
T = 2048
D = 1024
NH = 16
DH = 64
LAT = 256
E = 8
HID = 2048
EPS = 1e-5
NEGBIG = -1e30
QC = 512
NQC = T // QC
FC = 1024
NFC = T // FC

ENGINES = ("sync", "tensor", "vector", "scalar", "gpsimd")
GRAN = 512


class View:
    __slots__ = ("ap", "keys")

    def __init__(self, ap, keys):
        self.ap = ap
        self.keys = tuple(keys)


class Arena:
    def __init__(self, nc, es, name, cols, dt):
        self.name = name
        self.t = es.enter_context(nc.sbuf_tensor(name, [128, cols], dt))

    def v(self, off, ncols, rows=(0, 128)):
        ap = self.t[rows[0]:rows[1], off:off + ncols]
        c0, c1 = off // GRAN, (off + ncols - 1) // GRAN
        return View(ap, [(self.name, i) for i in range(c0, c1 + 1)])


def K(x):
    if isinstance(x, View):
        return list(x.keys)
    return [x]


class Builder:
    def __init__(self, nc):
        self.nc = nc
        self.recs = []
        self.lastw = {}
        self.lastr = {}

    def op(self, eng, fn, reads=(), writes=(), dma=False):
        rk = [k for x in reads for k in K(x)]
        wk = [k for x in writes for k in K(x)]
        idx = len(self.recs)
        deps = set()
        for r in rk:
            if r in self.lastw:
                deps.add(self.lastw[r])
        for w in wk:
            if w in self.lastw:
                deps.add(self.lastw[w])
            for i2 in self.lastr.get(w, {}).values():
                deps.add(i2)
        self.recs.append(dict(idx=idx, eng=eng, fn=fn, deps=deps, dma=dma,
                              inc=False, ordn=None, lane_key=None))
        for r in rk:
            self.lastr.setdefault(r, {})[eng] = idx
        for w in wk:
            self.lastw[w] = idx
            self.lastr[w] = {eng: idx}
        return idx

    def finalize_and_emit(self, es):
        nc = self.nc
        recs = self.recs
        NLANE = 8
        qcount = {}
        for rec in recs:
            if rec["dma"]:
                q = rec["eng"]
                n = qcount.get(q, 0)
                qcount[q] = n + 1
                rec["lane_key"] = ("dl", q, n % NLANE)
                rec["ordn"] = n // NLANE + 1
        for rec in recs:
            kept = {}
            for d in rec["deps"]:
                p = recs[d]
                if (p["eng"] == "tensor" and rec["eng"] == "tensor"
                        and not p["dma"] and not rec["dma"]):
                    continue
                key = p["lane_key"] if p["dma"] else (p["eng"], False)
                kept[key] = max(kept.get(key, -1), d)
            rec["edges"] = kept
            for d in kept.values():
                recs[d]["inc"] = True
        ccount = {}
        for rec in recs:
            if rec["dma"]:
                rec["inc"] = True
            elif rec["inc"]:
                key = (rec["eng"], False)
                ccount[key] = ccount.get(key, 0) + 1
                rec["ordn"] = ccount[key]
        sems = {}
        for q, n in qcount.items():
            for l in range(min(n, NLANE)):
                sems[("dl", q, l)] = es.enter_context(nc.semaphore(f"d_{q}_{l}"))
        for key in ccount:
            sems[key] = es.enter_context(nc.semaphore("c_" + key[0]))
        for rec in recs:
            waits = []
            for key, d in rec["edges"].items():
                p = recs[d]
                if p["ordn"] is None:
                    continue
                waits.append((key, sems[key], p["ordn"] * (16 if p["dma"] else 1)))
            if rec["dma"] and rec["ordn"] > 1:
                waits.append((rec["lane_key"], sems[rec["lane_key"]],
                              (rec["ordn"] - 1) * 16))
            rec["waits"] = waits
            if rec["inc"]:
                rec["sem"] = sems[rec["lane_key"] if rec["dma"]
                                  else (rec["eng"], False)]
                rec["iv"] = 16 if rec["dma"] else 1

        block = es.enter_context(nc.Block())
        per_eng = {e: [r for r in recs if r["eng"] == e] for e in ENGINES}

        def mk(eng_name):
            lst = per_eng[eng_name]

            def section(eng):
                hi = {}
                for rec in lst:
                    for key, sem, val in rec["waits"]:
                        if hi.get(key, 0) >= val:
                            continue
                        eng.wait_ge(sem, val)
                        hi[key] = val
                    inst = rec["fn"](eng)
                    if rec["inc"]:
                        inst.then_inc(rec["sem"], rec["iv"])
            return section

        for name in ENGINES:
            if per_eng[name]:
                getattr(block, name)(mk(name))


def _colblk(w):
    Kd, Md = w.shape
    t = w.reshape(Kd // 128, 128, Md // 128, 128).transpose(2, 1, 0, 3)
    return np.ascontiguousarray(t.reshape(Md // 128, 128, Kd))


def _host_inputs(inp):
    f32 = np.float32
    x = np.asarray(inp["L0_attn_x"], f32)
    shared = {}
    shared["wq"] = _colblk(np.asarray(inp["L0_attn_W_q"], f32))
    shared["wdkv"] = _colblk(np.asarray(inp["L0_attn_W_dkv"], f32))
    shared["wuk"] = _colblk(np.asarray(inp["L0_attn_W_uk"], f32))
    shared["wuv"] = _colblk(np.asarray(inp["L0_attn_W_uv"], f32))
    shared["wo"] = _colblk(np.asarray(inp["L0_attn_W_o"], f32))
    shared["latg"] = np.ascontiguousarray(
        np.asarray(inp["L0_attn_ln_gamma"], f32).reshape(2, 128).T)
    shared["latb"] = np.ascontiguousarray(
        np.asarray(inp["L0_attn_ln_beta"], f32).reshape(2, 128).T)
    gw = np.asarray(inp["L0_ffn_gate_W"], f32)
    shared["gwr"] = np.ascontiguousarray(
        gw.reshape(8, 128, 8).transpose(1, 0, 2).reshape(128, 64))
    shared["gbt"] = np.ascontiguousarray(
        np.tile(np.asarray(inp["L0_ffn_gate_b"], f32)[None, :], (128, 1)))
    w1 = np.asarray(inp["L0_ffn_W1"], f32)
    shared["w1h"] = np.stack([_colblk(w1[e]) for e in range(E)])
    b1 = np.asarray(inp["L0_ffn_b1"], f32)
    shared["b1r"] = np.ascontiguousarray(
        b1.reshape(8, 16, 128).transpose(2, 0, 1).reshape(128, 128))
    shared["w2h"] = np.ascontiguousarray(np.asarray(inp["L0_ffn_W2"], f32))
    shared["b2"] = np.ascontiguousarray(np.asarray(inp["L0_ffn_b2"], f32))
    shared["gamb"] = np.ascontiguousarray(
        np.tile(np.asarray(inp["L0_ffn_ln_gamma"], f32)[None, :], (128, 1)))
    shared["betb"] = np.ascontiguousarray(
        np.tile(np.asarray(inp["L0_ffn_ln_beta"], f32)[None, :], (128, 1)))

    pos = np.arange(T, dtype=f32)
    inv = (f32(1.0) / (f32(10000.0) ** (np.arange(32, dtype=f32) / f32(32.0)))).astype(f32)
    ang = pos[None, :] * inv[:, None]
    cos, sin = np.cos(ang).astype(f32), np.sin(ang).astype(f32)
    CS = np.empty((128, T), f32)
    SS = np.empty((128, T), f32)
    for r in range(128):
        rr = r % 64
        if rr < 32:
            CS[r], SS[r] = cos[rr], -sin[rr]
        else:
            CS[r], SS[r] = cos[rr - 32], sin[rr - 32]
    shared["csss"] = np.ascontiguousarray(np.concatenate([CS, SS], axis=1))
    qi = np.arange(128)[:, None]
    kj = np.arange(128)[None, :]
    shared["triA"] = np.where(kj > qi, f32(NEGBIG), f32(0.0)).astype(f32)
    shared["triQ"] = (kj >= qi).astype(ml_dtypes.bfloat16)
    bd4 = np.zeros((4, QC), ml_dtypes.bfloat16)
    for j in range(4):
        bd4[j, j * 128:(j + 1) * 128] = 1
    shared["bd4"] = bd4
    shared["ones4"] = np.ones((4, 128), ml_dtypes.bfloat16)
    shared["identf"] = np.eye(128, dtype=f32)
    shared["identb"] = np.eye(128, dtype=ml_dtypes.bfloat16)
    shared["onesf"] = np.ones((128, 1), f32)
    shared["epsc"] = np.full((128, 1), EPS, f32)
    shared["onesb"] = np.ones((128, 1), ml_dtypes.bfloat16)
    shared["ones1"] = np.ones((1, 128), f32)

    in_maps = []
    for c in range(NCORES):
        d = dict(shared)
        d["x"] = np.ascontiguousarray(x[2 * c:2 * c + 2].transpose(0, 2, 1))
        in_maps.append(d)
    return in_maps


def build_program(dbg=(), phases=('A1', 'A2', 'FFN'), nseq=SEQS):
    nc = bass.Bass("TRN2", target_bir_lowering=False)
    g = {}

    def dram(name, shape, dt=F32, kind="ExternalInput"):
        g[name] = nc.dram_tensor(name, list(shape), dt, kind=kind)

    dram("x", [SEQS, D, T])
    for n, sh in [("wq", [8, 128, 1024]), ("wdkv", [2, 128, 1024]),
                  ("wuk", [8, 128, 256]), ("wuv", [8, 128, 256]),
                  ("wo", [8, 128, 1024]), ("latg", [128, 2]), ("latb", [128, 2]),
                  ("gwr", [128, 64]), ("gbt", [128, 8]),
                  ("w1h", [8, 16, 128, 1024]), ("b1r", [128, 128]),
                  ("w2h", [8, 2048, 1024]), ("b2", [8, 1024]),
                  ("gamb", [128, 1024]), ("betb", [128, 1024]),
                  ("csss", [128, 2 * T]), ("triA", [128, 128]),
                  ("identf", [128, 128]), ("onesf", [128, 1]), ("ones1", [1, 128]),
                  ("epsc", [128, 1])]:
        dram(n, sh)
    for n, sh in [("triQ", [128, 128]), ("bd4", [4, QC]), ("ones4", [4, 128]),
                  ("identb", [128, 128]), ("onesb", [128, 1])]:
        dram(n, sh, BF16)
    dram("out", [SEQS, T, D], F32, kind="ExternalOutput")
    dram("w1s", [8, 16, 128, 1024], BF16, kind="Internal")
    dram("w2s", [8, 2048, 1024], BF16, kind="Internal")
    dram("wqs", [8, 128, 1024], BF16, kind="Internal")
    dram("wos", [8, 128, 1024], BF16, kind="Internal")
    dram("wdkvs", [2, 128, 1024], BF16, kind="Internal")
    dram("wuks", [8, 128, 256], BF16, kind="Internal")
    dram("wuvs", [8, 128, 256], BF16, kind="Internal")
    dram("x1tm", [SEQS, 16, 128, D], F32, kind="Internal")
    dram("ln1s", [SEQS, 8, 128, T], BF16, kind="Internal")

    dbg_bufs = {}
    for name in dbg:
        shape, dt = dbg[name] if isinstance(dbg, dict) else (None, None)
        dram("dbg_" + name, shape, dt, kind="ExternalOutput")
        dbg_bufs[name] = g["dbg_" + name]

    es = contextlib.ExitStack()
    ABF = Arena(nc, es, "ABF", 57344, BF16)
    AF32 = Arena(nc, es, "AF32", 21504 + 512, F32)

    cn = {}
    for n, sh, dt in [("triA", [128, 128], F32), ("identf", [128, 128], F32),
                      ("onesf", [128, 1], F32), ("ones1", [1, 128], F32),
                      ("epsc", [128, 1], F32),
                      ("b1r", [128, 128], F32), ("gbt", [128, 8], F32),
                      ("latg", [128, 2], F32), ("latb", [128, 2], F32),
                      ("triQ", [128, 128], BF16), ("bd4", [4, QC], BF16),
                      ("ones4", [4, 128], BF16), ("identb", [128, 128], BF16), ("onesb", [128, 1], BF16),
                      ("b2", [8, 1024], BF16), ("gw", [128, 64], BF16)]:
        cn[n] = es.enter_context(nc.sbuf_tensor("c_" + n, sh, dt))

    def C(n):
        return View(cn[n][:, :], ["c_" + n])

    def Cs(n, sl):
        return View(cn[n][sl], ["c_" + n])

    def kbf(ft, cols=slice(0, T)):
        return ABF.v(ft * T, T).ap[:, cols], [("ABF", i) for i in range(
            (ft * T) // GRAN, (ft * T + T - 1) // GRAN + 1)]

    def bfv(off, n, rows=(0, 128)):
        return ABF.v(off, n, rows)

    def fv(off, n, rows=(0, 128)):
        return AF32.v(off, n, rows)

    O_KBF, O_VTM, O_WRING, O_S1, O_S2 = 0, 16384, 32768, 38912, 43008
    O_QBF = 51200
    O_HBUF, O_W2T, O_W1R, O_M0T, O_C1T, O_LN1 = 0, 16384, 32768, 36864, 38912, 40960
    O_DSCR = 47104
    O_OFFS = 49152
    F_XA, F_AO, F_CS, F_SS = 0, 4096, 8192, 10240
    F_ROPE, F_RSW, F_SQ, F_LAT, F_STAGE, F_ROWS = 12288, 13312, 13824, 14336, 15360, 16384
    F_SM = 19456
    F_ACC, F_X1T, F_YT, F_GAMB, F_BETB = 0, 16384, 18432, 19456, 20480
    F_FSM = 21504

    b = Builder(nc)

    def DMA(dst, src, reads=(), writes=(), q="sync"):
        dap = dst.ap if isinstance(dst, View) else dst
        sap = src.ap if isinstance(src, View) else src
        rd = list(reads) + ([src] if isinstance(src, View) else [])
        wr = list(writes) + ([dst] if isinstance(dst, View) else [])
        b.op(q, lambda e, d=dap, s=sap: e.dma_start(out=d, in_=s),
             reads=rd, writes=wr, dma=True)

    def MM(out_ap, outk, lhs, rhs, start, stop):
        lap = lhs.ap if isinstance(lhs, View) else lhs
        rap = rhs.ap if isinstance(rhs, View) else rhs
        rd = [x for x in (lhs, rhs)]
        b.op("tensor", lambda e, o=out_ap, l=lap, r=rap, st=start, sp=stop:
             e.matmul(o, l, r, start=st, stop=sp),
             reads=rd, writes=[outk])

    def TR(out_ap, outk, in_v, ident):
        iap = in_v.ap if isinstance(in_v, View) else in_v
        b.op("tensor", lambda e, o=out_ap, i=iap, idn=ident.ap:
             e.transpose(o, i, idn), reads=[in_v, ident], writes=[outk])

    def EV(eng, meth, out, ins, reads=(), writes=(), **kw):
        oap = out.ap if isinstance(out, View) else out
        iaps = [i.ap if isinstance(i, View) else i for i in ins]
        rd = [i for i in ins if isinstance(i, View)] + list(reads)
        wr = ([out] if isinstance(out, View) else []) + list(writes)
        b.op(eng, lambda e, o=oap, ia=iaps, k=kw, m=meth:
             getattr(e, m)(o, *ia, **k), reads=rd, writes=wr)

    tap_keys = []

    def tap(name, src_ap, reads):
        if name in dbg_bufs:
            key = f"tapD_{name}_{len(tap_keys)}"
            tap_keys.append(key)
            b.op("sync", lambda e, d=dbg_bufs[name].ap(), s=src_ap:
                 e.dma_start(out=d, in_=s), reads=list(reads), writes=[key], dma=True)

    ps = {}
    ps["A"] = es.enter_context(nc.psum_tensor("psA", [128, 2048], F32))
    ps["B"] = es.enter_context(nc.psum_tensor("psB", [128, 1024], F32))
    ps["C"] = es.enter_context(nc.psum_tensor("psC", [128, 1024], F32))

    def psA(c0, c1, rows=(0, 128)):
        return ps["A"][rows[0]:rows[1], c0:c1], [f"psA{i}" for i in
                                                 range(c0 // 512, (c1 - 1) // 512 + 1)]

    def psB(c0, c1, rows=(0, 128)):
        return ps["B"][rows[0]:rows[1], c0:c1], [f"psB{i}" for i in
                                                 range(c0 // 512, (c1 - 1) // 512 + 1)]

    def psC(c0, c1, rows=(0, 128)):
        return ps["C"][rows[0]:rows[1], c0:c1], [f"psC{i}" for i in
                                                 range(c0 // 512, (c1 - 1) // 512 + 1)]

    def PV(appair):
        return View(appair[0], appair[1])

    def emit_startup():
        for n in ["triA", "identf", "onesf", "ones1", "epsc", "b1r", "gbt",
                  "latg", "latb", "triQ", "bd4", "ones4", "identb", "onesb"]:
            DMA(View(cn[n][:, :], ["c_" + n]), g[n].ap())
        for nm in ["wdkv", "wuk", "wuv", "wq", "wo"]:
            DMA(View(g[nm + "s"].ap()[:], [nm + "s"]), g[nm].ap()[:], q="gpsimd")
        DMA(View(cn["b2"][:, :], ["c_b2"]), g["b2"].ap(), q="gpsimd")
        DMA(View(cn["gw"][:, :], ["c_gw"]), g["gwr"].ap(), q="gpsimd")
        for e_ in range(E):
            DMA(View(g["w1s"].ap()[e_], [f"w1s{e_}"]), g["w1h"].ap()[e_], q="gpsimd")
            DMA(View(g["w2s"].ap()[e_], [f"w2s{e_}"]), g["w2h"].ap()[e_], q="gpsimd")

    def fm_ln_rows(tiles, n, dim, rowoff):
        nt = len(tiles)
        sq2 = [fv(F_SQ, 512), fv(F_YT, 512)]
        EV("scalar", "activation", sq2[0], [tiles[0]], func=AF.Square)
        for i, tv in enumerate(tiles):
            if i + 1 < nt:
                EV("scalar", "activation", sq2[(i + 1) % 2], [tiles[i + 1]],
                   func=AF.Square)
            oap, ok = psA(1024, 1024 + n, rows=(0, 1))
            MM(oap, ok[0], C("onesf"), tv, i == 0, i == nt - 1)
            oap, ok = psA(1536, 1536 + n, rows=(0, 1))
            MM(oap, ok[0], C("onesf"), sq2[i % 2], i == 0, i == nt - 1)
        mu = fv(F_ROWS + rowoff, n, rows=(0, 1))
        ex2 = fv(F_ROWS + rowoff + n, n, rows=(0, 1))
        r = fv(F_ROWS + rowoff + 2 * n, n, rows=(0, 1))
        b.op("vector", lambda e, o=mu.ap, i=ps["A"][0:1, 1024:1024 + n]:
             e.tensor_scalar_mul(o, i, 1.0 / dim), reads=["psA2"], writes=[mu])
        b.op("vector", lambda e, o=ex2.ap, i=ps["A"][0:1, 1536:1536 + n]:
             e.tensor_scalar_mul(o, i, 1.0 / dim), reads=["psA3"], writes=[ex2])
        b.op("vector", lambda e, o=r.ap, m=mu.ap: e.scalar_tensor_tensor(
            out=o, in0=m, scalar=-1.0, in1=m, op0=ALU.mult, op1=ALU.mult),
            reads=[mu], writes=[r])
        EV("vector", "tensor_tensor", r, [r, ex2], op=ALU.add)
        b.op("scalar", lambda e, o=r.ap: e.activation(o, o, AF.Ln, bias=cn["epsc"][0:1, 0:1], scale=1.0), reads=[r, "c_epsc"], writes=[r])
        EV("scalar", "activation", r, [r], func=AF.Exp, scale=-0.5)
        return mu, r

    def fm_bcast(row, bank):
        oap, ok = psA(bank * 512, bank * 512 + 512)
        MM(oap, ok[0], C("ones1"), row, True, True)
        return PV(psA(bank * 512, bank * 512 + 512))

    def ln_apply_fm(xv, mub, rb, outv, tmpv):
        EV("vector", "tensor_tensor", tmpv, [xv, mub], op=ALU.subtract)
        EV("vector", "tensor_tensor", outv, [tmpv, rb], op=ALU.mult)

    import os
    RS = int(os.environ.get("RS", "9"))
    NRM = int(os.environ.get("NRM", "4"))

    def rope_evac(psv, dst, colsl):
        if RS < 1:
            EV("vector", "tensor_copy", dst, [psv])
            return
        t1 = fv(F_ROPE, 512)
        t2 = fv(F_ROPE + 512, 512)
        rsw = fv(F_RSW, 512)
        csv = View(AF32.t[:, F_CS + colsl.start:F_CS + colsl.stop],
                   AF32.v(F_CS, 2048).keys)
        ssv = View(AF32.t[:, F_SS + colsl.start:F_SS + colsl.stop],
                   AF32.v(F_SS, 2048).keys)
        b.op("vector", lambda e, o=t1.ap, i=psv.ap, c=csv.ap:
             e.scalar_tensor_tensor(out=o, in0=i, scalar=1.0, in1=c,
                                    op0=ALU.mult, op1=ALU.mult),
             reads=[psv, csv], writes=[t1])
        if RS < 2:
            EV("vector", "tensor_copy", dst, [t1])
            return
        import os as _os
        _SWENG = _os.environ.get("SWENG", "vector")
        for dstr, srcr in [(0, 32), (32, 0), (64, 96), (96, 64)]:
            if _SWENG == "vector":
                b.op("vector", lambda e, o=rsw.ap[dstr:dstr + 32, :],
                     i=psv.ap[srcr:srcr + 32, :]: e.tensor_copy(o, i),
                     reads=[psv], writes=[rsw])
            else:
                b.op("scalar", lambda e, o=rsw.ap[dstr:dstr + 32, :],
                     i=psv.ap[srcr:srcr + 32, :]: e.activation(o, i, AF.Copy),
                     reads=[psv], writes=[rsw])
        if RS < 3:
            EV("vector", "tensor_copy", dst, [t1])
            return
        if RS >= 4:
            EV("gpsimd", "tensor_tensor", t2, [rsw, ssv], op=ALU.mult)
        else:
            EV("vector", "tensor_tensor", t2, [rsw, ssv], op=ALU.mult)
        EV("vector", "tensor_tensor", dst, [t1, t2], op=ALU.add)

    def emit_attention(s):
        xs = g["x"].ap()[s]
        for pp in range(2):
            kb = fv(F_SM + pp * 512, 512)
            EV("gpsimd", "memset", kb, [], constant=0.0)
        DMA(fv(F_CS, 2 * T), g["csss"].ap())

        import os
        A1S = int(os.environ.get("A1S", "9"))
        A1C = int(os.environ.get("A1C", "4"))
        for c in range(A1C):
            cs = slice(c * QC, (c + 1) * QC)
            xts = []
            for ft in range(8):
                xt = fv(F_XA + ft * 512, 512)
                DMA(xt, xs[ft * 128:(ft + 1) * 128, cs])
                xts.append(xt)
            if A1S < 1:
                continue
            mu, r = fm_ln_rows(xts, 512, D, 0)
            if A1S < 2:
                continue
            mub = fm_bcast(mu, 0)
            rb = fm_bcast(r, 1)
            ln0 = [bfv(O_S1 + ft * 512, 512) for ft in range(8)]
            tmp = fv(F_ROPE, 512)
            for ft in range(8):
                ln_apply_fm(xts[ft], mub, rb, ln0[ft], tmp)
            if A1S < 3:
                continue
            for of in range(2):
                w = bfv(O_WRING + of * 1024, 1024)
                DMA(w, g["wdkvs"].ap()[of], reads=["wdkvs"])
                for kc in range(8):
                    oap, ok = psB(of * 512, (of + 1) * 512)
                    MM(oap, ok[0], View(w.ap[:, kc * 128:(kc + 1) * 128], w.keys),
                       ln0[kc], kc == 0, kc == 7)
            latf = [fv(F_LAT + of * 512, 512) for of in range(2)]
            for of in range(2):
                EV("vector", "tensor_copy", latf[of], [PV(psB(of * 512, (of + 1) * 512))])
            if A1S < 4:
                continue
            mu2, r2 = fm_ln_rows(latf, 512, LAT, 1536)
            mub2 = fm_bcast(mu2, 0)
            rb2 = fm_bcast(r2, 1)
            for of in range(2):
                t = fv(F_ROPE, 512)
                EV("vector", "tensor_tensor", t, [latf[of], mub2], op=ALU.subtract)
                t2 = fv(F_ROPE + 512, 512)
                EV("vector", "tensor_tensor", t2, [t, rb2], op=ALU.mult)
                latn_t = bfv(O_S2 + of * T, T)
                dstv = View(latn_t.ap[:, cs], latn_t.keys)
                b.op("vector", lambda e, o=dstv.ap, i=t2.ap,
                     g1=cn["latg"][:, of:of + 1], b1=cn["latb"][:, of:of + 1]:
                     e.tensor_scalar(out=o, in0=i, scalar1=g1, scalar2=b1,
                                     op0=ALU.mult, op1=ALU.add),
                     reads=[t2, "c_latg", "c_latb"], writes=[dstv])
            if A1S < 5:
                continue
            kv_list = [("k", "wuks", None)] + ([("v", "wuvs", None)] if A1S >= 6 else [])
            for which, wsname, dsttile in kv_list:
                for of in range(8):
                    wsl = bfv(O_WRING + (of % 4) * 1024, 256)
                    DMA(wsl, g[wsname].ap()[of], reads=[wsname])
                    par = of % 2
                    for kc in range(2):
                        latn_t = bfv(O_S2 + kc * T, T)
                        oap, ok = psB(par * 512, (par + 1) * 512)
                        MM(oap, ok[0], View(wsl.ap[:, kc * 128:(kc + 1) * 128], wsl.keys),
                           View(latn_t.ap[:, cs], latn_t.keys), kc == 0, kc == 1)
                    pv = PV(psB(par * 512, (par + 1) * 512))
                    if which == "k":
                        kt_ = bfv(O_KBF + of * T, T)
                        rope_evac(pv, View(kt_.ap[:, cs], kt_.keys), cs)
                    else:
                        vb = fv(F_AO + par * 512, 512)
                        EV("scalar", "activation", vb, [pv], func=AF.Copy)
                        for blk in range(4):
                            kt = c * 4 + blk
                            tb = blk % 2
                            oap, ok = psC(tb * 512, tb * 512 + 128)
                            b.op("tensor", lambda e, o=oap,
                                 i=vb.ap[:, blk * 128:(blk + 1) * 128],
                                 idn=cn["identf"][:, :]: e.transpose(o, i, idn),
                                 reads=[vb, "c_identf"], writes=[ok[0]])
                            vtm_t = bfv(O_VTM + kt * 1024, 1024)
                            dv = View(vtm_t.ap[:, of * 128:(of + 1) * 128], vtm_t.keys)
                            EV("vector", "tensor_copy", dv, [PV(psC(tb * 512, tb * 512 + 128))])
            if c == 0:
                tap("ln0", ln0[0].ap, [ln0[0]])
                tap("latn", bfv(O_S2, T).ap[:, 0:512], [bfv(O_S2, T)])

        for qc in range(NQC if 'A2' in phases else 0):
            qs = slice(qc * QC, (qc + 1) * QC)
            xts = []
            for ft in range(8):
                xt = fv(F_XA + ft * 512, 512)
                DMA(xt, xs[ft * 128:(ft + 1) * 128, qs])
                xts.append(xt)
            mu, r = fm_ln_rows(xts, 512, D, 0)
            mub = fm_bcast(mu, 0)
            rb = fm_bcast(r, 1)
            ln0 = [bfv(O_S1 + ft * 512, 512) for ft in range(8)]
            tmp = fv(F_ROPE, 512)
            for ft in range(8):
                ln_apply_fm(xts[ft], mub, rb, ln0[ft], tmp)
            for of in range(8):
                w = bfv(O_WRING + (of % 4) * 1024, 1024)
                DMA(w, g["wqs"].ap()[of], reads=["wqs"])
                par = of % 2
                for kc in range(8):
                    oap, ok = psB(par * 512, (par + 1) * 512)
                    MM(oap, ok[0], View(w.ap[:, kc * 128:(kc + 1) * 128], w.keys),
                       ln0[kc], kc == 0, kc == 7)
                qb = bfv(O_QBF + of * 512, 512)
                rope_evac(PV(psB(par * 512, (par + 1) * 512)), qb, qs)
            if qc == 0:
                tap("qbf", bfv(O_QBF, 512).ap, [bfv(O_QBF, 512)])

            prevB = None
            for h in range(NH):
                ga = _head_passA(s, qc, h)
                _interleave(prevB, ga)
                prevB = _head_passB(s, qc, h)
            _interleave(prevB)

            _window_tail(s, qc, qs, xs)

    def _head_passA(s, qc, h):
        par_h = h % 2
        SMo = F_SM + par_h * 512
        hp, ho = h // 2, (h % 2) * 64
        qtile = bfv(O_QBF + hp * 512, 512)
        ktile = bfv(O_KBF + hp * T, T)
        nmax = fv(SMo, 4)
        k0v = fv(SMo + 20, 1)
        k0b = fv(SMo + 21, 1)
        offc2 = fv(SMo + 32, 32)
        for ql in range(4):
            qt = qc * 4 + ql
            ext = (qt + 1) * 128
            lhs = View(qtile.ap[ho:ho + 64, ql * 128:(ql + 1) * 128], qtile.keys)
            nch = (ext + 511) // 512
            for cc in range(nch):
                n0, n1 = cc * 512, min(ext, (cc + 1) * 512)
                oap, ok = psA(n0, n1)
                MM(oap, ok[0], lhs, View(ktile.ap[ho:ho + 64, n0:n1], ktile.keys),
                   True, True)
            dbank = (qt * 128) // 512
            dv = ps["A"][:, qt * 128:ext]
            b.op("vector", lambda e, o=dv, t=cn["triA"][:, :]:
                 e.tensor_tensor(o, o, t, op=ALU.add),
                 reads=[f"psA{dbank}", "c_triA"], writes=[f"psA{dbank}"])
            b.op("vector", lambda e, o=nmax.ap[:, ql:ql + 1], i=ps["A"][:, 0:ext]:
                 e.reduce_max(o, i, axis=AX, negate=True),
                 reads=[f"psA{i}" for i in range(nch)], writes=[nmax])
            yield
        k0rbuf = fv(SMo + 288, 128, rows=(0, 32))
        k0t = fv(SMo + 256, 32)
        b.op("vector", lambda e, o=k0rbuf.ap[0:1, :], i=cn["ones1"][0:1, :],
             sc=nmax.ap[0:1, 0:1]: e.tensor_single_scalar(o, i, sc, op=ALU.mult),
             reads=["c_ones1", nmax], writes=[k0rbuf])
        for j in range(4):
            b.op("vector", lambda e, o=k0t.ap[j * 32:(j + 1) * 32, 0:32],
                 i=k0rbuf.ap[0:32, j * 32:(j + 1) * 32]: e.transpose(o, i),
                 reads=[k0rbuf], writes=[k0t])
        k0v = View(k0t.ap[:, 0:1], k0t.keys)
        b.op("vector", lambda e, o=k0b.ap, i=k0v.ap: e.tensor_scalar_mul(o, i, 0.125),
             reads=[k0v], writes=[k0b])
        b.op("vector", lambda e, o=offc2.ap[:, 0:4], i=nmax.ap, sc=k0v.ap:
             e.tensor_scalar_sub(o, i, sc), reads=[nmax, k0v], writes=[offc2])
        yield
        obuf = fv(SMo + 128, 128, rows=(0, 32))
        for j in range(4):
            b.op("vector", lambda e, o=obuf.ap[0:32, j * 32:(j + 1) * 32],
                 i=offc2.ap[j * 32:(j + 1) * 32, 0:32]: e.transpose(o, i),
                 reads=[offc2], writes=[obuf])
        offsTb = View(obuf.ap[0:4, :], obuf.keys)
        masked = bfv(O_OFFS + par_h * 1024 + 128, 512, rows=(0, 4))
        for j in range(4):
            b.op("gpsimd", lambda e, o=masked.ap[:, j * 128:(j + 1) * 128],
                 i=offsTb.ap, bd=cn["bd4"][:, j * 128:(j + 1) * 128]:
                 e.tensor_tensor(o, i, bd, op=ALU.mult),
                 reads=[offsTb, "c_bd4"], writes=[masked])
        yield

    def _head_passB(s, qc, h):
        par_h = h % 2
        SMo = F_SM + par_h * 512
        hp, ho = h // 2, (h % 2) * 64
        qtile = bfv(O_QBF + hp * 512, 512)
        ktile = bfv(O_KBF + hp * T, T)
        k0b = fv(SMo + 21, 1)
        masked = bfv(O_OFFS + par_h * 1024 + 128, 512, rows=(0, 4))
        nkt = (qc + 1) * 4
        srow = fv(F_ROWS + 2048 + par_h * 512, 512, rows=(0, 1))
        rb64 = fv(F_ROPE + par_h * 512, 512, rows=(0, 64))

        def emit_sT(kt):
            qlo = max(qc * QC, kt * 128)
            n = qc * QC + QC - qlo
            par = kt % 2
            pbap, pbk = psB(par * 512, par * 512 + n)
            MM(pbap, pbk[0],
               View(ktile.ap[ho:ho + 64, kt * 128:(kt + 1) * 128], ktile.keys),
               View(qtile.ap[ho:ho + 64, QC - n:QC], qtile.keys), True, False)
            MM(pbap, pbk[0], C("ones4"),
               View(masked.ap[:, QC - n:QC], masked.keys), False, True)
            return (kt, n, par, pbap, pbk)

        def emit_tail(st):
            kt, n, par, pbap, pbk = st
            pt = bfv(O_S1 + 2048 + par * 512, 512)
            b.op("scalar", lambda e, o=pt.ap[:, 0:n], i=pbap, bi=k0b.ap:
                 e.activation(o, i, AF.Exp, scale=0.125, bias=bi),
                 reads=[pbk[0], k0b], writes=[pt])
            if kt * 128 >= qc * QC:
                b.op("vector", lambda e, o=pt.ap[:, 0:128], t=cn["triQ"][:, :]:
                     e.tensor_tensor(o, o, t, op=ALU.mult),
                     reads=[pt, "c_triQ"], writes=[pt])
            vtm_t = bfv(O_VTM + kt * 1024, 1024)
            avap, avk = psC(512 + QC - n, 512 + QC, rows=(0, 64))
            b.op("tensor", lambda e, o=avap,
                 l=vtm_t.ap[:, hp * 128 + ho:hp * 128 + ho + 64],
                 r=pt.ap[:, 0:n], st=(kt == 0), sp=(kt == nkt - 1):
                 e.matmul(o, l, r, start=st, stop=sp),
                 reads=[vtm_t, pt], writes=["psC1"])
            if NRM >= 1:
                oap, ok = psC(QC - n, QC, rows=(0, 1))
                MM(oap, ok[0], C("onesb"), View(pt.ap[:, 0:n], pt.keys),
                   kt == 0, kt == nkt - 1)

        prev = None
        for kt in range(nkt):
            st = emit_sT(kt)
            if prev is not None:
                emit_tail(prev)
            prev = st
            yield
        emit_tail(prev)
        if NRM >= 2:
            b.op("vector", lambda e, o=srow.ap, i=ps["C"][0:1, 0:QC]:
                 e.tensor_copy(o, i), reads=["psC0"], writes=[srow])
            EV("scalar", "activation", srow, [srow], func=AF.Ln)
            EV("scalar", "activation", srow, [srow], func=AF.Exp, scale=-1.0)
        if NRM >= 3:
            oap, ok = psC(0, QC, rows=(0, 64))
            b.op("tensor", lambda e, o=oap, l=cn["ones1"][0:1, 0:64], r=srow.ap:
                 e.matmul(o, l, r, start=True, stop=True),
                 reads=["c_ones1", srow], writes=["psC0"])
            b.op("vector", lambda e, o=rb64.ap, i=ps["C"][0:64, 0:QC]:
                 e.tensor_copy(o, i), reads=["psC0"], writes=[rb64])
        yield
        afm = bfv(O_S2 + hp * 512, 512)
        if NRM >= 4:
            b.op("vector", lambda e, o=afm.ap[ho:ho + 64, :], i=ps["C"][0:64, 512:1024],
                 rb=rb64.ap: e.scalar_tensor_tensor(
                     out=o, in0=i, scalar=1.0, in1=rb, op0=ALU.mult, op1=ALU.mult),
                 reads=["psC1", rb64], writes=[afm])
        else:
            b.op("vector", lambda e, o=afm.ap[ho:ho + 64, :], i=ps["C"][0:64, 512:1024]:
                 e.tensor_copy(o, i), reads=["psC1"], writes=[afm])
        yield

    def _interleave(*gens):
        live = [g for g in gens if g is not None]
        while live:
            for gi in list(live):
                try:
                    next(gi)
                except StopIteration:
                    live.remove(gi)

    def _window_tail(s, qc, qs, xs):
        aos = []
        for of in range(8):
            w = bfv(O_WRING + (of % 4) * 1024, 1024)
            DMA(w, g["wos"].ap()[of], reads=["wos"])
            par = of % 2
            for kc in range(8):
                afm = bfv(O_S2 + kc * 512, 512)
                oap, ok = psB(par * 512, (par + 1) * 512)
                MM(oap, ok[0], View(w.ap[:, kc * 128:(kc + 1) * 128], w.keys),
                   afm, kc == 0, kc == 7)
            ao = fv(F_AO + of * 512, 512)
            EV("vector", "tensor_copy", ao, [PV(psB(par * 512, (par + 1) * 512))])
            aos.append(ao)
        mu, r = fm_ln_rows(aos, 512, D, 0)
        mub = fm_bcast(mu, 0)
        rb = fm_bcast(r, 1)
        xts = []
        for ft in range(8):
            xt = fv(F_XA + ft * 512, 512)
            DMA(xt, xs[ft * 128:(ft + 1) * 128, qs])
            xts.append(xt)
        for ft in range(8):
            EV("vector", "tensor_tensor", aos[ft], [aos[ft], mub], op=ALU.subtract)
            EV("vector", "tensor_tensor", aos[ft], [aos[ft], rb], op=ALU.mult)
            EV("vector", "tensor_tensor", xts[ft], [xts[ft], aos[ft]], op=ALU.add)
        if qc == 0:
            tap("x1", xts[0].ap, [xts[0]])
        mu1, r1 = fm_ln_rows(xts, 512, D, 0)
        mub1 = fm_bcast(mu1, 0)
        rb1 = fm_bcast(r1, 1)
        for ft in range(8):
            EV("vector", "tensor_tensor", aos[ft], [xts[ft], mub1], op=ALU.subtract)
            lt = bfv(O_S2 + ft * 512, 512)
            EV("vector", "tensor_tensor", lt, [aos[ft], rb1], op=ALU.mult)
            DMA(View(g["ln1s"].ap()[s, ft, :, qs], [f"ln1D{s}_{ft}_{qc}"]), lt)
        stage = fv(F_STAGE, 1024)
        for tt in range(4):
            ttg = qc * 4 + tt
            for ft in range(8):
                par = ft % 2
                oap, ok = psC(par * 512 + 128, par * 512 + 256)
                b.op("tensor", lambda e, o=oap,
                     i=xts[ft].ap[:, tt * 128:(tt + 1) * 128],
                     idn=cn["identf"][:, :]: e.transpose(o, i, idn),
                     reads=[xts[ft], "c_identf"], writes=[ok[0]])
                dv = View(stage.ap[:, ft * 128:(ft + 1) * 128], stage.keys)
                b.op("vector", lambda e, o=dv.ap, i=oap: e.tensor_copy(o, i),
                     reads=[ok[0]], writes=[dv])
            DMA(View(g["x1tm"].ap()[s, ttg], [f"x1tmD{s}_{ttg}"]), stage)

    def emit_ffn(s):
        DMA(fv(F_GAMB, 1024), g["gamb"].ap())
        DMA(fv(F_BETB, 1024), g["betb"].ap())
        accs = [fv(F_ACC + tt * 1024, 1024) for tt in range(16)]
        for tt in range(16):
            EV("gpsimd", "memset", accs[tt], [], constant=0.0)
        ln1t = [bfv(O_LN1 + kc * T, T) for kc in range(8)]
        for kc in range(8):
            for ch in range(NFC):
                DMA(View(ln1t[kc].ap[:, ch * FC:(ch + 1) * FC], ln1t[kc].keys),
                    g["ln1s"].ap()[s, kc, :, ch * FC:(ch + 1) * FC],
                    reads=[f"ln1D{s}_{kc}_{ch * 2}", f"ln1D{s}_{kc}_{ch * 2 + 1}"])
        gb = fv(F_FSM, 8)
        gt = fv(F_FSM + 8, 8)
        grow = fv(F_FSM + 16, 1)
        m0a = fv(F_FSM + 32, 128)
        c1a = fv(F_FSM + 160, 128)
        v0a = fv(F_FSM + 288, 16)
        v1a = fv(F_FSM + 304, 16)
        bnr = fv(F_FSM + 320, 12)
        st2 = fv(F_FSM + 332, 2)
        rstd = fv(F_FSM + 334, 1)
        lnv = fv(F_FSM + 335, 1)
        for tt in range(16):
            for kc in range(8):
                oap, ok = psB(0, 8)
                MM(oap, ok[0],
                   View(ln1t[kc].ap[:, tt * 128:(tt + 1) * 128], ln1t[kc].keys),
                   Cs("gw", (slice(None), slice(kc * 8, (kc + 1) * 8))),
                   kc == 0, kc == 7)
            b.op("vector", lambda e, o=gb.ap, i=ps["B"][:, 0:8],
                 t=cn["gbt"][:, :]: e.tensor_tensor(o, i, t, op=ALU.add),
                 reads=["psB0", "c_gbt"], writes=[gb])
            EV("vector", "reduce_max", grow, [gb], axis=AX)
            EV("vector", "tensor_scalar_sub", gb, [gb, grow])
            EV("scalar", "activation", gb, [gb], func=AF.Exp)
            EV("vector", "reduce_sum", grow, [gb], axis=AX)
            EV("vector", "reciprocal", grow, [grow])
            EV("vector", "tensor_scalar_mul", gb, [gb, grow])
            ov0 = View(v0a.ap[:, tt:tt + 1], v0a.keys)
            EV("vector", "reduce_max", ov0, [gb], axis=AX)
            om0 = View(m0a.ap[:, tt * 8:(tt + 1) * 8], m0a.keys)
            b.op("vector", lambda e, o=om0.ap, i=gb.ap, sc=ov0.ap:
                 e.tensor_single_scalar(o, i, sc, op=ALU.is_ge),
                 reads=[gb, v0a], writes=[m0a])
            b.op("vector", lambda e, o=gt.ap, m=om0.ap, i=gb.ap:
                 e.scalar_tensor_tensor(out=o, in0=m, scalar=NEGBIG, in1=i,
                                        op0=ALU.mult, op1=ALU.add),
                 reads=[m0a, gb], writes=[gt])
            ov1 = View(v1a.ap[:, tt:tt + 1], v1a.keys)
            EV("vector", "reduce_max", ov1, [gt], axis=AX)
            oc1 = View(c1a.ap[:, tt * 8:(tt + 1) * 8], c1a.keys)
            b.op("vector", lambda e, o=oc1.ap, i=gt.ap, sc=ov1.ap:
                 e.tensor_single_scalar(o, i, sc, op=ALU.is_ge),
                 reads=[gt, v1a], writes=[c1a])
            b.op("vector", lambda e, o=oc1.ap, i=oc1.ap, sc=ov1.ap:
                 e.tensor_scalar_mul(o, i, sc), reads=[c1a, v1a], writes=[c1a])
            for src, srck, dsto in [(m0a, "m0", O_M0T), (c1a, "c1", O_C1T)]:
                oap, ok = psB(0, 128, rows=(0, 8))
                b.op("tensor", lambda e, o=oap,
                     i=src.ap[:, tt * 8:(tt + 1) * 8], idn=cn["identf"][:, :]:
                     e.transpose(o, i, idn),
                     reads=[src, "c_identf"], writes=[ok[0]])
                dT = bfv(dsto, 2048, rows=(0, 8))
                b.op("vector", lambda e, o=dT.ap[:, tt * 128:(tt + 1) * 128],
                     i=ps["B"][0:8, 0:128]: e.tensor_copy(o, i),
                     reads=["psB0"], writes=[dT])
        if s == 0:
            tap("gates", m0a.ap, [m0a])

        m0T = bfv(O_M0T, 2048, rows=(0, 8))
        c1T = bfv(O_C1T, 2048, rows=(0, 8))
        w2ts = [bfv(O_W2T + hf * 1024, 1024) for hf in range(16)]
        hts = [bfv(O_HBUF + hf * 1024, 1024) for hf in range(16)]

        yt = fv(F_YT, 1024)

        def tm_rstd(v, pr):
            bnr_p = fv(F_FSM + 336 + pr * 24, 12)
            st2_p = fv(F_FSM + 348 + pr * 24, 2)
            rstd_p = fv(F_FSM + 350 + pr * 24, 1)
            lnv_p = fv(F_FSM + 351 + pr * 24, 1)
            EV("vector", "bn_stats", View(bnr_p.ap[:, 0:6], bnr_p.keys),
               [View(v.ap[:, 0:512], v.keys)])
            EV("vector", "bn_stats", View(bnr_p.ap[:, 6:12], bnr_p.keys),
               [View(v.ap[:, 512:1024], v.keys)])
            EV("vector", "bn_aggr", st2_p, [bnr_p])
            b.op("scalar", lambda e, o=lnv_p.ap, i=st2_p.ap[:, 1:2]:
                 e.activation(o, i, AF.Ln, bias=cn["epsc"][:, 0:1], scale=1.0),
                 reads=[st2_p, "c_epsc"], writes=[lnv_p])
            EV("scalar", "activation", rstd_p, [lnv_p], func=AF.Exp, scale=-0.5)
            return st2_p, rstd_p

        def gen_ychain(ttg):
            pr = ttg % 2
            ytp = yt
            xt = fv(F_X1T + pr * 1024, 1024)
            DMA(xt, g["x1tm"].ap()[s, ttg], reads=[f"x1tmD{s}_{ttg}"])
            st2_p, rstd_p = tm_rstd(xt, pr)
            b.op("vector", lambda e, o=ytp.ap, i=xt.ap, m=st2_p.ap[:, 0:1],
                 rs=rstd_p.ap: e.tensor_scalar(out=o, in0=i, scalar1=m, scalar2=rs,
                                               op0=ALU.subtract, op1=ALU.mult),
                 reads=[xt, st2_p, rstd_p], writes=[ytp])
            EV("vector", "tensor_tensor", ytp, [ytp, accs[ttg]], op=ALU.add)
            yield
            st2_p, rstd_p = tm_rstd(ytp, pr)
            b.op("vector", lambda e, o=ytp.ap, m=st2_p.ap[:, 0:1], rs=rstd_p.ap:
                 e.tensor_scalar(out=o, in0=o, scalar1=m, scalar2=rs,
                                 op0=ALU.subtract, op1=ALU.mult),
                 reads=[ytp, st2_p, rstd_p], writes=[ytp])
            EV("gpsimd", "tensor_tensor", ytp, [ytp, fv(F_GAMB, 1024)], op=ALU.mult)
            EV("gpsimd", "tensor_tensor", ytp, [ytp, fv(F_BETB, 1024)], op=ALU.add)
            yield
            st2_p, rstd_p = tm_rstd(ytp, pr)
            b.op("vector", lambda e, o=ytp.ap, m=st2_p.ap[:, 0:1], rs=rstd_p.ap:
                 e.tensor_scalar(out=o, in0=o, scalar1=m, scalar2=rs,
                                 op0=ALU.subtract, op1=ALU.mult),
                 reads=[ytp, st2_p, rstd_p], writes=[ytp])
            EV("vector", "tensor_tensor", xt, [ytp, xt], op=ALU.add)
            DMA(View(g["out"].ap()[s, ttg * 128:(ttg + 1) * 128, :],
                     [f"outD{s}_{ttg}"]), xt)
            yield

        def gen_block(it, e_, ch, maska):
            for hf in range(4):
                wv = bfv(O_W1R + hf * 1024, 1024)
                DMA(wv, g["w1s"].ap()[e_, hf], reads=[f"w1s{e_}"])
            for hf in range(16):
                wv = bfv(O_W1R + (hf % 4) * 1024, 1024)
                par = hf % 2
                pa0, pk0 = psA(par * 1024, par * 1024 + 512)
                pa1, pk1 = psA(par * 1024 + 512, par * 1024 + 1024)
                for kc in range(8):
                    lhsv = View(wv.ap[:, kc * 128:(kc + 1) * 128], wv.keys)
                    for nh, (pa, pk) in enumerate([(pa0, pk0), (pa1, pk1)]):
                        MM(pa, pk[0], lhsv,
                           View(ln1t[kc].ap[:, ch * FC + nh * 512:
                                            ch * FC + (nh + 1) * 512],
                                ln1t[kc].keys),
                           kc == 0, kc == 7)
                b.op("scalar", lambda e, o=hts[hf].ap,
                     i=ps["A"][:, par * 1024:(par + 1) * 1024],
                     bi=cn["b1r"][:, e_ * 16 + hf:e_ * 16 + hf + 1]:
                     e.activation(o, i, AF.Relu, bias=bi),
                     reads=[pk0[0], pk1[0], "c_b1r"], writes=[hts[hf]])
                if hf + 4 < 16:
                    wv2 = bfv(O_W1R + ((hf + 4) % 4) * 1024, 1024)
                    DMA(wv2, g["w1s"].ap()[e_, hf + 4], reads=[f"w1s{e_}"])
                yield
            for tt in range(8):
                ttg = ch * 8 + tt
                par = tt % 2
                if par == 0:
                    poap = ps["B"]
                    pks = ("psB0", "psB1")
                else:
                    poap = ps["A"][:, 0:1024]
                    pks = ("psA0", "psA1")
                for hf in range(16):
                    lhsv = View(hts[hf].ap[:, tt * 128:(tt + 1) * 128],
                                hts[hf].keys)
                    for nh in range(2):
                        b.op("tensor", lambda e,
                             o=poap[:, nh * 512:(nh + 1) * 512],
                             l=lhsv.ap,
                             r=w2ts[hf].ap[:, nh * 512:(nh + 1) * 512],
                             st=(hf == 0), sp=(hf == 15):
                             e.matmul(o, l, r, start=st, stop=sp),
                             reads=[lhsv, w2ts[hf]], writes=[pks[nh]])
                scol = maska.ap[:, ttg * 8 + e_:ttg * 8 + e_ + 1]
                b.op("vector", lambda e, o=accs[ttg].ap, i=poap[:, 0:1024],
                     sc=scol: e.scalar_tensor_tensor(
                         out=o, in0=i, scalar=sc, in1=o,
                         op0=ALU.mult, op1=ALU.add),
                     reads=[pks[0], pks[1], accs[ttg], maska],
                     writes=[accs[ttg]])
                yield

        def gen_epi(it, ch, maskT):
            for tt in range(8):
                ttg = ch * 8 + tt
                for nh in range(2):
                    oap, ok = psC(nh * 512, (nh + 1) * 512)
                    b.op("tensor", lambda e, o=oap,
                         l=maskT.ap[:, ttg * 128:(ttg + 1) * 128],
                         r=cn["b2"][:, nh * 512:(nh + 1) * 512]:
                         e.matmul(o, l, r, start=True, stop=True),
                         reads=[maskT, "c_b2"], writes=[ok[0]])
                b.op("vector", lambda e, o=accs[ttg].ap, i=ps["C"][:, 0:1024]:
                     e.tensor_tensor(o, o, i, op=ALU.add),
                     reads=["psC0", "psC1", accs[ttg]], writes=[accs[ttg]])
                yield
                if it == 0:
                    for ft in range(8):
                        par = ft % 2
                        oap, ok = psC(par * 512, par * 512 + 128)
                        b.op("tensor", lambda e, o=oap,
                             i=accs[ttg].ap[:, ft * 128:(ft + 1) * 128],
                             idn=cn["identf"][:, :]: e.transpose(o, i, idn),
                             reads=[accs[ttg], "c_identf"], writes=[ok[0]])
                        b.op("vector", lambda e,
                             o=ln1t[ft].ap[:, ttg * 128:(ttg + 1) * 128],
                             i=oap: e.scalar_tensor_tensor(
                                 out=o, in0=i, scalar=0.5,
                                 in1=o, op0=ALU.mult, op1=ALU.add),
                             reads=[ok[0], ln1t[ft]], writes=[ln1t[ft]])
                        if ft == 3:
                            yield
                    b.op("vector", lambda e, o=accs[ttg].ap,
                         sc=v0a.ap[:, ttg:ttg + 1]: e.tensor_scalar_mul(o, o, sc),
                         reads=[accs[ttg], v0a], writes=[accs[ttg]])
                    yield
                else:
                    for _ in gen_ychain(ttg):
                        yield

        for it in range(2):
            maska, maskT = (m0a, m0T) if it == 0 else (c1a, c1T)
            for e_ in range(E):
                for hf in range(16):
                    DMA(w2ts[hf], g["w2s"].ap()[e_, hf * 128:(hf + 1) * 128, :],
                        reads=[f"w2s{e_}"])
                if e_ < E - 1:
                    _interleave(gen_block(it, e_, 0, maska))
                    _interleave(gen_block(it, e_, 1, maska))
                else:
                    _interleave(gen_block(it, e_, 0, maska))
                    if it == 0 and s == 0:
                        tap("acc0", accs[0].ap, [accs[0]])
                    _interleave(gen_epi(it, 0, maskT), gen_block(it, e_, 1, maska))
                    _interleave(gen_epi(it, 1, maskT))

    emit_startup()
    for s in range(nseq):
        if 'A1' in phases:
            emit_attention(s)
        if 'FFN' in phases:
            emit_ffn(s)
    b.op("sync", lambda e: e.nop(),
         reads=[f"outD{s}_{t}" for s in range(SEQS) for t in range(16)] + tap_keys)

    b.finalize_and_emit(es)
    nc._keepalive = es
    return nc


def kernel(**inputs):
    in_maps = _host_inputs(inputs)
    nc = build_program()
    res = run_bass_kernel_spmd(nc, in_maps, core_ids=list(range(NCORES)))
    out = np.concatenate([r["out"] for r in res.results], axis=0)
    return np.ascontiguousarray(out.reshape(16, 2048, 1024))


# version 09
# speedup vs baseline: 1.0250x; 1.0250x over previous
import contextlib
import numpy as np
import ml_dtypes

import concourse.bass as bass
import concourse.mybir as mybir
from concourse.bass_utils import run_bass_kernel_spmd

F32 = mybir.dt.float32
BF16 = mybir.dt.bfloat16
ALU = mybir.AluOpType
AF = mybir.ActivationFunctionType
AX = mybir.AxisListType.X

NCORES = 8
SEQS = 2
T = 2048
D = 1024
NH = 16
DH = 64
LAT = 256
E = 8
HID = 2048
EPS = 1e-5
NEGBIG = -1e30
QC = 512
NQC = T // QC
FC = 1024
NFC = T // FC

ENGINES = ("sync", "tensor", "vector", "scalar", "gpsimd")
GRAN = 512


class View:
    __slots__ = ("ap", "keys")

    def __init__(self, ap, keys):
        self.ap = ap
        self.keys = tuple(keys)


class Arena:
    def __init__(self, nc, es, name, cols, dt):
        self.name = name
        self.t = es.enter_context(nc.sbuf_tensor(name, [128, cols], dt))

    def v(self, off, ncols, rows=(0, 128)):
        ap = self.t[rows[0]:rows[1], off:off + ncols]
        c0, c1 = off // GRAN, (off + ncols - 1) // GRAN
        return View(ap, [(self.name, i) for i in range(c0, c1 + 1)])


def K(x):
    if isinstance(x, View):
        return list(x.keys)
    return [x]


class Builder:
    def __init__(self, nc):
        self.nc = nc
        self.recs = []
        self.lastw = {}
        self.lastr = {}

    def op(self, eng, fn, reads=(), writes=(), dma=False):
        rk = [k for x in reads for k in K(x)]
        wk = [k for x in writes for k in K(x)]
        idx = len(self.recs)
        deps = set()
        for r in rk:
            if r in self.lastw:
                deps.add(self.lastw[r])
        for w in wk:
            if w in self.lastw:
                deps.add(self.lastw[w])
            for i2 in self.lastr.get(w, {}).values():
                deps.add(i2)
        self.recs.append(dict(idx=idx, eng=eng, fn=fn, deps=deps, dma=dma,
                              inc=False, ordn=None, lane_key=None))
        for r in rk:
            self.lastr.setdefault(r, {})[eng] = idx
        for w in wk:
            self.lastw[w] = idx
            self.lastr[w] = {eng: idx}
        return idx

    def finalize_and_emit(self, es):
        nc = self.nc
        recs = self.recs
        NLANE = 8
        qcount = {}
        for rec in recs:
            if rec["dma"]:
                q = rec["eng"]
                n = qcount.get(q, 0)
                qcount[q] = n + 1
                rec["lane_key"] = ("dl", q, n % NLANE)
                rec["ordn"] = n // NLANE + 1
        for rec in recs:
            kept = {}
            for d in rec["deps"]:
                p = recs[d]
                if (p["eng"] == "tensor" and rec["eng"] == "tensor"
                        and not p["dma"] and not rec["dma"]):
                    continue
                key = p["lane_key"] if p["dma"] else (p["eng"], False)
                kept[key] = max(kept.get(key, -1), d)
            rec["edges"] = kept
            for d in kept.values():
                recs[d]["inc"] = True
        ccount = {}
        for rec in recs:
            if rec["dma"]:
                rec["inc"] = True
            elif rec["inc"]:
                key = (rec["eng"], False)
                ccount[key] = ccount.get(key, 0) + 1
                rec["ordn"] = ccount[key]
        sems = {}
        for q, n in qcount.items():
            for l in range(min(n, NLANE)):
                sems[("dl", q, l)] = es.enter_context(nc.semaphore(f"d_{q}_{l}"))
        for key in ccount:
            sems[key] = es.enter_context(nc.semaphore("c_" + key[0]))
        for rec in recs:
            waits = []
            for key, d in rec["edges"].items():
                p = recs[d]
                if p["ordn"] is None:
                    continue
                waits.append((key, sems[key], p["ordn"] * (16 if p["dma"] else 1)))
            if rec["dma"] and rec["ordn"] > 1:
                waits.append((rec["lane_key"], sems[rec["lane_key"]],
                              (rec["ordn"] - 1) * 16))
            rec["waits"] = waits
            if rec["inc"]:
                rec["sem"] = sems[rec["lane_key"] if rec["dma"]
                                  else (rec["eng"], False)]
                rec["iv"] = 16 if rec["dma"] else 1

        block = es.enter_context(nc.Block())
        per_eng = {e: [r for r in recs if r["eng"] == e] for e in ENGINES}

        def mk(eng_name):
            lst = per_eng[eng_name]

            def section(eng):
                hi = {}
                for rec in lst:
                    for key, sem, val in rec["waits"]:
                        if hi.get(key, 0) >= val:
                            continue
                        eng.wait_ge(sem, val)
                        hi[key] = val
                    inst = rec["fn"](eng)
                    if rec["inc"]:
                        inst.then_inc(rec["sem"], rec["iv"])
            return section

        for name in ENGINES:
            if per_eng[name]:
                getattr(block, name)(mk(name))


def _colblk(w):
    Kd, Md = w.shape
    t = w.reshape(Kd // 128, 128, Md // 128, 128).transpose(2, 1, 0, 3)
    return np.ascontiguousarray(t.reshape(Md // 128, 128, Kd))


def _host_inputs(inp):
    f32 = np.float32
    x = np.asarray(inp["L0_attn_x"], f32)
    shared = {}
    shared["wq"] = _colblk(np.asarray(inp["L0_attn_W_q"], f32))
    shared["wdkv"] = _colblk(np.asarray(inp["L0_attn_W_dkv"], f32))
    shared["wuk"] = _colblk(np.asarray(inp["L0_attn_W_uk"], f32))
    shared["wuv"] = _colblk(np.asarray(inp["L0_attn_W_uv"], f32))
    shared["wo"] = _colblk(np.asarray(inp["L0_attn_W_o"], f32))
    shared["latg"] = np.ascontiguousarray(
        np.asarray(inp["L0_attn_ln_gamma"], f32).reshape(2, 128).T)
    shared["latb"] = np.ascontiguousarray(
        np.asarray(inp["L0_attn_ln_beta"], f32).reshape(2, 128).T)
    gw = np.asarray(inp["L0_ffn_gate_W"], f32)
    shared["gwr"] = np.ascontiguousarray(
        gw.reshape(8, 128, 8).transpose(1, 0, 2).reshape(128, 64))
    shared["gbt"] = np.ascontiguousarray(
        np.tile(np.asarray(inp["L0_ffn_gate_b"], f32)[None, :], (128, 1)))
    w1 = np.asarray(inp["L0_ffn_W1"], f32)
    shared["w1h"] = np.stack([_colblk(w1[e]) for e in range(E)])
    b1 = np.asarray(inp["L0_ffn_b1"], f32)
    shared["b1r"] = np.ascontiguousarray(
        b1.reshape(8, 16, 128).transpose(2, 0, 1).reshape(128, 128))
    shared["w2h"] = np.ascontiguousarray(np.asarray(inp["L0_ffn_W2"], f32))
    shared["b2"] = np.ascontiguousarray(np.asarray(inp["L0_ffn_b2"], f32))
    shared["gamb"] = np.ascontiguousarray(
        np.tile(np.asarray(inp["L0_ffn_ln_gamma"], f32)[None, :], (128, 1)))
    shared["betb"] = np.ascontiguousarray(
        np.tile(np.asarray(inp["L0_ffn_ln_beta"], f32)[None, :], (128, 1)))

    pos = np.arange(T, dtype=f32)
    inv = (f32(1.0) / (f32(10000.0) ** (np.arange(32, dtype=f32) / f32(32.0)))).astype(f32)
    ang = pos[None, :] * inv[:, None]
    cos, sin = np.cos(ang).astype(f32), np.sin(ang).astype(f32)
    CS = np.empty((128, T), f32)
    SS = np.empty((128, T), f32)
    for r in range(128):
        rr = r % 64
        if rr < 32:
            CS[r], SS[r] = cos[rr], -sin[rr]
        else:
            CS[r], SS[r] = cos[rr - 32], sin[rr - 32]
    shared["csss"] = np.ascontiguousarray(np.concatenate([CS, SS], axis=1))
    qi = np.arange(128)[:, None]
    kj = np.arange(128)[None, :]
    shared["triA"] = np.where(kj > qi, f32(NEGBIG), f32(0.0)).astype(f32)
    shared["triQ"] = (kj >= qi).astype(ml_dtypes.bfloat16)
    bd4 = np.zeros((4, QC), ml_dtypes.bfloat16)
    for j in range(4):
        bd4[j, j * 128:(j + 1) * 128] = 1
    shared["bd4"] = bd4
    shared["ones4"] = np.ones((4, 128), ml_dtypes.bfloat16)
    shared["identf"] = np.eye(128, dtype=f32)
    shared["identb"] = np.eye(128, dtype=ml_dtypes.bfloat16)
    shared["onesf"] = np.ones((128, 1), f32)
    shared["epsc"] = np.full((128, 1), EPS, f32)
    shared["onesb"] = np.ones((128, 1), ml_dtypes.bfloat16)
    shared["ones1"] = np.ones((1, 128), f32)

    in_maps = []
    for c in range(NCORES):
        d = dict(shared)
        d["x"] = np.ascontiguousarray(x[2 * c:2 * c + 2].transpose(0, 2, 1))
        in_maps.append(d)
    return in_maps


def build_program(dbg=(), phases=('A1', 'A2', 'FFN'), nseq=SEQS):
    nc = bass.Bass("TRN2", target_bir_lowering=False)
    g = {}

    def dram(name, shape, dt=F32, kind="ExternalInput"):
        g[name] = nc.dram_tensor(name, list(shape), dt, kind=kind)

    dram("x", [SEQS, D, T])
    for n, sh in [("wq", [8, 128, 1024]), ("wdkv", [2, 128, 1024]),
                  ("wuk", [8, 128, 256]), ("wuv", [8, 128, 256]),
                  ("wo", [8, 128, 1024]), ("latg", [128, 2]), ("latb", [128, 2]),
                  ("gwr", [128, 64]), ("gbt", [128, 8]),
                  ("w1h", [8, 16, 128, 1024]), ("b1r", [128, 128]),
                  ("w2h", [8, 2048, 1024]), ("b2", [8, 1024]),
                  ("gamb", [128, 1024]), ("betb", [128, 1024]),
                  ("csss", [128, 2 * T]), ("triA", [128, 128]),
                  ("identf", [128, 128]), ("onesf", [128, 1]), ("ones1", [1, 128]),
                  ("epsc", [128, 1])]:
        dram(n, sh)
    for n, sh in [("triQ", [128, 128]), ("bd4", [4, QC]), ("ones4", [4, 128]),
                  ("identb", [128, 128]), ("onesb", [128, 1])]:
        dram(n, sh, BF16)
    dram("out", [SEQS, T, D], F32, kind="ExternalOutput")
    dram("w1s", [8, 16, 128, 1024], BF16, kind="Internal")
    dram("w2s", [8, 2048, 1024], BF16, kind="Internal")
    dram("wqs", [8, 128, 1024], BF16, kind="Internal")
    dram("wos", [8, 128, 1024], BF16, kind="Internal")
    dram("wdkvs", [2, 128, 1024], BF16, kind="Internal")
    dram("wuks", [8, 128, 256], BF16, kind="Internal")
    dram("wuvs", [8, 128, 256], BF16, kind="Internal")
    dram("x1tm", [SEQS, 16, 128, D], F32, kind="Internal")
    dram("ln1s", [SEQS, 8, 128, T], BF16, kind="Internal")

    dbg_bufs = {}
    for name in dbg:
        shape, dt = dbg[name] if isinstance(dbg, dict) else (None, None)
        dram("dbg_" + name, shape, dt, kind="ExternalOutput")
        dbg_bufs[name] = g["dbg_" + name]

    es = contextlib.ExitStack()
    ABF = Arena(nc, es, "ABF", 57344, BF16)
    AF32 = Arena(nc, es, "AF32", 21504 + 512, F32)

    cn = {}
    for n, sh, dt in [("triA", [128, 128], F32), ("identf", [128, 128], F32),
                      ("onesf", [128, 1], F32), ("ones1", [1, 128], F32),
                      ("epsc", [128, 1], F32),
                      ("b1r", [128, 128], F32), ("gbt", [128, 8], F32),
                      ("latg", [128, 2], F32), ("latb", [128, 2], F32),
                      ("triQ", [128, 128], BF16), ("bd4", [4, QC], BF16),
                      ("ones4", [4, 128], BF16), ("identb", [128, 128], BF16), ("onesb", [128, 1], BF16),
                      ("b2", [8, 1024], BF16), ("gw", [128, 64], BF16)]:
        cn[n] = es.enter_context(nc.sbuf_tensor("c_" + n, sh, dt))

    def C(n):
        return View(cn[n][:, :], ["c_" + n])

    def Cs(n, sl):
        return View(cn[n][sl], ["c_" + n])

    def kbf(ft, cols=slice(0, T)):
        return ABF.v(ft * T, T).ap[:, cols], [("ABF", i) for i in range(
            (ft * T) // GRAN, (ft * T + T - 1) // GRAN + 1)]

    def bfv(off, n, rows=(0, 128)):
        return ABF.v(off, n, rows)

    def fv(off, n, rows=(0, 128)):
        return AF32.v(off, n, rows)

    O_KBF, O_VTM, O_WRING, O_S1, O_S2 = 0, 16384, 32768, 38912, 43008
    O_QBF = 51200
    O_HBUF, O_W2T, O_W1R, O_M0T, O_C1T, O_LN1 = 0, 16384, 32768, 36864, 38912, 40960
    O_DSCR = 47104
    O_OFFS = 49152
    F_XA, F_AO, F_CS, F_SS = 0, 4096, 8192, 10240
    F_ROPE, F_RSW, F_SQ, F_LAT, F_STAGE, F_ROWS = 12288, 13312, 13824, 14336, 15360, 16384
    F_SM = 19456
    F_ACC, F_X1T, F_YT, F_GAMB, F_BETB = 0, 16384, 18432, 19456, 20480
    F_FSM = 21504

    b = Builder(nc)

    def DMA(dst, src, reads=(), writes=(), q="sync"):
        dap = dst.ap if isinstance(dst, View) else dst
        sap = src.ap if isinstance(src, View) else src
        rd = list(reads) + ([src] if isinstance(src, View) else [])
        wr = list(writes) + ([dst] if isinstance(dst, View) else [])
        b.op(q, lambda e, d=dap, s=sap: e.dma_start(out=d, in_=s),
             reads=rd, writes=wr, dma=True)

    def MM(out_ap, outk, lhs, rhs, start, stop):
        lap = lhs.ap if isinstance(lhs, View) else lhs
        rap = rhs.ap if isinstance(rhs, View) else rhs
        rd = [x for x in (lhs, rhs)]
        b.op("tensor", lambda e, o=out_ap, l=lap, r=rap, st=start, sp=stop:
             e.matmul(o, l, r, start=st, stop=sp),
             reads=rd, writes=[outk])

    def TR(out_ap, outk, in_v, ident):
        iap = in_v.ap if isinstance(in_v, View) else in_v
        b.op("tensor", lambda e, o=out_ap, i=iap, idn=ident.ap:
             e.transpose(o, i, idn), reads=[in_v, ident], writes=[outk])

    def EV(eng, meth, out, ins, reads=(), writes=(), **kw):
        oap = out.ap if isinstance(out, View) else out
        iaps = [i.ap if isinstance(i, View) else i for i in ins]
        rd = [i for i in ins if isinstance(i, View)] + list(reads)
        wr = ([out] if isinstance(out, View) else []) + list(writes)
        b.op(eng, lambda e, o=oap, ia=iaps, k=kw, m=meth:
             getattr(e, m)(o, *ia, **k), reads=rd, writes=wr)

    tap_keys = []

    def tap(name, src_ap, reads):
        if name in dbg_bufs:
            key = f"tapD_{name}_{len(tap_keys)}"
            tap_keys.append(key)
            b.op("sync", lambda e, d=dbg_bufs[name].ap(), s=src_ap:
                 e.dma_start(out=d, in_=s), reads=list(reads), writes=[key], dma=True)

    ps = {}
    ps["A"] = es.enter_context(nc.psum_tensor("psA", [128, 2048], F32))
    ps["B"] = es.enter_context(nc.psum_tensor("psB", [128, 1024], F32))
    ps["C"] = es.enter_context(nc.psum_tensor("psC", [128, 1024], F32))

    def psA(c0, c1, rows=(0, 128)):
        return ps["A"][rows[0]:rows[1], c0:c1], [f"psA{i}" for i in
                                                 range(c0 // 512, (c1 - 1) // 512 + 1)]

    def psB(c0, c1, rows=(0, 128)):
        return ps["B"][rows[0]:rows[1], c0:c1], [f"psB{i}" for i in
                                                 range(c0 // 512, (c1 - 1) // 512 + 1)]

    def psC(c0, c1, rows=(0, 128)):
        return ps["C"][rows[0]:rows[1], c0:c1], [f"psC{i}" for i in
                                                 range(c0 // 512, (c1 - 1) // 512 + 1)]

    def PV(appair):
        return View(appair[0], appair[1])

    def emit_startup():
        for n in ["triA", "identf", "onesf", "ones1", "epsc", "b1r", "gbt",
                  "latg", "latb", "triQ", "bd4", "ones4", "identb", "onesb"]:
            DMA(View(cn[n][:, :], ["c_" + n]), g[n].ap())
        for nm in ["wdkv", "wuk", "wuv", "wq", "wo"]:
            DMA(View(g[nm + "s"].ap()[:], [nm + "s"]), g[nm].ap()[:], q="gpsimd")
        DMA(View(cn["b2"][:, :], ["c_b2"]), g["b2"].ap(), q="gpsimd")
        DMA(View(cn["gw"][:, :], ["c_gw"]), g["gwr"].ap(), q="gpsimd")
        for e_ in range(E):
            DMA(View(g["w1s"].ap()[e_], [f"w1s{e_}"]), g["w1h"].ap()[e_], q="gpsimd")
            DMA(View(g["w2s"].ap()[e_], [f"w2s{e_}"]), g["w2h"].ap()[e_], q="gpsimd")

    def fm_ln_rows(tiles, n, dim, rowoff):
        nt = len(tiles)
        sq2 = [fv(F_SQ, 512), fv(F_YT, 512)]
        EV("scalar", "activation", sq2[0], [tiles[0]], func=AF.Square)
        for i, tv in enumerate(tiles):
            if i + 1 < nt:
                EV("scalar", "activation", sq2[(i + 1) % 2], [tiles[i + 1]],
                   func=AF.Square)
            oap, ok = psA(1024, 1024 + n, rows=(0, 1))
            MM(oap, ok[0], C("onesf"), tv, i == 0, i == nt - 1)
            oap, ok = psA(1536, 1536 + n, rows=(0, 1))
            MM(oap, ok[0], C("onesf"), sq2[i % 2], i == 0, i == nt - 1)
        mu = fv(F_ROWS + rowoff, n, rows=(0, 1))
        ex2 = fv(F_ROWS + rowoff + n, n, rows=(0, 1))
        r = fv(F_ROWS + rowoff + 2 * n, n, rows=(0, 1))
        b.op("vector", lambda e, o=mu.ap, i=ps["A"][0:1, 1024:1024 + n]:
             e.tensor_scalar_mul(o, i, 1.0 / dim), reads=["psA2"], writes=[mu])
        b.op("vector", lambda e, o=ex2.ap, i=ps["A"][0:1, 1536:1536 + n]:
             e.tensor_scalar_mul(o, i, 1.0 / dim), reads=["psA3"], writes=[ex2])
        b.op("vector", lambda e, o=r.ap, m=mu.ap: e.scalar_tensor_tensor(
            out=o, in0=m, scalar=-1.0, in1=m, op0=ALU.mult, op1=ALU.mult),
            reads=[mu], writes=[r])
        EV("vector", "tensor_tensor", r, [r, ex2], op=ALU.add)
        b.op("scalar", lambda e, o=r.ap: e.activation(o, o, AF.Ln, bias=cn["epsc"][0:1, 0:1], scale=1.0), reads=[r, "c_epsc"], writes=[r])
        EV("scalar", "activation", r, [r], func=AF.Exp, scale=-0.5)
        return mu, r

    def fm_bcast(row, bank):
        oap, ok = psA(bank * 512, bank * 512 + 512)
        MM(oap, ok[0], C("ones1"), row, True, True)
        return PV(psA(bank * 512, bank * 512 + 512))

    def ln_apply_fm(xv, mub, rb, outv, tmpv):
        EV("vector", "tensor_tensor", tmpv, [xv, mub], op=ALU.subtract)
        EV("vector", "tensor_tensor", outv, [tmpv, rb], op=ALU.mult)

    import os
    RS = int(os.environ.get("RS", "9"))
    NRM = int(os.environ.get("NRM", "4"))

    def rope_evac(psv, dst, colsl):
        if RS < 1:
            EV("vector", "tensor_copy", dst, [psv])
            return
        t1 = fv(F_ROPE, 512)
        t2 = fv(F_ROPE + 512, 512)
        rsw = fv(F_RSW, 512)
        csv = View(AF32.t[:, F_CS + colsl.start:F_CS + colsl.stop],
                   AF32.v(F_CS, 2048).keys)
        ssv = View(AF32.t[:, F_SS + colsl.start:F_SS + colsl.stop],
                   AF32.v(F_SS, 2048).keys)
        b.op("vector", lambda e, o=t1.ap, i=psv.ap, c=csv.ap:
             e.scalar_tensor_tensor(out=o, in0=i, scalar=1.0, in1=c,
                                    op0=ALU.mult, op1=ALU.mult),
             reads=[psv, csv], writes=[t1])
        if RS < 2:
            EV("vector", "tensor_copy", dst, [t1])
            return
        import os as _os
        _SWENG = _os.environ.get("SWENG", "vector")
        for dstr, srcr in [(0, 32), (32, 0), (64, 96), (96, 64)]:
            if _SWENG == "vector":
                b.op("vector", lambda e, o=rsw.ap[dstr:dstr + 32, :],
                     i=psv.ap[srcr:srcr + 32, :]: e.tensor_copy(o, i),
                     reads=[psv], writes=[rsw])
            else:
                b.op("scalar", lambda e, o=rsw.ap[dstr:dstr + 32, :],
                     i=psv.ap[srcr:srcr + 32, :]: e.activation(o, i, AF.Copy),
                     reads=[psv], writes=[rsw])
        if RS < 3:
            EV("vector", "tensor_copy", dst, [t1])
            return
        if RS >= 4:
            EV("gpsimd", "tensor_tensor", t2, [rsw, ssv], op=ALU.mult)
        else:
            EV("vector", "tensor_tensor", t2, [rsw, ssv], op=ALU.mult)
        EV("vector", "tensor_tensor", dst, [t1, t2], op=ALU.add)

    def emit_attention(s):
        xs = g["x"].ap()[s]
        for pp in range(2):
            kb = fv(F_SM + pp * 512, 512)
            EV("gpsimd", "memset", kb, [], constant=0.0)
        DMA(fv(F_CS, 2 * T), g["csss"].ap())

        import os
        A1S = int(os.environ.get("A1S", "9"))
        A1C = int(os.environ.get("A1C", "4"))
        for c in range(A1C):
            cs = slice(c * QC, (c + 1) * QC)
            xts = []
            for ft in range(8):
                xt = fv(F_XA + ft * 512, 512)
                DMA(xt, xs[ft * 128:(ft + 1) * 128, cs])
                xts.append(xt)
            if A1S < 1:
                continue
            mu, r = fm_ln_rows(xts, 512, D, 0)
            if A1S < 2:
                continue
            mub = fm_bcast(mu, 0)
            rb = fm_bcast(r, 1)
            ln0 = [bfv(O_S1 + ft * 512, 512) for ft in range(8)]
            tmp = fv(F_ROPE, 512)
            for ft in range(8):
                ln_apply_fm(xts[ft], mub, rb, ln0[ft], tmp)
            if A1S < 3:
                continue
            for of in range(2):
                w = bfv(O_WRING + of * 1024, 1024)
                DMA(w, g["wdkvs"].ap()[of], reads=["wdkvs"])
                for kc in range(8):
                    oap, ok = psB(of * 512, (of + 1) * 512)
                    MM(oap, ok[0], View(w.ap[:, kc * 128:(kc + 1) * 128], w.keys),
                       ln0[kc], kc == 0, kc == 7)
            latf = [fv(F_LAT + of * 512, 512) for of in range(2)]
            for of in range(2):
                EV("vector", "tensor_copy", latf[of], [PV(psB(of * 512, (of + 1) * 512))])
            if A1S < 4:
                continue
            mu2, r2 = fm_ln_rows(latf, 512, LAT, 1536)
            mub2 = fm_bcast(mu2, 0)
            rb2 = fm_bcast(r2, 1)
            for of in range(2):
                t = fv(F_ROPE, 512)
                EV("vector", "tensor_tensor", t, [latf[of], mub2], op=ALU.subtract)
                t2 = fv(F_ROPE + 512, 512)
                EV("vector", "tensor_tensor", t2, [t, rb2], op=ALU.mult)
                latn_t = bfv(O_S2 + of * T, T)
                dstv = View(latn_t.ap[:, cs], latn_t.keys)
                b.op("vector", lambda e, o=dstv.ap, i=t2.ap,
                     g1=cn["latg"][:, of:of + 1], b1=cn["latb"][:, of:of + 1]:
                     e.tensor_scalar(out=o, in0=i, scalar1=g1, scalar2=b1,
                                     op0=ALU.mult, op1=ALU.add),
                     reads=[t2, "c_latg", "c_latb"], writes=[dstv])
            if A1S < 5:
                continue
            kv_list = [("k", "wuks", None)] + ([("v", "wuvs", None)] if A1S >= 6 else [])
            for which, wsname, dsttile in kv_list:
                for of in range(8):
                    wsl = bfv(O_WRING + (of % 4) * 1024, 256)
                    DMA(wsl, g[wsname].ap()[of], reads=[wsname])
                    par = of % 2
                    for kc in range(2):
                        latn_t = bfv(O_S2 + kc * T, T)
                        oap, ok = psB(par * 512, (par + 1) * 512)
                        MM(oap, ok[0], View(wsl.ap[:, kc * 128:(kc + 1) * 128], wsl.keys),
                           View(latn_t.ap[:, cs], latn_t.keys), kc == 0, kc == 1)
                    pv = PV(psB(par * 512, (par + 1) * 512))
                    if which == "k":
                        kt_ = bfv(O_KBF + of * T, T)
                        rope_evac(pv, View(kt_.ap[:, cs], kt_.keys), cs)
                    else:
                        vb = fv(F_AO + par * 512, 512)
                        EV("scalar", "activation", vb, [pv], func=AF.Copy)
                        for blk in range(4):
                            kt = c * 4 + blk
                            tb = blk % 2
                            oap, ok = psC(tb * 512, tb * 512 + 128)
                            b.op("tensor", lambda e, o=oap,
                                 i=vb.ap[:, blk * 128:(blk + 1) * 128],
                                 idn=cn["identf"][:, :]: e.transpose(o, i, idn),
                                 reads=[vb, "c_identf"], writes=[ok[0]])
                            vtm_t = bfv(O_VTM + kt * 1024, 1024)
                            dv = View(vtm_t.ap[:, of * 128:(of + 1) * 128], vtm_t.keys)
                            EV("vector", "tensor_copy", dv, [PV(psC(tb * 512, tb * 512 + 128))])
            if c == 0:
                tap("ln0", ln0[0].ap, [ln0[0]])
                tap("latn", bfv(O_S2, T).ap[:, 0:512], [bfv(O_S2, T)])

        for qc in range(NQC if 'A2' in phases else 0):
            qs = slice(qc * QC, (qc + 1) * QC)
            xts = []
            for ft in range(8):
                xt = fv(F_XA + ft * 512, 512)
                DMA(xt, xs[ft * 128:(ft + 1) * 128, qs])
                xts.append(xt)
            mu, r = fm_ln_rows(xts, 512, D, 0)
            mub = fm_bcast(mu, 0)
            rb = fm_bcast(r, 1)
            ln0 = [bfv(O_S1 + ft * 512, 512) for ft in range(8)]
            tmp = fv(F_ROPE, 512)
            for ft in range(8):
                ln_apply_fm(xts[ft], mub, rb, ln0[ft], tmp)
            for of in range(8):
                w = bfv(O_WRING + (of % 4) * 1024, 1024)
                DMA(w, g["wqs"].ap()[of], reads=["wqs"])
                par = of % 2
                for kc in range(8):
                    oap, ok = psB(par * 512, (par + 1) * 512)
                    MM(oap, ok[0], View(w.ap[:, kc * 128:(kc + 1) * 128], w.keys),
                       ln0[kc], kc == 0, kc == 7)
                qb = bfv(O_QBF + of * 512, 512)
                rope_evac(PV(psB(par * 512, (par + 1) * 512)), qb, qs)
            if qc == 0:
                tap("qbf", bfv(O_QBF, 512).ap, [bfv(O_QBF, 512)])

            prevB = None
            for h in range(NH):
                ga = _head_passA(s, qc, h)
                _interleave(prevB, ga)
                prevB = _head_passB(s, qc, h)
            _interleave(prevB)

            _window_tail(s, qc, qs, xs)

    def _head_passA(s, qc, h):
        par_h = h % 2
        SMo = F_SM + par_h * 512
        hp, ho = h // 2, (h % 2) * 64
        qtile = bfv(O_QBF + hp * 512, 512)
        ktile = bfv(O_KBF + hp * T, T)
        nmax = fv(SMo, 4)
        k0v = fv(SMo + 20, 1)
        k0b = fv(SMo + 21, 1)
        offc2 = fv(SMo + 32, 32)
        for ql in range(4):
            qt = qc * 4 + ql
            ext = (qt + 1) * 128
            lhs = View(qtile.ap[ho:ho + 64, ql * 128:(ql + 1) * 128], qtile.keys)
            nch = (ext + 511) // 512
            for cc in range(nch):
                n0, n1 = cc * 512, min(ext, (cc + 1) * 512)
                oap, ok = psA(n0, n1)
                MM(oap, ok[0], lhs, View(ktile.ap[ho:ho + 64, n0:n1], ktile.keys),
                   True, True)
            b.op("vector", lambda e, o=nmax.ap[:, ql:ql + 1], i=ps["A"][:, 0:ext]:
                 e.reduce_max(o, i, axis=AX, negate=True),
                 reads=[f"psA{i}" for i in range(nch)], writes=[nmax])
            yield
        k0rbuf = fv(SMo + 288, 128, rows=(0, 32))
        k0t = fv(SMo + 256, 32)
        b.op("vector", lambda e, o=k0rbuf.ap[0:1, :], i=cn["ones1"][0:1, :],
             sc=nmax.ap[0:1, 0:1]: e.tensor_single_scalar(o, i, sc, op=ALU.mult),
             reads=["c_ones1", nmax], writes=[k0rbuf])
        for j in range(4):
            b.op("vector", lambda e, o=k0t.ap[j * 32:(j + 1) * 32, 0:32],
                 i=k0rbuf.ap[0:32, j * 32:(j + 1) * 32]: e.transpose(o, i),
                 reads=[k0rbuf], writes=[k0t])
        k0v = View(k0t.ap[:, 0:1], k0t.keys)
        b.op("vector", lambda e, o=k0b.ap, i=k0v.ap: e.tensor_scalar_mul(o, i, 0.125),
             reads=[k0v], writes=[k0b])
        b.op("vector", lambda e, o=offc2.ap[:, 0:4], i=nmax.ap, sc=k0v.ap:
             e.tensor_scalar_sub(o, i, sc), reads=[nmax, k0v], writes=[offc2])
        yield
        obuf = fv(SMo + 128, 128, rows=(0, 32))
        for j in range(4):
            b.op("vector", lambda e, o=obuf.ap[0:32, j * 32:(j + 1) * 32],
                 i=offc2.ap[j * 32:(j + 1) * 32, 0:32]: e.transpose(o, i),
                 reads=[offc2], writes=[obuf])
        offsTb = View(obuf.ap[0:4, :], obuf.keys)
        masked = bfv(O_OFFS + par_h * 1024 + 128, 512, rows=(0, 4))
        for j in range(4):
            b.op("gpsimd", lambda e, o=masked.ap[:, j * 128:(j + 1) * 128],
                 i=offsTb.ap, bd=cn["bd4"][:, j * 128:(j + 1) * 128]:
                 e.tensor_tensor(o, i, bd, op=ALU.mult),
                 reads=[offsTb, "c_bd4"], writes=[masked])
        yield

    def _head_passB(s, qc, h):
        par_h = h % 2
        SMo = F_SM + par_h * 512
        hp, ho = h // 2, (h % 2) * 64
        qtile = bfv(O_QBF + hp * 512, 512)
        ktile = bfv(O_KBF + hp * T, T)
        k0b = fv(SMo + 21, 1)
        masked = bfv(O_OFFS + par_h * 1024 + 128, 512, rows=(0, 4))
        nkt = (qc + 1) * 4
        srow = fv(F_ROWS + 2048 + par_h * 512, 512, rows=(0, 1))
        rb64 = fv(F_ROPE + par_h * 512, 512, rows=(0, 64))

        def emit_sT(kt):
            qlo = max(qc * QC, kt * 128)
            n = qc * QC + QC - qlo
            par = kt % 2
            pbap, pbk = psB(par * 512, par * 512 + n)
            MM(pbap, pbk[0],
               View(ktile.ap[ho:ho + 64, kt * 128:(kt + 1) * 128], ktile.keys),
               View(qtile.ap[ho:ho + 64, QC - n:QC], qtile.keys), True, False)
            MM(pbap, pbk[0], C("ones4"),
               View(masked.ap[:, QC - n:QC], masked.keys), False, True)
            return (kt, n, par, pbap, pbk)

        def emit_tail(st):
            kt, n, par, pbap, pbk = st
            pt = bfv(O_S1 + 2048 + par * 512, 512)
            b.op("scalar", lambda e, o=pt.ap[:, 0:n], i=pbap, bi=k0b.ap:
                 e.activation(o, i, AF.Exp, scale=0.125, bias=bi),
                 reads=[pbk[0], k0b], writes=[pt])
            if kt * 128 >= qc * QC:
                b.op("vector", lambda e, o=pt.ap[:, 0:128], t=cn["triQ"][:, :]:
                     e.tensor_tensor(o, o, t, op=ALU.mult),
                     reads=[pt, "c_triQ"], writes=[pt])
            vtm_t = bfv(O_VTM + kt * 1024, 1024)
            avap, avk = psC(512 + QC - n, 512 + QC, rows=(0, 64))
            b.op("tensor", lambda e, o=avap,
                 l=vtm_t.ap[:, hp * 128 + ho:hp * 128 + ho + 64],
                 r=pt.ap[:, 0:n], st=(kt == 0), sp=(kt == nkt - 1):
                 e.matmul(o, l, r, start=st, stop=sp),
                 reads=[vtm_t, pt], writes=["psC1"])
            if NRM >= 1:
                oap, ok = psC(QC - n, QC, rows=(0, 1))
                MM(oap, ok[0], C("onesb"), View(pt.ap[:, 0:n], pt.keys),
                   kt == 0, kt == nkt - 1)

        prev = None
        for kt in range(nkt):
            st = emit_sT(kt)
            if prev is not None:
                emit_tail(prev)
            prev = st
            yield
        emit_tail(prev)
        if NRM >= 2:
            b.op("vector", lambda e, o=srow.ap, i=ps["C"][0:1, 0:QC]:
                 e.tensor_copy(o, i), reads=["psC0"], writes=[srow])
            EV("scalar", "activation", srow, [srow], func=AF.Ln)
            EV("scalar", "activation", srow, [srow], func=AF.Exp, scale=-1.0)
        if NRM >= 3:
            oap, ok = psC(0, QC, rows=(0, 64))
            b.op("tensor", lambda e, o=oap, l=cn["ones1"][0:1, 0:64], r=srow.ap:
                 e.matmul(o, l, r, start=True, stop=True),
                 reads=["c_ones1", srow], writes=["psC0"])
            b.op("vector", lambda e, o=rb64.ap, i=ps["C"][0:64, 0:QC]:
                 e.tensor_copy(o, i), reads=["psC0"], writes=[rb64])
        yield
        afm = bfv(O_S2 + hp * 512, 512)
        if NRM >= 4:
            b.op("vector", lambda e, o=afm.ap[ho:ho + 64, :], i=ps["C"][0:64, 512:1024],
                 rb=rb64.ap: e.scalar_tensor_tensor(
                     out=o, in0=i, scalar=1.0, in1=rb, op0=ALU.mult, op1=ALU.mult),
                 reads=["psC1", rb64], writes=[afm])
        else:
            b.op("vector", lambda e, o=afm.ap[ho:ho + 64, :], i=ps["C"][0:64, 512:1024]:
                 e.tensor_copy(o, i), reads=["psC1"], writes=[afm])
        yield

    def _interleave(*gens):
        live = [g for g in gens if g is not None]
        while live:
            for gi in list(live):
                try:
                    next(gi)
                except StopIteration:
                    live.remove(gi)

    def _window_tail(s, qc, qs, xs):
        aos = []
        for of in range(8):
            w = bfv(O_WRING + (of % 4) * 1024, 1024)
            DMA(w, g["wos"].ap()[of], reads=["wos"])
            par = of % 2
            for kc in range(8):
                afm = bfv(O_S2 + kc * 512, 512)
                oap, ok = psB(par * 512, (par + 1) * 512)
                MM(oap, ok[0], View(w.ap[:, kc * 128:(kc + 1) * 128], w.keys),
                   afm, kc == 0, kc == 7)
            ao = fv(F_AO + of * 512, 512)
            EV("vector", "tensor_copy", ao, [PV(psB(par * 512, (par + 1) * 512))])
            aos.append(ao)
        mu, r = fm_ln_rows(aos, 512, D, 0)
        mub = fm_bcast(mu, 0)
        rb = fm_bcast(r, 1)
        xts = []
        for ft in range(8):
            xt = fv(F_XA + ft * 512, 512)
            DMA(xt, xs[ft * 128:(ft + 1) * 128, qs])
            xts.append(xt)
        for ft in range(8):
            EV("vector", "tensor_tensor", aos[ft], [aos[ft], mub], op=ALU.subtract)
            EV("vector", "tensor_tensor", aos[ft], [aos[ft], rb], op=ALU.mult)
            EV("vector", "tensor_tensor", xts[ft], [xts[ft], aos[ft]], op=ALU.add)
        if qc == 0:
            tap("x1", xts[0].ap, [xts[0]])
        mu1, r1 = fm_ln_rows(xts, 512, D, 0)
        mub1 = fm_bcast(mu1, 0)
        rb1 = fm_bcast(r1, 1)
        for ft in range(8):
            EV("vector", "tensor_tensor", aos[ft], [xts[ft], mub1], op=ALU.subtract)
            lt = bfv(O_S2 + ft * 512, 512)
            EV("vector", "tensor_tensor", lt, [aos[ft], rb1], op=ALU.mult)
            DMA(View(g["ln1s"].ap()[s, ft, :, qs], [f"ln1D{s}_{ft}_{qc}"]), lt)
        stage = fv(F_STAGE, 1024)
        for tt in range(4):
            ttg = qc * 4 + tt
            for ft in range(8):
                par = ft % 2
                oap, ok = psC(par * 512 + 128, par * 512 + 256)
                b.op("tensor", lambda e, o=oap,
                     i=xts[ft].ap[:, tt * 128:(tt + 1) * 128],
                     idn=cn["identf"][:, :]: e.transpose(o, i, idn),
                     reads=[xts[ft], "c_identf"], writes=[ok[0]])
                dv = View(stage.ap[:, ft * 128:(ft + 1) * 128], stage.keys)
                b.op("vector", lambda e, o=dv.ap, i=oap: e.tensor_copy(o, i),
                     reads=[ok[0]], writes=[dv])
            DMA(View(g["x1tm"].ap()[s, ttg], [f"x1tmD{s}_{ttg}"]), stage)

    def emit_ffn(s):
        DMA(fv(F_GAMB, 1024), g["gamb"].ap())
        DMA(fv(F_BETB, 1024), g["betb"].ap())
        accs = [fv(F_ACC + tt * 1024, 1024) for tt in range(16)]
        for tt in range(16):
            EV("gpsimd", "memset", accs[tt], [], constant=0.0)
        ln1t = [bfv(O_LN1 + kc * T, T) for kc in range(8)]
        for kc in range(8):
            for ch in range(NFC):
                DMA(View(ln1t[kc].ap[:, ch * FC:(ch + 1) * FC], ln1t[kc].keys),
                    g["ln1s"].ap()[s, kc, :, ch * FC:(ch + 1) * FC],
                    reads=[f"ln1D{s}_{kc}_{ch * 2}", f"ln1D{s}_{kc}_{ch * 2 + 1}"])
        gb = fv(F_FSM, 8)
        gt = fv(F_FSM + 8, 8)
        grow = fv(F_FSM + 16, 1)
        m0a = fv(F_FSM + 32, 128)
        c1a = fv(F_FSM + 160, 128)
        v0a = fv(F_FSM + 288, 16)
        v1a = fv(F_FSM + 304, 16)
        bnr = fv(F_FSM + 320, 12)
        st2 = fv(F_FSM + 332, 2)
        rstd = fv(F_FSM + 334, 1)
        lnv = fv(F_FSM + 335, 1)
        for tt in range(16):
            for kc in range(8):
                oap, ok = psB(0, 8)
                MM(oap, ok[0],
                   View(ln1t[kc].ap[:, tt * 128:(tt + 1) * 128], ln1t[kc].keys),
                   Cs("gw", (slice(None), slice(kc * 8, (kc + 1) * 8))),
                   kc == 0, kc == 7)
            b.op("vector", lambda e, o=gb.ap, i=ps["B"][:, 0:8],
                 t=cn["gbt"][:, :]: e.tensor_tensor(o, i, t, op=ALU.add),
                 reads=["psB0", "c_gbt"], writes=[gb])
            EV("vector", "reduce_max", grow, [gb], axis=AX)
            EV("vector", "tensor_scalar_sub", gb, [gb, grow])
            EV("scalar", "activation", gb, [gb], func=AF.Exp)
            EV("vector", "reduce_sum", grow, [gb], axis=AX)
            EV("vector", "reciprocal", grow, [grow])
            EV("vector", "tensor_scalar_mul", gb, [gb, grow])
            ov0 = View(v0a.ap[:, tt:tt + 1], v0a.keys)
            EV("vector", "reduce_max", ov0, [gb], axis=AX)
            om0 = View(m0a.ap[:, tt * 8:(tt + 1) * 8], m0a.keys)
            b.op("vector", lambda e, o=om0.ap, i=gb.ap, sc=ov0.ap:
                 e.tensor_single_scalar(o, i, sc, op=ALU.is_ge),
                 reads=[gb, v0a], writes=[m0a])
            b.op("vector", lambda e, o=gt.ap, m=om0.ap, i=gb.ap:
                 e.scalar_tensor_tensor(out=o, in0=m, scalar=NEGBIG, in1=i,
                                        op0=ALU.mult, op1=ALU.add),
                 reads=[m0a, gb], writes=[gt])
            ov1 = View(v1a.ap[:, tt:tt + 1], v1a.keys)
            EV("vector", "reduce_max", ov1, [gt], axis=AX)
            oc1 = View(c1a.ap[:, tt * 8:(tt + 1) * 8], c1a.keys)
            b.op("vector", lambda e, o=oc1.ap, i=gt.ap, sc=ov1.ap:
                 e.tensor_single_scalar(o, i, sc, op=ALU.is_ge),
                 reads=[gt, v1a], writes=[c1a])
            b.op("vector", lambda e, o=oc1.ap, i=oc1.ap, sc=ov1.ap:
                 e.tensor_scalar_mul(o, i, sc), reads=[c1a, v1a], writes=[c1a])
            for src, srck, dsto in [(m0a, "m0", O_M0T), (c1a, "c1", O_C1T)]:
                oap, ok = psB(0, 128, rows=(0, 8))
                b.op("tensor", lambda e, o=oap,
                     i=src.ap[:, tt * 8:(tt + 1) * 8], idn=cn["identf"][:, :]:
                     e.transpose(o, i, idn),
                     reads=[src, "c_identf"], writes=[ok[0]])
                dT = bfv(dsto, 2048, rows=(0, 8))
                b.op("vector", lambda e, o=dT.ap[:, tt * 128:(tt + 1) * 128],
                     i=ps["B"][0:8, 0:128]: e.tensor_copy(o, i),
                     reads=["psB0"], writes=[dT])
        if s == 0:
            tap("gates", m0a.ap, [m0a])

        m0T = bfv(O_M0T, 2048, rows=(0, 8))
        c1T = bfv(O_C1T, 2048, rows=(0, 8))
        w2ts = [bfv(O_W2T + hf * 1024, 1024) for hf in range(16)]
        hts = [bfv(O_HBUF + hf * 1024, 1024) for hf in range(16)]

        yt = fv(F_YT, 1024)

        def tm_rstd(v, pr):
            bnr_p = fv(F_FSM + 336 + pr * 24, 12)
            st2_p = fv(F_FSM + 348 + pr * 24, 2)
            rstd_p = fv(F_FSM + 350 + pr * 24, 1)
            lnv_p = fv(F_FSM + 351 + pr * 24, 1)
            EV("vector", "bn_stats", View(bnr_p.ap[:, 0:6], bnr_p.keys),
               [View(v.ap[:, 0:512], v.keys)])
            EV("vector", "bn_stats", View(bnr_p.ap[:, 6:12], bnr_p.keys),
               [View(v.ap[:, 512:1024], v.keys)])
            EV("vector", "bn_aggr", st2_p, [bnr_p])
            b.op("scalar", lambda e, o=lnv_p.ap, i=st2_p.ap[:, 1:2]:
                 e.activation(o, i, AF.Ln, bias=cn["epsc"][:, 0:1], scale=1.0),
                 reads=[st2_p, "c_epsc"], writes=[lnv_p])
            EV("scalar", "activation", rstd_p, [lnv_p], func=AF.Exp, scale=-0.5)
            return st2_p, rstd_p

        def gen_ychain(ttg):
            pr = ttg % 2
            ytp = yt
            xt = fv(F_X1T + pr * 1024, 1024)
            DMA(xt, g["x1tm"].ap()[s, ttg], reads=[f"x1tmD{s}_{ttg}"])
            st2_p, rstd_p = tm_rstd(xt, pr)
            b.op("vector", lambda e, o=ytp.ap, i=xt.ap, m=st2_p.ap[:, 0:1],
                 rs=rstd_p.ap: e.tensor_scalar(out=o, in0=i, scalar1=m, scalar2=rs,
                                               op0=ALU.subtract, op1=ALU.mult),
                 reads=[xt, st2_p, rstd_p], writes=[ytp])
            EV("vector", "tensor_tensor", ytp, [ytp, accs[ttg]], op=ALU.add)
            yield
            st2_p, rstd_p = tm_rstd(ytp, pr)
            b.op("vector", lambda e, o=ytp.ap, m=st2_p.ap[:, 0:1], rs=rstd_p.ap:
                 e.tensor_scalar(out=o, in0=o, scalar1=m, scalar2=rs,
                                 op0=ALU.subtract, op1=ALU.mult),
                 reads=[ytp, st2_p, rstd_p], writes=[ytp])
            EV("gpsimd", "tensor_tensor", ytp, [ytp, fv(F_GAMB, 1024)], op=ALU.mult)
            EV("gpsimd", "tensor_tensor", ytp, [ytp, fv(F_BETB, 1024)], op=ALU.add)
            yield
            st2_p, rstd_p = tm_rstd(ytp, pr)
            b.op("vector", lambda e, o=ytp.ap, m=st2_p.ap[:, 0:1], rs=rstd_p.ap:
                 e.tensor_scalar(out=o, in0=o, scalar1=m, scalar2=rs,
                                 op0=ALU.subtract, op1=ALU.mult),
                 reads=[ytp, st2_p, rstd_p], writes=[ytp])
            EV("vector", "tensor_tensor", xt, [ytp, xt], op=ALU.add)
            DMA(View(g["out"].ap()[s, ttg * 128:(ttg + 1) * 128, :],
                     [f"outD{s}_{ttg}"]), xt)
            yield

        def gen_block(it, e_, ch, maska):
            for hf in range(4):
                wv = bfv(O_W1R + hf * 1024, 1024)
                DMA(wv, g["w1s"].ap()[e_, hf], reads=[f"w1s{e_}"])
            for hf in range(16):
                wv = bfv(O_W1R + (hf % 4) * 1024, 1024)
                par = hf % 2
                pa0, pk0 = psA(par * 1024, par * 1024 + 512)
                pa1, pk1 = psA(par * 1024 + 512, par * 1024 + 1024)
                for kc in range(8):
                    lhsv = View(wv.ap[:, kc * 128:(kc + 1) * 128], wv.keys)
                    for nh, (pa, pk) in enumerate([(pa0, pk0), (pa1, pk1)]):
                        MM(pa, pk[0], lhsv,
                           View(ln1t[kc].ap[:, ch * FC + nh * 512:
                                            ch * FC + (nh + 1) * 512],
                                ln1t[kc].keys),
                           kc == 0, kc == 7)
                b.op("scalar", lambda e, o=hts[hf].ap,
                     i=ps["A"][:, par * 1024:(par + 1) * 1024],
                     bi=cn["b1r"][:, e_ * 16 + hf:e_ * 16 + hf + 1]:
                     e.activation(o, i, AF.Relu, bias=bi),
                     reads=[pk0[0], pk1[0], "c_b1r"], writes=[hts[hf]])
                if hf + 4 < 16:
                    wv2 = bfv(O_W1R + ((hf + 4) % 4) * 1024, 1024)
                    DMA(wv2, g["w1s"].ap()[e_, hf + 4], reads=[f"w1s{e_}"])
                yield
            for tt in range(8):
                ttg = ch * 8 + tt
                par = tt % 2
                if par == 0:
                    poap = ps["B"]
                    pks = ("psB0", "psB1")
                else:
                    poap = ps["A"][:, 0:1024]
                    pks = ("psA0", "psA1")
                for hf in range(16):
                    lhsv = View(hts[hf].ap[:, tt * 128:(tt + 1) * 128],
                                hts[hf].keys)
                    for nh in range(2):
                        b.op("tensor", lambda e,
                             o=poap[:, nh * 512:(nh + 1) * 512],
                             l=lhsv.ap,
                             r=w2ts[hf].ap[:, nh * 512:(nh + 1) * 512],
                             st=(hf == 0), sp=(hf == 15):
                             e.matmul(o, l, r, start=st, stop=sp),
                             reads=[lhsv, w2ts[hf]], writes=[pks[nh]])
                scol = maska.ap[:, ttg * 8 + e_:ttg * 8 + e_ + 1]
                b.op("vector", lambda e, o=accs[ttg].ap, i=poap[:, 0:1024],
                     sc=scol: e.scalar_tensor_tensor(
                         out=o, in0=i, scalar=sc, in1=o,
                         op0=ALU.mult, op1=ALU.add),
                     reads=[pks[0], pks[1], accs[ttg], maska],
                     writes=[accs[ttg]])
                yield

        def gen_epi(it, ch, maskT):
            for tt in range(8):
                ttg = ch * 8 + tt
                for nh in range(2):
                    oap, ok = psC(nh * 512, (nh + 1) * 512)
                    b.op("tensor", lambda e, o=oap,
                         l=maskT.ap[:, ttg * 128:(ttg + 1) * 128],
                         r=cn["b2"][:, nh * 512:(nh + 1) * 512]:
                         e.matmul(o, l, r, start=True, stop=True),
                         reads=[maskT, "c_b2"], writes=[ok[0]])
                b.op("vector", lambda e, o=accs[ttg].ap, i=ps["C"][:, 0:1024]:
                     e.tensor_tensor(o, o, i, op=ALU.add),
                     reads=["psC0", "psC1", accs[ttg]], writes=[accs[ttg]])
                yield
                if it == 0:
                    for ft in range(8):
                        par = ft % 2
                        oap, ok = psC(par * 512, par * 512 + 128)
                        b.op("tensor", lambda e, o=oap,
                             i=accs[ttg].ap[:, ft * 128:(ft + 1) * 128],
                             idn=cn["identf"][:, :]: e.transpose(o, i, idn),
                             reads=[accs[ttg], "c_identf"], writes=[ok[0]])
                        b.op("vector", lambda e,
                             o=ln1t[ft].ap[:, ttg * 128:(ttg + 1) * 128],
                             i=oap: e.scalar_tensor_tensor(
                                 out=o, in0=i, scalar=0.5,
                                 in1=o, op0=ALU.mult, op1=ALU.add),
                             reads=[ok[0], ln1t[ft]], writes=[ln1t[ft]])
                        if ft == 3:
                            yield
                    b.op("vector", lambda e, o=accs[ttg].ap,
                         sc=v0a.ap[:, ttg:ttg + 1]: e.tensor_scalar_mul(o, o, sc),
                         reads=[accs[ttg], v0a], writes=[accs[ttg]])
                    yield
                else:
                    for _ in gen_ychain(ttg):
                        yield

        for it in range(2):
            maska, maskT = (m0a, m0T) if it == 0 else (c1a, c1T)
            for e_ in range(E):
                for hf in range(16):
                    DMA(w2ts[hf], g["w2s"].ap()[e_, hf * 128:(hf + 1) * 128, :],
                        reads=[f"w2s{e_}"])
                if e_ < E - 1:
                    _interleave(gen_block(it, e_, 0, maska))
                    _interleave(gen_block(it, e_, 1, maska))
                else:
                    _interleave(gen_block(it, e_, 0, maska))
                    if it == 0 and s == 0:
                        tap("acc0", accs[0].ap, [accs[0]])
                    _interleave(gen_epi(it, 0, maskT), gen_block(it, e_, 1, maska))
                    _interleave(gen_epi(it, 1, maskT))

    emit_startup()
    for s in range(nseq):
        if 'A1' in phases:
            emit_attention(s)
        if 'FFN' in phases:
            emit_ffn(s)
    b.op("sync", lambda e: e.nop(),
         reads=[f"outD{s}_{t}" for s in range(SEQS) for t in range(16)] + tap_keys)

    b.finalize_and_emit(es)
    nc._keepalive = es
    return nc


def kernel(**inputs):
    in_maps = _host_inputs(inputs)
    nc = build_program()
    res = run_bass_kernel_spmd(nc, in_maps, core_ids=list(range(NCORES)))
    out = np.concatenate([r["out"] for r in res.results], axis=0)
    return np.ascontiguousarray(out.reshape(16, 2048, 1024))


# version 10
# speedup vs baseline: 1.0383x; 1.0130x over previous
import contextlib
import numpy as np
import ml_dtypes

import concourse.bass as bass
import concourse.mybir as mybir
from concourse.bass_utils import run_bass_kernel_spmd

F32 = mybir.dt.float32
BF16 = mybir.dt.bfloat16
ALU = mybir.AluOpType
AF = mybir.ActivationFunctionType
AX = mybir.AxisListType.X

NCORES = 8
SEQS = 2
T = 2048
D = 1024
NH = 16
DH = 64
LAT = 256
E = 8
HID = 2048
EPS = 1e-5
NEGBIG = -1e30
QC = 512
NQC = T // QC
FC = 1024
NFC = T // FC

ENGINES = ("sync", "tensor", "vector", "scalar", "gpsimd")
GRAN = 512


class View:
    __slots__ = ("ap", "keys")

    def __init__(self, ap, keys):
        self.ap = ap
        self.keys = tuple(keys)


class Arena:
    def __init__(self, nc, es, name, cols, dt):
        self.name = name
        self.t = es.enter_context(nc.sbuf_tensor(name, [128, cols], dt))

    def v(self, off, ncols, rows=(0, 128)):
        ap = self.t[rows[0]:rows[1], off:off + ncols]
        c0, c1 = off // GRAN, (off + ncols - 1) // GRAN
        return View(ap, [(self.name, i) for i in range(c0, c1 + 1)])


def K(x):
    if isinstance(x, View):
        return list(x.keys)
    return [x]


class Builder:
    def __init__(self, nc):
        self.nc = nc
        self.recs = []
        self.lastw = {}
        self.lastr = {}

    def op(self, eng, fn, reads=(), writes=(), dma=False):
        rk = [k for x in reads for k in K(x)]
        wk = [k for x in writes for k in K(x)]
        idx = len(self.recs)
        deps = set()
        for r in rk:
            if r in self.lastw:
                deps.add(self.lastw[r])
        for w in wk:
            if w in self.lastw:
                deps.add(self.lastw[w])
            for i2 in self.lastr.get(w, {}).values():
                deps.add(i2)
        self.recs.append(dict(idx=idx, eng=eng, fn=fn, deps=deps, dma=dma,
                              inc=False, ordn=None, lane_key=None))
        for r in rk:
            self.lastr.setdefault(r, {})[eng] = idx
        for w in wk:
            self.lastw[w] = idx
            self.lastr[w] = {eng: idx}
        return idx

    def finalize_and_emit(self, es):
        nc = self.nc
        recs = self.recs
        NLANE = 8
        qcount = {}
        for rec in recs:
            if rec["dma"]:
                q = rec["eng"]
                n = qcount.get(q, 0)
                qcount[q] = n + 1
                rec["lane_key"] = ("dl", q, n % NLANE)
                rec["ordn"] = n // NLANE + 1
        for rec in recs:
            kept = {}
            for d in rec["deps"]:
                p = recs[d]
                if (p["eng"] == "tensor" and rec["eng"] == "tensor"
                        and not p["dma"] and not rec["dma"]):
                    continue
                key = p["lane_key"] if p["dma"] else (p["eng"], False)
                kept[key] = max(kept.get(key, -1), d)
            rec["edges"] = kept
            for d in kept.values():
                recs[d]["inc"] = True
        ccount = {}
        for rec in recs:
            if rec["dma"]:
                rec["inc"] = True
            elif rec["inc"]:
                key = (rec["eng"], False)
                ccount[key] = ccount.get(key, 0) + 1
                rec["ordn"] = ccount[key]
        sems = {}
        for q, n in qcount.items():
            for l in range(min(n, NLANE)):
                sems[("dl", q, l)] = es.enter_context(nc.semaphore(f"d_{q}_{l}"))
        for key in ccount:
            sems[key] = es.enter_context(nc.semaphore("c_" + key[0]))
        for rec in recs:
            waits = []
            for key, d in rec["edges"].items():
                p = recs[d]
                if p["ordn"] is None:
                    continue
                waits.append((key, sems[key], p["ordn"] * (16 if p["dma"] else 1)))
            if rec["dma"] and rec["ordn"] > 1:
                waits.append((rec["lane_key"], sems[rec["lane_key"]],
                              (rec["ordn"] - 1) * 16))
            rec["waits"] = waits
            if rec["inc"]:
                rec["sem"] = sems[rec["lane_key"] if rec["dma"]
                                  else (rec["eng"], False)]
                rec["iv"] = 16 if rec["dma"] else 1

        block = es.enter_context(nc.Block())
        per_eng = {e: [r for r in recs if r["eng"] == e] for e in ENGINES}

        def mk(eng_name):
            lst = per_eng[eng_name]

            def section(eng):
                hi = {}
                for rec in lst:
                    for key, sem, val in rec["waits"]:
                        if hi.get(key, 0) >= val:
                            continue
                        eng.wait_ge(sem, val)
                        hi[key] = val
                    inst = rec["fn"](eng)
                    if rec["inc"]:
                        inst.then_inc(rec["sem"], rec["iv"])
            return section

        for name in ENGINES:
            if per_eng[name]:
                getattr(block, name)(mk(name))


def _colblk(w):
    Kd, Md = w.shape
    t = w.reshape(Kd // 128, 128, Md // 128, 128).transpose(2, 1, 0, 3)
    return np.ascontiguousarray(t.reshape(Md // 128, 128, Kd))


def _host_inputs(inp):
    f32 = np.float32
    x = np.asarray(inp["L0_attn_x"], f32)
    shared = {}
    swapi = np.arange(1024)
    swapi = (swapi // 64) * 64 + ((swapi % 64) + 32) % 64
    wq_ = np.asarray(inp["L0_attn_W_q"], f32)
    shared["wq"] = _colblk(wq_)
    shared["wqw"] = _colblk(np.ascontiguousarray(wq_[:, swapi]))
    shared["wdkv"] = _colblk(np.asarray(inp["L0_attn_W_dkv"], f32))
    wuk_ = np.asarray(inp["L0_attn_W_uk"], f32)
    shared["wuk"] = _colblk(wuk_)
    shared["wukw"] = _colblk(np.ascontiguousarray(wuk_[:, swapi]))
    shared["wuv"] = _colblk(np.asarray(inp["L0_attn_W_uv"], f32))
    shared["wo"] = _colblk(np.asarray(inp["L0_attn_W_o"], f32))
    shared["latg"] = np.ascontiguousarray(
        np.asarray(inp["L0_attn_ln_gamma"], f32).reshape(2, 128).T)
    shared["latb"] = np.ascontiguousarray(
        np.asarray(inp["L0_attn_ln_beta"], f32).reshape(2, 128).T)
    gw = np.asarray(inp["L0_ffn_gate_W"], f32)
    shared["gwr"] = np.ascontiguousarray(
        gw.reshape(8, 128, 8).transpose(1, 0, 2).reshape(128, 64))
    shared["gbt"] = np.ascontiguousarray(
        np.tile(np.asarray(inp["L0_ffn_gate_b"], f32)[None, :], (128, 1)))
    w1 = np.asarray(inp["L0_ffn_W1"], f32)
    shared["w1h"] = np.stack([_colblk(w1[e]) for e in range(E)])
    b1 = np.asarray(inp["L0_ffn_b1"], f32)
    shared["b1r"] = np.ascontiguousarray(
        b1.reshape(8, 16, 128).transpose(2, 0, 1).reshape(128, 128))
    shared["w2h"] = np.ascontiguousarray(np.asarray(inp["L0_ffn_W2"], f32))
    shared["b2"] = np.ascontiguousarray(np.asarray(inp["L0_ffn_b2"], f32))
    shared["gamb"] = np.ascontiguousarray(
        np.tile(np.asarray(inp["L0_ffn_ln_gamma"], f32)[None, :], (128, 1)))
    shared["betb"] = np.ascontiguousarray(
        np.tile(np.asarray(inp["L0_ffn_ln_beta"], f32)[None, :], (128, 1)))

    pos = np.arange(T, dtype=f32)
    inv = (f32(1.0) / (f32(10000.0) ** (np.arange(32, dtype=f32) / f32(32.0)))).astype(f32)
    ang = pos[None, :] * inv[:, None]
    cos, sin = np.cos(ang).astype(f32), np.sin(ang).astype(f32)
    CS = np.empty((128, T), f32)
    SS = np.empty((128, T), f32)
    for r in range(128):
        rr = r % 64
        if rr < 32:
            CS[r], SS[r] = cos[rr], -sin[rr]
        else:
            CS[r], SS[r] = cos[rr - 32], sin[rr - 32]
    shared["csss"] = np.ascontiguousarray(np.concatenate([CS, SS], axis=1))
    qi = np.arange(128)[:, None]
    kj = np.arange(128)[None, :]
    shared["triA"] = np.where(kj > qi, f32(NEGBIG), f32(0.0)).astype(f32)
    shared["triQ"] = (kj >= qi).astype(ml_dtypes.bfloat16)
    bd4 = np.zeros((4, QC), ml_dtypes.bfloat16)
    for j in range(4):
        bd4[j, j * 128:(j + 1) * 128] = 1
    shared["bd4"] = bd4
    shared["ones4"] = np.ones((4, 128), ml_dtypes.bfloat16)
    shared["identf"] = np.eye(128, dtype=f32)
    shared["identb"] = np.eye(128, dtype=ml_dtypes.bfloat16)
    shared["onesf"] = np.ones((128, 1), f32)
    shared["epsc"] = np.full((128, 1), EPS, f32)
    shared["onesb"] = np.ones((128, 1), ml_dtypes.bfloat16)
    shared["ones1"] = np.ones((1, 128), f32)

    in_maps = []
    for c in range(NCORES):
        d = dict(shared)
        d["x"] = np.ascontiguousarray(x[2 * c:2 * c + 2].transpose(0, 2, 1))
        in_maps.append(d)
    return in_maps


def build_program(dbg=(), phases=('A1', 'A2', 'FFN'), nseq=SEQS):
    nc = bass.Bass("TRN2", target_bir_lowering=False)
    g = {}

    def dram(name, shape, dt=F32, kind="ExternalInput"):
        g[name] = nc.dram_tensor(name, list(shape), dt, kind=kind)

    dram("x", [SEQS, D, T])
    for n, sh in [("wq", [8, 128, 1024]), ("wqw", [8, 128, 1024]),
                  ("wukw", [8, 128, 256]), ("wdkv", [2, 128, 1024]),
                  ("wuk", [8, 128, 256]), ("wuv", [8, 128, 256]),
                  ("wo", [8, 128, 1024]), ("latg", [128, 2]), ("latb", [128, 2]),
                  ("gwr", [128, 64]), ("gbt", [128, 8]),
                  ("w1h", [8, 16, 128, 1024]), ("b1r", [128, 128]),
                  ("w2h", [8, 2048, 1024]), ("b2", [8, 1024]),
                  ("gamb", [128, 1024]), ("betb", [128, 1024]),
                  ("csss", [128, 2 * T]), ("triA", [128, 128]),
                  ("identf", [128, 128]), ("onesf", [128, 1]), ("ones1", [1, 128]),
                  ("epsc", [128, 1])]:
        dram(n, sh)
    for n, sh in [("triQ", [128, 128]), ("bd4", [4, QC]), ("ones4", [4, 128]),
                  ("identb", [128, 128]), ("onesb", [128, 1])]:
        dram(n, sh, BF16)
    dram("out", [SEQS, T, D], F32, kind="ExternalOutput")
    dram("w1s", [8, 16, 128, 1024], BF16, kind="Internal")
    dram("w2s", [8, 2048, 1024], BF16, kind="Internal")
    dram("wqs", [8, 128, 1024], BF16, kind="Internal")
    dram("wos", [8, 128, 1024], BF16, kind="Internal")
    dram("wdkvs", [2, 128, 1024], BF16, kind="Internal")
    dram("wuks", [8, 128, 256], BF16, kind="Internal")
    dram("wqws", [8, 128, 1024], BF16, kind="Internal")
    dram("wukws", [8, 128, 256], BF16, kind="Internal")
    dram("wuvs", [8, 128, 256], BF16, kind="Internal")
    dram("x1tm", [SEQS, 16, 128, D], F32, kind="Internal")
    dram("ln1s", [SEQS, 8, 128, T], BF16, kind="Internal")

    dbg_bufs = {}
    for name in dbg:
        shape, dt = dbg[name] if isinstance(dbg, dict) else (None, None)
        dram("dbg_" + name, shape, dt, kind="ExternalOutput")
        dbg_bufs[name] = g["dbg_" + name]

    es = contextlib.ExitStack()
    ABF = Arena(nc, es, "ABF", 57344, BF16)
    AF32 = Arena(nc, es, "AF32", 21504 + 512, F32)

    cn = {}
    for n, sh, dt in [("triA", [128, 128], F32), ("identf", [128, 128], F32),
                      ("onesf", [128, 1], F32), ("ones1", [1, 128], F32),
                      ("epsc", [128, 1], F32),
                      ("b1r", [128, 128], F32), ("gbt", [128, 8], F32),
                      ("latg", [128, 2], F32), ("latb", [128, 2], F32),
                      ("triQ", [128, 128], BF16), ("bd4", [4, QC], BF16),
                      ("ones4", [4, 128], BF16), ("identb", [128, 128], BF16), ("onesb", [128, 1], BF16),
                      ("b2", [8, 1024], BF16), ("gw", [128, 64], BF16)]:
        cn[n] = es.enter_context(nc.sbuf_tensor("c_" + n, sh, dt))

    def C(n):
        return View(cn[n][:, :], ["c_" + n])

    def Cs(n, sl):
        return View(cn[n][sl], ["c_" + n])

    def kbf(ft, cols=slice(0, T)):
        return ABF.v(ft * T, T).ap[:, cols], [("ABF", i) for i in range(
            (ft * T) // GRAN, (ft * T + T - 1) // GRAN + 1)]

    def bfv(off, n, rows=(0, 128)):
        return ABF.v(off, n, rows)

    def fv(off, n, rows=(0, 128)):
        return AF32.v(off, n, rows)

    O_KBF, O_VTM, O_WRING, O_S1, O_S2 = 0, 16384, 32768, 38912, 43008
    O_QBF = 51200
    O_HBUF, O_W2T, O_W1R, O_M0T, O_C1T, O_LN1 = 0, 16384, 32768, 36864, 38912, 40960
    O_DSCR = 47104
    O_OFFS = 49152
    F_XA, F_AO, F_CS, F_SS = 0, 4096, 8192, 10240
    F_ROPE, F_RSW, F_SQ, F_LAT, F_STAGE, F_ROWS = 12288, 13312, 13824, 14336, 15360, 16384
    F_SM = 19456
    F_ACC, F_X1T, F_YT, F_GAMB, F_BETB = 0, 16384, 18432, 19456, 20480
    F_FSM = 21504

    b = Builder(nc)

    def DMA(dst, src, reads=(), writes=(), q="sync"):
        dap = dst.ap if isinstance(dst, View) else dst
        sap = src.ap if isinstance(src, View) else src
        rd = list(reads) + ([src] if isinstance(src, View) else [])
        wr = list(writes) + ([dst] if isinstance(dst, View) else [])
        b.op(q, lambda e, d=dap, s=sap: e.dma_start(out=d, in_=s),
             reads=rd, writes=wr, dma=True)

    def MM(out_ap, outk, lhs, rhs, start, stop):
        lap = lhs.ap if isinstance(lhs, View) else lhs
        rap = rhs.ap if isinstance(rhs, View) else rhs
        rd = [x for x in (lhs, rhs)]
        b.op("tensor", lambda e, o=out_ap, l=lap, r=rap, st=start, sp=stop:
             e.matmul(o, l, r, start=st, stop=sp),
             reads=rd, writes=[outk])

    def TR(out_ap, outk, in_v, ident):
        iap = in_v.ap if isinstance(in_v, View) else in_v
        b.op("tensor", lambda e, o=out_ap, i=iap, idn=ident.ap:
             e.transpose(o, i, idn), reads=[in_v, ident], writes=[outk])

    def EV(eng, meth, out, ins, reads=(), writes=(), **kw):
        oap = out.ap if isinstance(out, View) else out
        iaps = [i.ap if isinstance(i, View) else i for i in ins]
        rd = [i for i in ins if isinstance(i, View)] + list(reads)
        wr = ([out] if isinstance(out, View) else []) + list(writes)
        b.op(eng, lambda e, o=oap, ia=iaps, k=kw, m=meth:
             getattr(e, m)(o, *ia, **k), reads=rd, writes=wr)

    tap_keys = []

    def tap(name, src_ap, reads):
        if name in dbg_bufs:
            key = f"tapD_{name}_{len(tap_keys)}"
            tap_keys.append(key)
            b.op("sync", lambda e, d=dbg_bufs[name].ap(), s=src_ap:
                 e.dma_start(out=d, in_=s), reads=list(reads), writes=[key], dma=True)

    ps = {}
    ps["A"] = es.enter_context(nc.psum_tensor("psA", [128, 2048], F32))
    ps["B"] = es.enter_context(nc.psum_tensor("psB", [128, 1024], F32))
    ps["C"] = es.enter_context(nc.psum_tensor("psC", [128, 1024], F32))

    def psA(c0, c1, rows=(0, 128)):
        return ps["A"][rows[0]:rows[1], c0:c1], [f"psA{i}" for i in
                                                 range(c0 // 512, (c1 - 1) // 512 + 1)]

    def psB(c0, c1, rows=(0, 128)):
        return ps["B"][rows[0]:rows[1], c0:c1], [f"psB{i}" for i in
                                                 range(c0 // 512, (c1 - 1) // 512 + 1)]

    def psC(c0, c1, rows=(0, 128)):
        return ps["C"][rows[0]:rows[1], c0:c1], [f"psC{i}" for i in
                                                 range(c0 // 512, (c1 - 1) // 512 + 1)]

    def PV(appair):
        return View(appair[0], appair[1])

    def emit_startup():
        for n in ["triA", "identf", "onesf", "ones1", "epsc", "b1r", "gbt",
                  "latg", "latb", "triQ", "bd4", "ones4", "identb", "onesb"]:
            DMA(View(cn[n][:, :], ["c_" + n]), g[n].ap())
        for nm in ["wdkv", "wuk", "wukw", "wuv", "wq", "wqw", "wo"]:
            DMA(View(g[nm + "s"].ap()[:], [nm + "s"]), g[nm].ap()[:], q="gpsimd")
        DMA(View(cn["b2"][:, :], ["c_b2"]), g["b2"].ap(), q="gpsimd")
        DMA(View(cn["gw"][:, :], ["c_gw"]), g["gwr"].ap(), q="gpsimd")
        for e_ in range(E):
            DMA(View(g["w1s"].ap()[e_], [f"w1s{e_}"]), g["w1h"].ap()[e_], q="gpsimd")
            DMA(View(g["w2s"].ap()[e_], [f"w2s{e_}"]), g["w2h"].ap()[e_], q="gpsimd")

    def fm_ln_rows(tiles, n, dim, rowoff):
        nt = len(tiles)
        sq2 = [fv(F_SQ, 512), fv(F_YT, 512)]
        EV("scalar", "activation", sq2[0], [tiles[0]], func=AF.Square)
        for i, tv in enumerate(tiles):
            if i + 1 < nt:
                EV("scalar", "activation", sq2[(i + 1) % 2], [tiles[i + 1]],
                   func=AF.Square)
            oap, ok = psA(1024, 1024 + n, rows=(0, 1))
            MM(oap, ok[0], C("onesf"), tv, i == 0, i == nt - 1)
            oap, ok = psA(1536, 1536 + n, rows=(0, 1))
            MM(oap, ok[0], C("onesf"), sq2[i % 2], i == 0, i == nt - 1)
        mu = fv(F_ROWS + rowoff, n, rows=(0, 1))
        ex2 = fv(F_ROWS + rowoff + n, n, rows=(0, 1))
        r = fv(F_ROWS + rowoff + 2 * n, n, rows=(0, 1))
        b.op("vector", lambda e, o=mu.ap, i=ps["A"][0:1, 1024:1024 + n]:
             e.tensor_scalar_mul(o, i, 1.0 / dim), reads=["psA2"], writes=[mu])
        b.op("vector", lambda e, o=ex2.ap, i=ps["A"][0:1, 1536:1536 + n]:
             e.tensor_scalar_mul(o, i, 1.0 / dim), reads=["psA3"], writes=[ex2])
        b.op("vector", lambda e, o=r.ap, m=mu.ap: e.scalar_tensor_tensor(
            out=o, in0=m, scalar=-1.0, in1=m, op0=ALU.mult, op1=ALU.mult),
            reads=[mu], writes=[r])
        EV("vector", "tensor_tensor", r, [r, ex2], op=ALU.add)
        b.op("scalar", lambda e, o=r.ap: e.activation(o, o, AF.Ln, bias=cn["epsc"][0:1, 0:1], scale=1.0), reads=[r, "c_epsc"], writes=[r])
        EV("scalar", "activation", r, [r], func=AF.Exp, scale=-0.5)
        return mu, r

    def fm_bcast(row, bank):
        oap, ok = psA(bank * 512, bank * 512 + 512)
        MM(oap, ok[0], C("ones1"), row, True, True)
        return PV(psA(bank * 512, bank * 512 + 512))

    def ln_apply_fm(xv, mub, rb, outv, tmpv):
        EV("vector", "tensor_tensor", tmpv, [xv, mub], op=ALU.subtract)
        EV("vector", "tensor_tensor", outv, [tmpv, rb], op=ALU.mult)

    def rope_evac2(psv, psw, dst, colsl):
        t1 = fv(F_ROPE, 512)
        t2 = fv(F_ROPE + 512, 512)
        csv = View(AF32.t[:, F_CS + colsl.start:F_CS + colsl.stop],
                   AF32.v(F_CS, 2048).keys)
        ssv = View(AF32.t[:, F_SS + colsl.start:F_SS + colsl.stop],
                   AF32.v(F_SS, 2048).keys)
        b.op("vector", lambda e, o=t1.ap, i=psv.ap, c=csv.ap:
             e.scalar_tensor_tensor(out=o, in0=i, scalar=1.0, in1=c,
                                    op0=ALU.mult, op1=ALU.mult),
             reads=[psv, csv], writes=[t1])
        b.op("vector", lambda e, o=t2.ap, i=psw.ap, c=ssv.ap:
             e.scalar_tensor_tensor(out=o, in0=i, scalar=1.0, in1=c,
                                    op0=ALU.mult, op1=ALU.mult),
             reads=[psw, ssv], writes=[t2])
        EV("vector", "tensor_tensor", dst, [t1, t2], op=ALU.add)

    import os
    RS = int(os.environ.get("RS", "9"))
    NRM = int(os.environ.get("NRM", "4"))

    def rope_evac(psv, dst, colsl):
        if RS < 1:
            EV("vector", "tensor_copy", dst, [psv])
            return
        t1 = fv(F_ROPE, 512)
        t2 = fv(F_ROPE + 512, 512)
        rsw = fv(F_RSW, 512)
        csv = View(AF32.t[:, F_CS + colsl.start:F_CS + colsl.stop],
                   AF32.v(F_CS, 2048).keys)
        ssv = View(AF32.t[:, F_SS + colsl.start:F_SS + colsl.stop],
                   AF32.v(F_SS, 2048).keys)
        b.op("vector", lambda e, o=t1.ap, i=psv.ap, c=csv.ap:
             e.scalar_tensor_tensor(out=o, in0=i, scalar=1.0, in1=c,
                                    op0=ALU.mult, op1=ALU.mult),
             reads=[psv, csv], writes=[t1])
        if RS < 2:
            EV("vector", "tensor_copy", dst, [t1])
            return
        import os as _os
        _SWENG = _os.environ.get("SWENG", "vector")
        for dstr, srcr in [(0, 32), (32, 0), (64, 96), (96, 64)]:
            if _SWENG == "vector":
                b.op("vector", lambda e, o=rsw.ap[dstr:dstr + 32, :],
                     i=psv.ap[srcr:srcr + 32, :]: e.tensor_copy(o, i),
                     reads=[psv], writes=[rsw])
            else:
                b.op("scalar", lambda e, o=rsw.ap[dstr:dstr + 32, :],
                     i=psv.ap[srcr:srcr + 32, :]: e.activation(o, i, AF.Copy),
                     reads=[psv], writes=[rsw])
        if RS < 3:
            EV("vector", "tensor_copy", dst, [t1])
            return
        if RS >= 4:
            EV("gpsimd", "tensor_tensor", t2, [rsw, ssv], op=ALU.mult)
        else:
            EV("vector", "tensor_tensor", t2, [rsw, ssv], op=ALU.mult)
        EV("vector", "tensor_tensor", dst, [t1, t2], op=ALU.add)

    def emit_attention(s):
        xs = g["x"].ap()[s]
        for pp in range(2):
            kb = fv(F_SM + pp * 512, 512)
            EV("gpsimd", "memset", kb, [], constant=0.0)
        DMA(fv(F_CS, 2 * T), g["csss"].ap())

        import os
        A1S = int(os.environ.get("A1S", "9"))
        A1C = int(os.environ.get("A1C", "4"))
        for c in range(A1C):
            cs = slice(c * QC, (c + 1) * QC)
            xts = []
            for ft in range(8):
                xt = fv(F_XA + ft * 512, 512)
                DMA(xt, xs[ft * 128:(ft + 1) * 128, cs])
                xts.append(xt)
            if A1S < 1:
                continue
            mu, r = fm_ln_rows(xts, 512, D, 0)
            if A1S < 2:
                continue
            mub = fm_bcast(mu, 0)
            rb = fm_bcast(r, 1)
            ln0 = [bfv(O_S1 + ft * 512, 512) for ft in range(8)]
            tmp = fv(F_ROPE, 512)
            for ft in range(8):
                ln_apply_fm(xts[ft], mub, rb, ln0[ft], tmp)
            if A1S < 3:
                continue
            for of in range(2):
                w = bfv(O_WRING + of * 1024, 1024)
                DMA(w, g["wdkvs"].ap()[of], reads=["wdkvs"])
                for kc in range(8):
                    oap, ok = psB(of * 512, (of + 1) * 512)
                    MM(oap, ok[0], View(w.ap[:, kc * 128:(kc + 1) * 128], w.keys),
                       ln0[kc], kc == 0, kc == 7)
            latf = [fv(F_LAT + of * 512, 512) for of in range(2)]
            for of in range(2):
                EV("vector", "tensor_copy", latf[of], [PV(psB(of * 512, (of + 1) * 512))])
            if A1S < 4:
                continue
            mu2, r2 = fm_ln_rows(latf, 512, LAT, 1536)
            mub2 = fm_bcast(mu2, 0)
            rb2 = fm_bcast(r2, 1)
            for of in range(2):
                t = fv(F_ROPE, 512)
                EV("vector", "tensor_tensor", t, [latf[of], mub2], op=ALU.subtract)
                t2 = fv(F_ROPE + 512, 512)
                EV("vector", "tensor_tensor", t2, [t, rb2], op=ALU.mult)
                latn_t = bfv(O_S2 + of * T, T)
                dstv = View(latn_t.ap[:, cs], latn_t.keys)
                b.op("vector", lambda e, o=dstv.ap, i=t2.ap,
                     g1=cn["latg"][:, of:of + 1], b1=cn["latb"][:, of:of + 1]:
                     e.tensor_scalar(out=o, in0=i, scalar1=g1, scalar2=b1,
                                     op0=ALU.mult, op1=ALU.add),
                     reads=[t2, "c_latg", "c_latb"], writes=[dstv])
            if A1S < 5:
                continue
            kv_list = [("k", "wuks", None)] + ([("v", "wuvs", None)] if A1S >= 6 else [])
            for which, wsname, dsttile in kv_list:
                for of in range(8):
                    par = of % 2
                    if which == "k":
                        wsl = bfv(O_WRING + (of % 2) * 1024, 256)
                        DMA(wsl, g["wuks"].ap()[of], reads=["wuks"])
                        wsw = bfv(O_WRING + 2048 + (of % 2) * 1024, 256)
                        DMA(wsw, g["wukws"].ap()[of], reads=["wukws"])
                        for kc in range(2):
                            latn_t = bfv(O_S2 + kc * T, T)
                            oap, ok = psB(0, 512)
                            MM(oap, ok[0], View(wsl.ap[:, kc * 128:(kc + 1) * 128], wsl.keys),
                               View(latn_t.ap[:, cs], latn_t.keys), kc == 0, kc == 1)
                            oap, ok = psB(512, 1024)
                            MM(oap, ok[0], View(wsw.ap[:, kc * 128:(kc + 1) * 128], wsw.keys),
                               View(latn_t.ap[:, cs], latn_t.keys), kc == 0, kc == 1)
                        kt_ = bfv(O_KBF + of * T, T)
                        rope_evac2(PV(psB(0, 512)), PV(psB(512, 1024)),
                                   View(kt_.ap[:, cs], kt_.keys), cs)
                        continue
                    wsl = bfv(O_WRING + (of % 4) * 1024, 256)
                    DMA(wsl, g[wsname].ap()[of], reads=[wsname])
                    for kc in range(2):
                        latn_t = bfv(O_S2 + kc * T, T)
                        oap, ok = psB(par * 512, (par + 1) * 512)
                        MM(oap, ok[0], View(wsl.ap[:, kc * 128:(kc + 1) * 128], wsl.keys),
                           View(latn_t.ap[:, cs], latn_t.keys), kc == 0, kc == 1)
                    pv = PV(psB(par * 512, (par + 1) * 512))
                    if False:
                        pass
                    else:
                        vb = fv(F_AO + par * 512, 512)
                        EV("scalar", "activation", vb, [pv], func=AF.Copy)
                        for blk in range(4):
                            kt = c * 4 + blk
                            tb = blk % 2
                            oap, ok = psC(tb * 512, tb * 512 + 128)
                            b.op("tensor", lambda e, o=oap,
                                 i=vb.ap[:, blk * 128:(blk + 1) * 128],
                                 idn=cn["identf"][:, :]: e.transpose(o, i, idn),
                                 reads=[vb, "c_identf"], writes=[ok[0]])
                            vtm_t = bfv(O_VTM + kt * 1024, 1024)
                            dv = View(vtm_t.ap[:, of * 128:(of + 1) * 128], vtm_t.keys)
                            EV("vector", "tensor_copy", dv, [PV(psC(tb * 512, tb * 512 + 128))])
            if c == 0:
                tap("ln0", ln0[0].ap, [ln0[0]])
                tap("latn", bfv(O_S2, T).ap[:, 0:512], [bfv(O_S2, T)])

        for qc in range(NQC if 'A2' in phases else 0):
            qs = slice(qc * QC, (qc + 1) * QC)
            xts = []
            for ft in range(8):
                xt = fv(F_XA + ft * 512, 512)
                DMA(xt, xs[ft * 128:(ft + 1) * 128, qs])
                xts.append(xt)
            mu, r = fm_ln_rows(xts, 512, D, 0)
            mub = fm_bcast(mu, 0)
            rb = fm_bcast(r, 1)
            ln0 = [bfv(O_S1 + ft * 512, 512) for ft in range(8)]
            tmp = fv(F_ROPE, 512)
            for ft in range(8):
                ln_apply_fm(xts[ft], mub, rb, ln0[ft], tmp)
            for of in range(8):
                w = bfv(O_WRING + (of % 2) * 1024, 1024)
                DMA(w, g["wqs"].ap()[of], reads=["wqs"])
                ww = bfv(O_WRING + 2048 + (of % 2) * 1024, 1024)
                DMA(ww, g["wqws"].ap()[of], reads=["wqws"])
                for kc in range(8):
                    oap, ok = psB(0, 512)
                    MM(oap, ok[0], View(w.ap[:, kc * 128:(kc + 1) * 128], w.keys),
                       ln0[kc], kc == 0, kc == 7)
                    oap, ok = psB(512, 1024)
                    MM(oap, ok[0], View(ww.ap[:, kc * 128:(kc + 1) * 128], ww.keys),
                       ln0[kc], kc == 0, kc == 7)
                qb = bfv(O_QBF + of * 512, 512)
                rope_evac2(PV(psB(0, 512)), PV(psB(512, 1024)), qb, qs)
            if qc == 0:
                tap("qbf", bfv(O_QBF, 512).ap, [bfv(O_QBF, 512)])

            prevB = None
            for h in range(NH):
                ga = _head_passA(s, qc, h)
                _interleave(prevB, ga)
                prevB = _head_passB(s, qc, h)
            _interleave(prevB)

            _window_tail(s, qc, qs, xs)

    def _head_passA(s, qc, h):
        par_h = h % 2
        SMo = F_SM + par_h * 512
        hp, ho = h // 2, (h % 2) * 64
        qtile = bfv(O_QBF + hp * 512, 512)
        ktile = bfv(O_KBF + hp * T, T)
        nmax = fv(SMo, 4)
        k0v = fv(SMo + 20, 1)
        k0b = fv(SMo + 21, 1)
        offc2 = fv(SMo + 32, 32)
        for ql in range(4):
            qt = qc * 4 + ql
            ext = (qt + 1) * 128
            lhs = View(qtile.ap[ho:ho + 64, ql * 128:(ql + 1) * 128], qtile.keys)
            nch = (ext + 511) // 512
            for cc in range(nch):
                n0, n1 = cc * 512, min(ext, (cc + 1) * 512)
                oap, ok = psA(n0, n1)
                MM(oap, ok[0], lhs, View(ktile.ap[ho:ho + 64, n0:n1], ktile.keys),
                   True, True)
            b.op("vector", lambda e, o=nmax.ap[:, ql:ql + 1], i=ps["A"][:, 0:ext]:
                 e.reduce_max(o, i, axis=AX, negate=True),
                 reads=[f"psA{i}" for i in range(nch)], writes=[nmax])
            yield
        k0rbuf = fv(SMo + 288, 128, rows=(0, 32))
        k0t = fv(SMo + 256, 32)
        b.op("vector", lambda e, o=k0rbuf.ap[0:1, :], i=cn["ones1"][0:1, :],
             sc=nmax.ap[0:1, 0:1]: e.tensor_single_scalar(o, i, sc, op=ALU.mult),
             reads=["c_ones1", nmax], writes=[k0rbuf])
        for j in range(4):
            b.op("vector", lambda e, o=k0t.ap[j * 32:(j + 1) * 32, 0:32],
                 i=k0rbuf.ap[0:32, j * 32:(j + 1) * 32]: e.transpose(o, i),
                 reads=[k0rbuf], writes=[k0t])
        k0v = View(k0t.ap[:, 0:1], k0t.keys)
        b.op("vector", lambda e, o=k0b.ap, i=k0v.ap: e.tensor_scalar_mul(o, i, 0.125),
             reads=[k0v], writes=[k0b])
        b.op("vector", lambda e, o=offc2.ap[:, 0:4], i=nmax.ap, sc=k0v.ap:
             e.tensor_scalar_sub(o, i, sc), reads=[nmax, k0v], writes=[offc2])
        yield
        obuf = fv(SMo + 128, 128, rows=(0, 32))
        for j in range(4):
            b.op("vector", lambda e, o=obuf.ap[0:32, j * 32:(j + 1) * 32],
                 i=offc2.ap[j * 32:(j + 1) * 32, 0:32]: e.transpose(o, i),
                 reads=[offc2], writes=[obuf])
        offsTb = View(obuf.ap[0:4, :], obuf.keys)
        masked = bfv(O_OFFS + par_h * 1024 + 128, 512, rows=(0, 4))
        for j in range(4):
            b.op("gpsimd", lambda e, o=masked.ap[:, j * 128:(j + 1) * 128],
                 i=offsTb.ap, bd=cn["bd4"][:, j * 128:(j + 1) * 128]:
                 e.tensor_tensor(o, i, bd, op=ALU.mult),
                 reads=[offsTb, "c_bd4"], writes=[masked])
        yield

    def _head_passB(s, qc, h):
        par_h = h % 2
        SMo = F_SM + par_h * 512
        hp, ho = h // 2, (h % 2) * 64
        qtile = bfv(O_QBF + hp * 512, 512)
        ktile = bfv(O_KBF + hp * T, T)
        k0b = fv(SMo + 21, 1)
        masked = bfv(O_OFFS + par_h * 1024 + 128, 512, rows=(0, 4))
        nkt = (qc + 1) * 4
        srow = fv(F_ROWS + 2048 + par_h * 512, 512, rows=(0, 1))
        rb64 = fv(F_ROPE + par_h * 512, 512, rows=(0, 64))

        def emit_sT(kt):
            qlo = max(qc * QC, kt * 128)
            n = qc * QC + QC - qlo
            par = kt % 2
            pbap, pbk = psB(par * 512, par * 512 + n)
            MM(pbap, pbk[0],
               View(ktile.ap[ho:ho + 64, kt * 128:(kt + 1) * 128], ktile.keys),
               View(qtile.ap[ho:ho + 64, QC - n:QC], qtile.keys), True, False)
            MM(pbap, pbk[0], C("ones4"),
               View(masked.ap[:, QC - n:QC], masked.keys), False, True)
            return (kt, n, par, pbap, pbk)

        def emit_tail(st):
            kt, n, par, pbap, pbk = st
            pt = bfv(O_S1 + 2048 + par * 512, 512)
            b.op("scalar", lambda e, o=pt.ap[:, 0:n], i=pbap, bi=k0b.ap:
                 e.activation(o, i, AF.Exp, scale=0.125, bias=bi),
                 reads=[pbk[0], k0b], writes=[pt])
            if kt * 128 >= qc * QC:
                b.op("vector", lambda e, o=pt.ap[:, 0:128], t=cn["triQ"][:, :]:
                     e.tensor_tensor(o, o, t, op=ALU.mult),
                     reads=[pt, "c_triQ"], writes=[pt])
            vtm_t = bfv(O_VTM + kt * 1024, 1024)
            avap, avk = psC(512 + QC - n, 512 + QC, rows=(0, 64))
            b.op("tensor", lambda e, o=avap,
                 l=vtm_t.ap[:, hp * 128 + ho:hp * 128 + ho + 64],
                 r=pt.ap[:, 0:n], st=(kt == 0), sp=(kt == nkt - 1):
                 e.matmul(o, l, r, start=st, stop=sp),
                 reads=[vtm_t, pt], writes=["psC1"])
            if NRM >= 1:
                oap, ok = psC(QC - n, QC, rows=(0, 1))
                MM(oap, ok[0], C("onesb"), View(pt.ap[:, 0:n], pt.keys),
                   kt == 0, kt == nkt - 1)

        prev = None
        for kt in range(nkt):
            st = emit_sT(kt)
            if prev is not None:
                emit_tail(prev)
            prev = st
            yield
        emit_tail(prev)
        if NRM >= 2:
            b.op("vector", lambda e, o=srow.ap, i=ps["C"][0:1, 0:QC]:
                 e.tensor_copy(o, i), reads=["psC0"], writes=[srow])
            EV("scalar", "activation", srow, [srow], func=AF.Ln)
            EV("scalar", "activation", srow, [srow], func=AF.Exp, scale=-1.0)
        if NRM >= 3:
            oap, ok = psC(0, QC, rows=(0, 64))
            b.op("tensor", lambda e, o=oap, l=cn["ones1"][0:1, 0:64], r=srow.ap:
                 e.matmul(o, l, r, start=True, stop=True),
                 reads=["c_ones1", srow], writes=["psC0"])
            b.op("vector", lambda e, o=rb64.ap, i=ps["C"][0:64, 0:QC]:
                 e.tensor_copy(o, i), reads=["psC0"], writes=[rb64])
        yield
        afm = bfv(O_S2 + hp * 512, 512)
        if NRM >= 4:
            b.op("vector", lambda e, o=afm.ap[ho:ho + 64, :], i=ps["C"][0:64, 512:1024],
                 rb=rb64.ap: e.scalar_tensor_tensor(
                     out=o, in0=i, scalar=1.0, in1=rb, op0=ALU.mult, op1=ALU.mult),
                 reads=["psC1", rb64], writes=[afm])
        else:
            b.op("vector", lambda e, o=afm.ap[ho:ho + 64, :], i=ps["C"][0:64, 512:1024]:
                 e.tensor_copy(o, i), reads=["psC1"], writes=[afm])
        yield

    def _interleave(*gens):
        live = [g for g in gens if g is not None]
        while live:
            for gi in list(live):
                try:
                    next(gi)
                except StopIteration:
                    live.remove(gi)

    def _window_tail(s, qc, qs, xs):
        aos = []
        for of in range(8):
            w = bfv(O_WRING + (of % 4) * 1024, 1024)
            DMA(w, g["wos"].ap()[of], reads=["wos"])
            par = of % 2
            for kc in range(8):
                afm = bfv(O_S2 + kc * 512, 512)
                oap, ok = psB(par * 512, (par + 1) * 512)
                MM(oap, ok[0], View(w.ap[:, kc * 128:(kc + 1) * 128], w.keys),
                   afm, kc == 0, kc == 7)
            ao = fv(F_AO + of * 512, 512)
            EV("vector", "tensor_copy", ao, [PV(psB(par * 512, (par + 1) * 512))])
            aos.append(ao)
        mu, r = fm_ln_rows(aos, 512, D, 0)
        mub = fm_bcast(mu, 0)
        rb = fm_bcast(r, 1)
        xts = []
        for ft in range(8):
            xt = fv(F_XA + ft * 512, 512)
            DMA(xt, xs[ft * 128:(ft + 1) * 128, qs])
            xts.append(xt)
        for ft in range(8):
            EV("vector", "tensor_tensor", aos[ft], [aos[ft], mub], op=ALU.subtract)
            EV("vector", "tensor_tensor", aos[ft], [aos[ft], rb], op=ALU.mult)
            EV("vector", "tensor_tensor", xts[ft], [xts[ft], aos[ft]], op=ALU.add)
        if qc == 0:
            tap("x1", xts[0].ap, [xts[0]])
        mu1, r1 = fm_ln_rows(xts, 512, D, 0)
        mub1 = fm_bcast(mu1, 0)
        rb1 = fm_bcast(r1, 1)
        for ft in range(8):
            EV("vector", "tensor_tensor", aos[ft], [xts[ft], mub1], op=ALU.subtract)
            lt = bfv(O_S2 + ft * 512, 512)
            EV("vector", "tensor_tensor", lt, [aos[ft], rb1], op=ALU.mult)
            DMA(View(g["ln1s"].ap()[s, ft, :, qs], [f"ln1D{s}_{ft}_{qc}"]), lt)
        stage = fv(F_STAGE, 1024)
        for tt in range(4):
            ttg = qc * 4 + tt
            for ft in range(8):
                par = ft % 2
                oap, ok = psC(par * 512 + 128, par * 512 + 256)
                b.op("tensor", lambda e, o=oap,
                     i=xts[ft].ap[:, tt * 128:(tt + 1) * 128],
                     idn=cn["identf"][:, :]: e.transpose(o, i, idn),
                     reads=[xts[ft], "c_identf"], writes=[ok[0]])
                dv = View(stage.ap[:, ft * 128:(ft + 1) * 128], stage.keys)
                b.op("vector", lambda e, o=dv.ap, i=oap: e.tensor_copy(o, i),
                     reads=[ok[0]], writes=[dv])
            DMA(View(g["x1tm"].ap()[s, ttg], [f"x1tmD{s}_{ttg}"]), stage)

    def emit_ffn(s):
        DMA(fv(F_GAMB, 1024), g["gamb"].ap())
        DMA(fv(F_BETB, 1024), g["betb"].ap())
        accs = [fv(F_ACC + tt * 1024, 1024) for tt in range(16)]
        for tt in range(16):
            EV("gpsimd", "memset", accs[tt], [], constant=0.0)
        ln1t = [bfv(O_LN1 + kc * T, T) for kc in range(8)]
        for kc in range(8):
            for ch in range(NFC):
                DMA(View(ln1t[kc].ap[:, ch * FC:(ch + 1) * FC], ln1t[kc].keys),
                    g["ln1s"].ap()[s, kc, :, ch * FC:(ch + 1) * FC],
                    reads=[f"ln1D{s}_{kc}_{ch * 2}", f"ln1D{s}_{kc}_{ch * 2 + 1}"])
        gb = fv(F_FSM, 8)
        gt = fv(F_FSM + 8, 8)
        grow = fv(F_FSM + 16, 1)
        m0a = fv(F_FSM + 32, 128)
        c1a = fv(F_FSM + 160, 128)
        v0a = fv(F_FSM + 288, 16)
        v1a = fv(F_FSM + 304, 16)
        bnr = fv(F_FSM + 320, 12)
        st2 = fv(F_FSM + 332, 2)
        rstd = fv(F_FSM + 334, 1)
        lnv = fv(F_FSM + 335, 1)
        for tt in range(16):
            for kc in range(8):
                oap, ok = psB(0, 8)
                MM(oap, ok[0],
                   View(ln1t[kc].ap[:, tt * 128:(tt + 1) * 128], ln1t[kc].keys),
                   Cs("gw", (slice(None), slice(kc * 8, (kc + 1) * 8))),
                   kc == 0, kc == 7)
            b.op("vector", lambda e, o=gb.ap, i=ps["B"][:, 0:8],
                 t=cn["gbt"][:, :]: e.tensor_tensor(o, i, t, op=ALU.add),
                 reads=["psB0", "c_gbt"], writes=[gb])
            EV("vector", "reduce_max", grow, [gb], axis=AX)
            EV("vector", "tensor_scalar_sub", gb, [gb, grow])
            EV("scalar", "activation", gb, [gb], func=AF.Exp)
            EV("vector", "reduce_sum", grow, [gb], axis=AX)
            EV("vector", "reciprocal", grow, [grow])
            EV("vector", "tensor_scalar_mul", gb, [gb, grow])
            ov0 = View(v0a.ap[:, tt:tt + 1], v0a.keys)
            EV("vector", "reduce_max", ov0, [gb], axis=AX)
            om0 = View(m0a.ap[:, tt * 8:(tt + 1) * 8], m0a.keys)
            b.op("vector", lambda e, o=om0.ap, i=gb.ap, sc=ov0.ap:
                 e.tensor_single_scalar(o, i, sc, op=ALU.is_ge),
                 reads=[gb, v0a], writes=[m0a])
            b.op("vector", lambda e, o=gt.ap, m=om0.ap, i=gb.ap:
                 e.scalar_tensor_tensor(out=o, in0=m, scalar=NEGBIG, in1=i,
                                        op0=ALU.mult, op1=ALU.add),
                 reads=[m0a, gb], writes=[gt])
            ov1 = View(v1a.ap[:, tt:tt + 1], v1a.keys)
            EV("vector", "reduce_max", ov1, [gt], axis=AX)
            oc1 = View(c1a.ap[:, tt * 8:(tt + 1) * 8], c1a.keys)
            b.op("vector", lambda e, o=oc1.ap, i=gt.ap, sc=ov1.ap:
                 e.tensor_single_scalar(o, i, sc, op=ALU.is_ge),
                 reads=[gt, v1a], writes=[c1a])
            b.op("vector", lambda e, o=oc1.ap, i=oc1.ap, sc=ov1.ap:
                 e.tensor_scalar_mul(o, i, sc), reads=[c1a, v1a], writes=[c1a])
            for src, srck, dsto in [(m0a, "m0", O_M0T), (c1a, "c1", O_C1T)]:
                oap, ok = psB(0, 128, rows=(0, 8))
                b.op("tensor", lambda e, o=oap,
                     i=src.ap[:, tt * 8:(tt + 1) * 8], idn=cn["identf"][:, :]:
                     e.transpose(o, i, idn),
                     reads=[src, "c_identf"], writes=[ok[0]])
                dT = bfv(dsto, 2048, rows=(0, 8))
                b.op("vector", lambda e, o=dT.ap[:, tt * 128:(tt + 1) * 128],
                     i=ps["B"][0:8, 0:128]: e.tensor_copy(o, i),
                     reads=["psB0"], writes=[dT])
        if s == 0:
            tap("gates", m0a.ap, [m0a])

        m0T = bfv(O_M0T, 2048, rows=(0, 8))
        c1T = bfv(O_C1T, 2048, rows=(0, 8))
        w2ts = [bfv(O_W2T + hf * 1024, 1024) for hf in range(16)]
        hts = [bfv(O_HBUF + hf * 1024, 1024) for hf in range(16)]

        yt = fv(F_YT, 1024)

        def tm_rstd(v, pr):
            bnr_p = fv(F_FSM + 336 + pr * 24, 12)
            st2_p = fv(F_FSM + 348 + pr * 24, 2)
            rstd_p = fv(F_FSM + 350 + pr * 24, 1)
            lnv_p = fv(F_FSM + 351 + pr * 24, 1)
            EV("vector", "bn_stats", View(bnr_p.ap[:, 0:6], bnr_p.keys),
               [View(v.ap[:, 0:512], v.keys)])
            EV("vector", "bn_stats", View(bnr_p.ap[:, 6:12], bnr_p.keys),
               [View(v.ap[:, 512:1024], v.keys)])
            EV("vector", "bn_aggr", st2_p, [bnr_p])
            b.op("scalar", lambda e, o=lnv_p.ap, i=st2_p.ap[:, 1:2]:
                 e.activation(o, i, AF.Ln, bias=cn["epsc"][:, 0:1], scale=1.0),
                 reads=[st2_p, "c_epsc"], writes=[lnv_p])
            EV("scalar", "activation", rstd_p, [lnv_p], func=AF.Exp, scale=-0.5)
            return st2_p, rstd_p

        def gen_ychain(ttg):
            pr = ttg % 2
            ytp = yt
            xt = fv(F_X1T + pr * 1024, 1024)
            DMA(xt, g["x1tm"].ap()[s, ttg], reads=[f"x1tmD{s}_{ttg}"])
            st2_p, rstd_p = tm_rstd(xt, pr)
            b.op("vector", lambda e, o=ytp.ap, i=xt.ap, m=st2_p.ap[:, 0:1],
                 rs=rstd_p.ap: e.tensor_scalar(out=o, in0=i, scalar1=m, scalar2=rs,
                                               op0=ALU.subtract, op1=ALU.mult),
                 reads=[xt, st2_p, rstd_p], writes=[ytp])
            EV("vector", "tensor_tensor", ytp, [ytp, accs[ttg]], op=ALU.add)
            yield
            st2_p, rstd_p = tm_rstd(ytp, pr)
            b.op("vector", lambda e, o=ytp.ap, m=st2_p.ap[:, 0:1], rs=rstd_p.ap:
                 e.tensor_scalar(out=o, in0=o, scalar1=m, scalar2=rs,
                                 op0=ALU.subtract, op1=ALU.mult),
                 reads=[ytp, st2_p, rstd_p], writes=[ytp])
            EV("gpsimd", "tensor_tensor", ytp, [ytp, fv(F_GAMB, 1024)], op=ALU.mult)
            EV("gpsimd", "tensor_tensor", ytp, [ytp, fv(F_BETB, 1024)], op=ALU.add)
            yield
            st2_p, rstd_p = tm_rstd(ytp, pr)
            b.op("vector", lambda e, o=ytp.ap, m=st2_p.ap[:, 0:1], rs=rstd_p.ap:
                 e.tensor_scalar(out=o, in0=o, scalar1=m, scalar2=rs,
                                 op0=ALU.subtract, op1=ALU.mult),
                 reads=[ytp, st2_p, rstd_p], writes=[ytp])
            EV("vector", "tensor_tensor", xt, [ytp, xt], op=ALU.add)
            DMA(View(g["out"].ap()[s, ttg * 128:(ttg + 1) * 128, :],
                     [f"outD{s}_{ttg}"]), xt)
            yield

        def gen_block(it, e_, ch, maska):
            for hf in range(4):
                wv = bfv(O_W1R + hf * 1024, 1024)
                DMA(wv, g["w1s"].ap()[e_, hf], reads=[f"w1s{e_}"])
            for hf in range(16):
                wv = bfv(O_W1R + (hf % 4) * 1024, 1024)
                par = hf % 2
                pa0, pk0 = psA(par * 1024, par * 1024 + 512)
                pa1, pk1 = psA(par * 1024 + 512, par * 1024 + 1024)
                for kc in range(8):
                    lhsv = View(wv.ap[:, kc * 128:(kc + 1) * 128], wv.keys)
                    for nh, (pa, pk) in enumerate([(pa0, pk0), (pa1, pk1)]):
                        MM(pa, pk[0], lhsv,
                           View(ln1t[kc].ap[:, ch * FC + nh * 512:
                                            ch * FC + (nh + 1) * 512],
                                ln1t[kc].keys),
                           kc == 0, kc == 7)
                b.op("scalar", lambda e, o=hts[hf].ap,
                     i=ps["A"][:, par * 1024:(par + 1) * 1024],
                     bi=cn["b1r"][:, e_ * 16 + hf:e_ * 16 + hf + 1]:
                     e.activation(o, i, AF.Relu, bias=bi),
                     reads=[pk0[0], pk1[0], "c_b1r"], writes=[hts[hf]])
                if hf + 4 < 16:
                    wv2 = bfv(O_W1R + ((hf + 4) % 4) * 1024, 1024)
                    DMA(wv2, g["w1s"].ap()[e_, hf + 4], reads=[f"w1s{e_}"])
                yield
            for tt in range(8):
                ttg = ch * 8 + tt
                par = tt % 2
                if par == 0:
                    poap = ps["B"]
                    pks = ("psB0", "psB1")
                else:
                    poap = ps["A"][:, 0:1024]
                    pks = ("psA0", "psA1")
                for hf in range(16):
                    lhsv = View(hts[hf].ap[:, tt * 128:(tt + 1) * 128],
                                hts[hf].keys)
                    for nh in range(2):
                        b.op("tensor", lambda e,
                             o=poap[:, nh * 512:(nh + 1) * 512],
                             l=lhsv.ap,
                             r=w2ts[hf].ap[:, nh * 512:(nh + 1) * 512],
                             st=(hf == 0), sp=(hf == 15):
                             e.matmul(o, l, r, start=st, stop=sp),
                             reads=[lhsv, w2ts[hf]], writes=[pks[nh]])
                scol = maska.ap[:, ttg * 8 + e_:ttg * 8 + e_ + 1]
                b.op("vector", lambda e, o=accs[ttg].ap, i=poap[:, 0:1024],
                     sc=scol: e.scalar_tensor_tensor(
                         out=o, in0=i, scalar=sc, in1=o,
                         op0=ALU.mult, op1=ALU.add),
                     reads=[pks[0], pks[1], accs[ttg], maska],
                     writes=[accs[ttg]])
                yield

        def gen_epi(it, ch, maskT):
            for tt in range(8):
                ttg = ch * 8 + tt
                for nh in range(2):
                    oap, ok = psC(nh * 512, (nh + 1) * 512)
                    b.op("tensor", lambda e, o=oap,
                         l=maskT.ap[:, ttg * 128:(ttg + 1) * 128],
                         r=cn["b2"][:, nh * 512:(nh + 1) * 512]:
                         e.matmul(o, l, r, start=True, stop=True),
                         reads=[maskT, "c_b2"], writes=[ok[0]])
                b.op("vector", lambda e, o=accs[ttg].ap, i=ps["C"][:, 0:1024]:
                     e.tensor_tensor(o, o, i, op=ALU.add),
                     reads=["psC0", "psC1", accs[ttg]], writes=[accs[ttg]])
                yield
                if it == 0:
                    for ft in range(8):
                        par = ft % 2
                        oap, ok = psC(par * 512, par * 512 + 128)
                        b.op("tensor", lambda e, o=oap,
                             i=accs[ttg].ap[:, ft * 128:(ft + 1) * 128],
                             idn=cn["identf"][:, :]: e.transpose(o, i, idn),
                             reads=[accs[ttg], "c_identf"], writes=[ok[0]])
                        b.op("vector", lambda e,
                             o=ln1t[ft].ap[:, ttg * 128:(ttg + 1) * 128],
                             i=oap: e.scalar_tensor_tensor(
                                 out=o, in0=i, scalar=0.5,
                                 in1=o, op0=ALU.mult, op1=ALU.add),
                             reads=[ok[0], ln1t[ft]], writes=[ln1t[ft]])
                        if ft == 3:
                            yield
                    b.op("vector", lambda e, o=accs[ttg].ap,
                         sc=v0a.ap[:, ttg:ttg + 1]: e.tensor_scalar_mul(o, o, sc),
                         reads=[accs[ttg], v0a], writes=[accs[ttg]])
                    yield
                else:
                    for _ in gen_ychain(ttg):
                        yield

        for it in range(2):
            maska, maskT = (m0a, m0T) if it == 0 else (c1a, c1T)
            for e_ in range(E):
                for hf in range(16):
                    DMA(w2ts[hf], g["w2s"].ap()[e_, hf * 128:(hf + 1) * 128, :],
                        reads=[f"w2s{e_}"])
                if e_ < E - 1:
                    _interleave(gen_block(it, e_, 0, maska))
                    _interleave(gen_block(it, e_, 1, maska))
                else:
                    _interleave(gen_block(it, e_, 0, maska))
                    if it == 0 and s == 0:
                        tap("acc0", accs[0].ap, [accs[0]])
                    _interleave(gen_epi(it, 0, maskT), gen_block(it, e_, 1, maska))
                    _interleave(gen_epi(it, 1, maskT))

    emit_startup()
    for s in range(nseq):
        if 'A1' in phases:
            emit_attention(s)
        if 'FFN' in phases:
            emit_ffn(s)
    b.op("sync", lambda e: e.nop(),
         reads=[f"outD{s}_{t}" for s in range(SEQS) for t in range(16)] + tap_keys)

    b.finalize_and_emit(es)
    nc._keepalive = es
    return nc


def kernel(**inputs):
    in_maps = _host_inputs(inputs)
    nc = build_program()
    res = run_bass_kernel_spmd(nc, in_maps, core_ids=list(range(NCORES)))
    out = np.concatenate([r["out"] for r in res.results], axis=0)
    return np.ascontiguousarray(out.reshape(16, 2048, 1024))


# version 11 (final)
# speedup vs baseline: 1.0607x; 1.0215x over previous
import contextlib
import numpy as np
import ml_dtypes

import concourse.bass as bass
import concourse.mybir as mybir
from concourse.bass_utils import run_bass_kernel_spmd

F32 = mybir.dt.float32
BF16 = mybir.dt.bfloat16
ALU = mybir.AluOpType
AF = mybir.ActivationFunctionType
AX = mybir.AxisListType.X

NCORES = 8
SEQS = 2
T = 2048
D = 1024
NH = 16
DH = 64
LAT = 256
E = 8
HID = 2048
EPS = 1e-5
NEGBIG = -1e30
QC = 512
NQC = T // QC
FC = 1024
NFC = T // FC

ENGINES = ("sync", "tensor", "vector", "scalar", "gpsimd")
GRAN = 512


class View:
    __slots__ = ("ap", "keys")

    def __init__(self, ap, keys):
        self.ap = ap
        self.keys = tuple(keys)


class Arena:
    def __init__(self, nc, es, name, cols, dt):
        self.name = name
        self.t = es.enter_context(nc.sbuf_tensor(name, [128, cols], dt))

    def v(self, off, ncols, rows=(0, 128)):
        ap = self.t[rows[0]:rows[1], off:off + ncols]
        c0, c1 = off // GRAN, (off + ncols - 1) // GRAN
        return View(ap, [(self.name, i) for i in range(c0, c1 + 1)])


def K(x):
    if isinstance(x, View):
        return list(x.keys)
    return [x]


class Builder:
    def __init__(self, nc):
        self.nc = nc
        self.recs = []
        self.lastw = {}
        self.lastr = {}

    def op(self, eng, fn, reads=(), writes=(), dma=False):
        rk = [k for x in reads for k in K(x)]
        wk = [k for x in writes for k in K(x)]
        idx = len(self.recs)
        deps = set()
        for r in rk:
            if r in self.lastw:
                deps.add(self.lastw[r])
        for w in wk:
            if w in self.lastw:
                deps.add(self.lastw[w])
            for i2 in self.lastr.get(w, {}).values():
                deps.add(i2)
        self.recs.append(dict(idx=idx, eng=eng, fn=fn, deps=deps, dma=dma,
                              inc=False, ordn=None, lane_key=None))
        for r in rk:
            self.lastr.setdefault(r, {})[eng] = idx
        for w in wk:
            self.lastw[w] = idx
            self.lastr[w] = {eng: idx}
        return idx

    def finalize_and_emit(self, es):
        nc = self.nc
        recs = self.recs
        NLANE = 8
        qcount = {}
        for rec in recs:
            if rec["dma"]:
                q = rec["eng"]
                n = qcount.get(q, 0)
                qcount[q] = n + 1
                rec["lane_key"] = ("dl", q, n % NLANE)
                rec["ordn"] = n // NLANE + 1
        for rec in recs:
            kept = {}
            for d in rec["deps"]:
                p = recs[d]
                if (p["eng"] == "tensor" and rec["eng"] == "tensor"
                        and not p["dma"] and not rec["dma"]):
                    continue
                key = p["lane_key"] if p["dma"] else (p["eng"], False)
                kept[key] = max(kept.get(key, -1), d)
            rec["edges"] = kept
            for d in kept.values():
                recs[d]["inc"] = True
        ccount = {}
        for rec in recs:
            if rec["dma"]:
                rec["inc"] = True
            elif rec["inc"]:
                key = (rec["eng"], False)
                ccount[key] = ccount.get(key, 0) + 1
                rec["ordn"] = ccount[key]
        sems = {}
        for q, n in qcount.items():
            for l in range(min(n, NLANE)):
                sems[("dl", q, l)] = es.enter_context(nc.semaphore(f"d_{q}_{l}"))
        for key in ccount:
            sems[key] = es.enter_context(nc.semaphore("c_" + key[0]))
        for rec in recs:
            waits = []
            for key, d in rec["edges"].items():
                p = recs[d]
                if p["ordn"] is None:
                    continue
                waits.append((key, sems[key], p["ordn"] * (16 if p["dma"] else 1)))
            if rec["dma"] and rec["ordn"] > 1:
                waits.append((rec["lane_key"], sems[rec["lane_key"]],
                              (rec["ordn"] - 1) * 16))
            rec["waits"] = waits
            if rec["inc"]:
                rec["sem"] = sems[rec["lane_key"] if rec["dma"]
                                  else (rec["eng"], False)]
                rec["iv"] = 16 if rec["dma"] else 1

        block = es.enter_context(nc.Block())
        per_eng = {e: [r for r in recs if r["eng"] == e] for e in ENGINES}

        def mk(eng_name):
            lst = per_eng[eng_name]

            def section(eng):
                hi = {}
                for rec in lst:
                    for key, sem, val in rec["waits"]:
                        if hi.get(key, 0) >= val:
                            continue
                        eng.wait_ge(sem, val)
                        hi[key] = val
                    inst = rec["fn"](eng)
                    if rec["inc"]:
                        inst.then_inc(rec["sem"], rec["iv"])
            return section

        for name in ENGINES:
            if per_eng[name]:
                getattr(block, name)(mk(name))


def _colblk(w):
    Kd, Md = w.shape
    t = w.reshape(Kd // 128, 128, Md // 128, 128).transpose(2, 1, 0, 3)
    return np.ascontiguousarray(t.reshape(Md // 128, 128, Kd))


def _host_inputs(inp):
    f32 = np.float32
    x = np.asarray(inp["L0_attn_x"], f32)
    shared = {}
    swapi = np.arange(1024)
    swapi = (swapi // 64) * 64 + ((swapi % 64) + 32) % 64
    wq_ = np.asarray(inp["L0_attn_W_q"], f32)
    shared["wq"] = _colblk(wq_)
    shared["wqw"] = _colblk(np.ascontiguousarray(wq_[:, swapi]))
    shared["wdkv"] = _colblk(np.asarray(inp["L0_attn_W_dkv"], f32))
    wuk_ = np.asarray(inp["L0_attn_W_uk"], f32)
    shared["wuk"] = _colblk(wuk_)
    shared["wukw"] = _colblk(np.ascontiguousarray(wuk_[:, swapi]))
    shared["wuv"] = _colblk(np.asarray(inp["L0_attn_W_uv"], f32))
    shared["wo"] = _colblk(np.asarray(inp["L0_attn_W_o"], f32))
    shared["latg"] = np.ascontiguousarray(
        np.asarray(inp["L0_attn_ln_gamma"], f32).reshape(2, 128).T)
    shared["latb"] = np.ascontiguousarray(
        np.asarray(inp["L0_attn_ln_beta"], f32).reshape(2, 128).T)
    gw = np.asarray(inp["L0_ffn_gate_W"], f32)
    shared["gwr"] = np.ascontiguousarray(
        gw.reshape(8, 128, 8).transpose(1, 0, 2).reshape(128, 64))
    shared["gbt"] = np.ascontiguousarray(
        np.tile(np.asarray(inp["L0_ffn_gate_b"], f32)[None, :], (128, 1)))
    w1 = np.asarray(inp["L0_ffn_W1"], f32)
    shared["w1h"] = np.stack([_colblk(w1[e]) for e in range(E)])
    b1 = np.asarray(inp["L0_ffn_b1"], f32)
    shared["b1r"] = np.ascontiguousarray(
        b1.reshape(8, 16, 128).transpose(2, 0, 1).reshape(128, 128))
    shared["w2h"] = np.ascontiguousarray(np.asarray(inp["L0_ffn_W2"], f32))
    shared["b2"] = np.ascontiguousarray(np.asarray(inp["L0_ffn_b2"], f32))
    shared["gamb"] = np.ascontiguousarray(
        np.tile(np.asarray(inp["L0_ffn_ln_gamma"], f32)[None, :], (128, 1)))
    shared["betb"] = np.ascontiguousarray(
        np.tile(np.asarray(inp["L0_ffn_ln_beta"], f32)[None, :], (128, 1)))

    pos = np.arange(T, dtype=f32)
    inv = (f32(1.0) / (f32(10000.0) ** (np.arange(32, dtype=f32) / f32(32.0)))).astype(f32)
    ang = pos[None, :] * inv[:, None]
    cos, sin = np.cos(ang).astype(f32), np.sin(ang).astype(f32)
    CS = np.empty((128, T), f32)
    SS = np.empty((128, T), f32)
    for r in range(128):
        rr = r % 64
        if rr < 32:
            CS[r], SS[r] = cos[rr], -sin[rr]
        else:
            CS[r], SS[r] = cos[rr - 32], sin[rr - 32]
    shared["csss"] = np.ascontiguousarray(np.concatenate([CS, SS], axis=1))
    qi = np.arange(128)[:, None]
    kj = np.arange(128)[None, :]
    shared["triA"] = np.where(kj > qi, f32(NEGBIG), f32(0.0)).astype(f32)
    shared["triQ"] = (kj >= qi).astype(ml_dtypes.bfloat16)
    bd4 = np.zeros((4, QC), ml_dtypes.bfloat16)
    for j in range(4):
        bd4[j, j * 128:(j + 1) * 128] = 1
    shared["bd4"] = bd4
    shared["ones4"] = np.ones((4, 128), ml_dtypes.bfloat16)
    shared["identf"] = np.eye(128, dtype=f32)
    shared["identb"] = np.eye(128, dtype=ml_dtypes.bfloat16)
    shared["onesf"] = np.ones((128, 1), f32)
    shared["epsc"] = np.full((128, 1), EPS, f32)
    shared["onesb"] = np.ones((128, 1), ml_dtypes.bfloat16)
    shared["ones1"] = np.ones((1, 128), f32)

    in_maps = []
    for c in range(NCORES):
        d = dict(shared)
        d["x"] = np.ascontiguousarray(x[2 * c:2 * c + 2].transpose(0, 2, 1))
        in_maps.append(d)
    return in_maps


def build_program(dbg=(), phases=('A1', 'A2', 'FFN'), nseq=SEQS):
    nc = bass.Bass("TRN2", target_bir_lowering=False)
    g = {}

    def dram(name, shape, dt=F32, kind="ExternalInput"):
        g[name] = nc.dram_tensor(name, list(shape), dt, kind=kind)

    dram("x", [SEQS, D, T])
    for n, sh in [("wq", [8, 128, 1024]), ("wqw", [8, 128, 1024]),
                  ("wukw", [8, 128, 256]), ("wdkv", [2, 128, 1024]),
                  ("wuk", [8, 128, 256]), ("wuv", [8, 128, 256]),
                  ("wo", [8, 128, 1024]), ("latg", [128, 2]), ("latb", [128, 2]),
                  ("gwr", [128, 64]), ("gbt", [128, 8]),
                  ("w1h", [8, 16, 128, 1024]), ("b1r", [128, 128]),
                  ("w2h", [8, 2048, 1024]), ("b2", [8, 1024]),
                  ("gamb", [128, 1024]), ("betb", [128, 1024]),
                  ("csss", [128, 2 * T]), ("triA", [128, 128]),
                  ("identf", [128, 128]), ("onesf", [128, 1]), ("ones1", [1, 128]),
                  ("epsc", [128, 1])]:
        dram(n, sh)
    for n, sh in [("triQ", [128, 128]), ("bd4", [4, QC]), ("ones4", [4, 128]),
                  ("identb", [128, 128]), ("onesb", [128, 1])]:
        dram(n, sh, BF16)
    dram("out", [SEQS, T, D], F32, kind="ExternalOutput")
    dram("w1s", [8, 16, 128, 1024], BF16, kind="Internal")
    dram("w2s", [8, 2048, 1024], BF16, kind="Internal")
    dram("wqs", [8, 128, 1024], BF16, kind="Internal")
    dram("wos", [8, 128, 1024], BF16, kind="Internal")
    dram("wdkvs", [2, 128, 1024], BF16, kind="Internal")
    dram("wuks", [8, 128, 256], BF16, kind="Internal")
    dram("wqws", [8, 128, 1024], BF16, kind="Internal")
    dram("wukws", [8, 128, 256], BF16, kind="Internal")
    dram("wuvs", [8, 128, 256], BF16, kind="Internal")
    dram("x1tm", [SEQS, 16, 128, D], F32, kind="Internal")
    dram("ln1s", [SEQS, 8, 128, T], BF16, kind="Internal")

    dbg_bufs = {}
    for name in dbg:
        shape, dt = dbg[name] if isinstance(dbg, dict) else (None, None)
        dram("dbg_" + name, shape, dt, kind="ExternalOutput")
        dbg_bufs[name] = g["dbg_" + name]

    es = contextlib.ExitStack()
    ABF = Arena(nc, es, "ABF", 57344, BF16)
    AF32 = Arena(nc, es, "AF32", 21504 + 512, F32)

    cn = {}
    for n, sh, dt in [("triA", [128, 128], F32), ("identf", [128, 128], F32),
                      ("onesf", [128, 1], F32), ("ones1", [1, 128], F32),
                      ("epsc", [128, 1], F32),
                      ("b1r", [128, 128], F32), ("gbt", [128, 8], F32),
                      ("latg", [128, 2], F32), ("latb", [128, 2], F32),
                      ("triQ", [128, 128], BF16), ("bd4", [4, QC], BF16),
                      ("ones4", [4, 128], BF16), ("identb", [128, 128], BF16), ("onesb", [128, 1], BF16),
                      ("b2", [8, 1024], BF16), ("gw", [128, 64], BF16)]:
        cn[n] = es.enter_context(nc.sbuf_tensor("c_" + n, sh, dt))

    def C(n):
        return View(cn[n][:, :], ["c_" + n])

    def Cs(n, sl):
        return View(cn[n][sl], ["c_" + n])

    def kbf(ft, cols=slice(0, T)):
        return ABF.v(ft * T, T).ap[:, cols], [("ABF", i) for i in range(
            (ft * T) // GRAN, (ft * T + T - 1) // GRAN + 1)]

    def bfv(off, n, rows=(0, 128)):
        return ABF.v(off, n, rows)

    def fv(off, n, rows=(0, 128)):
        return AF32.v(off, n, rows)

    O_KBF, O_VTM, O_WRING, O_S1, O_S2 = 0, 16384, 32768, 38912, 43008
    O_QBF = 51200
    O_HBUF, O_W2T, O_W1R, O_M0T, O_C1T, O_LN1 = 0, 16384, 32768, 36864, 38912, 40960
    O_DSCR = 47104
    O_OFFS = 49152
    F_XA, F_AO, F_CS, F_SS = 0, 4096, 8192, 10240
    F_ROPE, F_RSW, F_SQ, F_LAT, F_STAGE, F_ROWS = 12288, 13312, 13824, 14336, 15360, 16384
    F_SM = 19456
    F_ACC, F_X1T, F_YT, F_GAMB, F_BETB = 0, 16384, 18432, 19456, 20480
    F_FSM = 21504

    b = Builder(nc)

    def DMA(dst, src, reads=(), writes=(), q="sync"):
        dap = dst.ap if isinstance(dst, View) else dst
        sap = src.ap if isinstance(src, View) else src
        rd = list(reads) + ([src] if isinstance(src, View) else [])
        wr = list(writes) + ([dst] if isinstance(dst, View) else [])
        b.op(q, lambda e, d=dap, s=sap: e.dma_start(out=d, in_=s),
             reads=rd, writes=wr, dma=True)

    def MM(out_ap, outk, lhs, rhs, start, stop):
        lap = lhs.ap if isinstance(lhs, View) else lhs
        rap = rhs.ap if isinstance(rhs, View) else rhs
        rd = [x for x in (lhs, rhs)]
        b.op("tensor", lambda e, o=out_ap, l=lap, r=rap, st=start, sp=stop:
             e.matmul(o, l, r, start=st, stop=sp),
             reads=rd, writes=[outk])

    def TR(out_ap, outk, in_v, ident):
        iap = in_v.ap if isinstance(in_v, View) else in_v
        b.op("tensor", lambda e, o=out_ap, i=iap, idn=ident.ap:
             e.transpose(o, i, idn), reads=[in_v, ident], writes=[outk])

    def EV(eng, meth, out, ins, reads=(), writes=(), **kw):
        oap = out.ap if isinstance(out, View) else out
        iaps = [i.ap if isinstance(i, View) else i for i in ins]
        rd = [i for i in ins if isinstance(i, View)] + list(reads)
        wr = ([out] if isinstance(out, View) else []) + list(writes)
        b.op(eng, lambda e, o=oap, ia=iaps, k=kw, m=meth:
             getattr(e, m)(o, *ia, **k), reads=rd, writes=wr)

    tap_keys = []

    def tap(name, src_ap, reads):
        if name in dbg_bufs:
            key = f"tapD_{name}_{len(tap_keys)}"
            tap_keys.append(key)
            b.op("sync", lambda e, d=dbg_bufs[name].ap(), s=src_ap:
                 e.dma_start(out=d, in_=s), reads=list(reads), writes=[key], dma=True)

    ps = {}
    ps["A"] = es.enter_context(nc.psum_tensor("psA", [128, 2048], F32))
    ps["B"] = es.enter_context(nc.psum_tensor("psB", [128, 1024], F32))
    ps["C"] = es.enter_context(nc.psum_tensor("psC", [128, 1024], F32))

    def psA(c0, c1, rows=(0, 128)):
        return ps["A"][rows[0]:rows[1], c0:c1], [f"psA{i}" for i in
                                                 range(c0 // 512, (c1 - 1) // 512 + 1)]

    def psB(c0, c1, rows=(0, 128)):
        return ps["B"][rows[0]:rows[1], c0:c1], [f"psB{i}" for i in
                                                 range(c0 // 512, (c1 - 1) // 512 + 1)]

    def psC(c0, c1, rows=(0, 128)):
        return ps["C"][rows[0]:rows[1], c0:c1], [f"psC{i}" for i in
                                                 range(c0 // 512, (c1 - 1) // 512 + 1)]

    def PV(appair):
        return View(appair[0], appair[1])

    def emit_startup():
        for n in ["triA", "identf", "onesf", "ones1", "epsc", "b1r", "gbt",
                  "latg", "latb", "triQ", "bd4", "ones4", "identb", "onesb"]:
            DMA(View(cn[n][:, :], ["c_" + n]), g[n].ap())
        for nm in ["wdkv", "wuk", "wukw", "wuv", "wq", "wqw", "wo"]:
            DMA(View(g[nm + "s"].ap()[:], [nm + "s"]), g[nm].ap()[:], q="gpsimd")
        DMA(View(cn["b2"][:, :], ["c_b2"]), g["b2"].ap(), q="gpsimd")
        DMA(View(cn["gw"][:, :], ["c_gw"]), g["gwr"].ap(), q="gpsimd")
        for e_ in range(E):
            DMA(View(g["w1s"].ap()[e_], [f"w1s{e_}"]), g["w1h"].ap()[e_], q="gpsimd")
            DMA(View(g["w2s"].ap()[e_], [f"w2s{e_}"]), g["w2h"].ap()[e_], q="gpsimd")

    def fm_ln_rows(tiles, n, dim, rowoff):
        nt = len(tiles)
        sq2 = [fv(F_SQ, 512), fv(F_YT, 512)]
        EV("scalar", "activation", sq2[0], [tiles[0]], func=AF.Square)
        for i, tv in enumerate(tiles):
            if i + 1 < nt:
                EV("scalar", "activation", sq2[(i + 1) % 2], [tiles[i + 1]],
                   func=AF.Square)
            oap, ok = psA(1024, 1024 + n, rows=(0, 1))
            MM(oap, ok[0], C("onesf"), tv, i == 0, i == nt - 1)
            oap, ok = psA(1536, 1536 + n, rows=(0, 1))
            MM(oap, ok[0], C("onesf"), sq2[i % 2], i == 0, i == nt - 1)
        mu = fv(F_ROWS + rowoff, n, rows=(0, 1))
        ex2 = fv(F_ROWS + rowoff + n, n, rows=(0, 1))
        r = fv(F_ROWS + rowoff + 2 * n, n, rows=(0, 1))
        b.op("vector", lambda e, o=mu.ap, i=ps["A"][0:1, 1024:1024 + n]:
             e.tensor_scalar_mul(o, i, 1.0 / dim), reads=["psA2"], writes=[mu])
        b.op("vector", lambda e, o=ex2.ap, i=ps["A"][0:1, 1536:1536 + n]:
             e.tensor_scalar_mul(o, i, 1.0 / dim), reads=["psA3"], writes=[ex2])
        b.op("vector", lambda e, o=r.ap, m=mu.ap: e.scalar_tensor_tensor(
            out=o, in0=m, scalar=-1.0, in1=m, op0=ALU.mult, op1=ALU.mult),
            reads=[mu], writes=[r])
        EV("vector", "tensor_tensor", r, [r, ex2], op=ALU.add)
        b.op("scalar", lambda e, o=r.ap: e.activation(o, o, AF.Ln, bias=cn["epsc"][0:1, 0:1], scale=1.0), reads=[r, "c_epsc"], writes=[r])
        EV("scalar", "activation", r, [r], func=AF.Exp, scale=-0.5)
        return mu, r

    def fm_bcast(row, bank):
        oap, ok = psA(bank * 512, bank * 512 + 512)
        MM(oap, ok[0], C("ones1"), row, True, True)
        return PV(psA(bank * 512, bank * 512 + 512))

    def ln_apply_fm(xv, mub, rb, outv, tmpv):
        EV("vector", "tensor_tensor", tmpv, [xv, mub], op=ALU.subtract)
        EV("vector", "tensor_tensor", outv, [tmpv, rb], op=ALU.mult)

    def rope_evac2(psv, psw, dst, colsl):
        t1 = fv(F_ROPE, 512)
        t2 = fv(F_ROPE + 512, 512)
        csv = View(AF32.t[:, F_CS + colsl.start:F_CS + colsl.stop],
                   AF32.v(F_CS, 2048).keys)
        ssv = View(AF32.t[:, F_SS + colsl.start:F_SS + colsl.stop],
                   AF32.v(F_SS, 2048).keys)
        b.op("vector", lambda e, o=t1.ap, i=psv.ap, c=csv.ap:
             e.scalar_tensor_tensor(out=o, in0=i, scalar=1.0, in1=c,
                                    op0=ALU.mult, op1=ALU.mult),
             reads=[psv, csv], writes=[t1])
        b.op("vector", lambda e, o=t2.ap, i=psw.ap, c=ssv.ap:
             e.scalar_tensor_tensor(out=o, in0=i, scalar=1.0, in1=c,
                                    op0=ALU.mult, op1=ALU.mult),
             reads=[psw, ssv], writes=[t2])
        EV("vector", "tensor_tensor", dst, [t1, t2], op=ALU.add)

    import os
    RS = int(os.environ.get("RS", "9"))
    NRM = int(os.environ.get("NRM", "4"))

    def rope_evac(psv, dst, colsl):
        if RS < 1:
            EV("vector", "tensor_copy", dst, [psv])
            return
        t1 = fv(F_ROPE, 512)
        t2 = fv(F_ROPE + 512, 512)
        rsw = fv(F_RSW, 512)
        csv = View(AF32.t[:, F_CS + colsl.start:F_CS + colsl.stop],
                   AF32.v(F_CS, 2048).keys)
        ssv = View(AF32.t[:, F_SS + colsl.start:F_SS + colsl.stop],
                   AF32.v(F_SS, 2048).keys)
        b.op("vector", lambda e, o=t1.ap, i=psv.ap, c=csv.ap:
             e.scalar_tensor_tensor(out=o, in0=i, scalar=1.0, in1=c,
                                    op0=ALU.mult, op1=ALU.mult),
             reads=[psv, csv], writes=[t1])
        if RS < 2:
            EV("vector", "tensor_copy", dst, [t1])
            return
        import os as _os
        _SWENG = _os.environ.get("SWENG", "vector")
        for dstr, srcr in [(0, 32), (32, 0), (64, 96), (96, 64)]:
            if _SWENG == "vector":
                b.op("vector", lambda e, o=rsw.ap[dstr:dstr + 32, :],
                     i=psv.ap[srcr:srcr + 32, :]: e.tensor_copy(o, i),
                     reads=[psv], writes=[rsw])
            else:
                b.op("scalar", lambda e, o=rsw.ap[dstr:dstr + 32, :],
                     i=psv.ap[srcr:srcr + 32, :]: e.activation(o, i, AF.Copy),
                     reads=[psv], writes=[rsw])
        if RS < 3:
            EV("vector", "tensor_copy", dst, [t1])
            return
        if RS >= 4:
            EV("gpsimd", "tensor_tensor", t2, [rsw, ssv], op=ALU.mult)
        else:
            EV("vector", "tensor_tensor", t2, [rsw, ssv], op=ALU.mult)
        EV("vector", "tensor_tensor", dst, [t1, t2], op=ALU.add)

    def emit_attention(s):
        xs = g["x"].ap()[s]
        for pp in range(2):
            kb = fv(F_SM + pp * 512, 512)
            EV("gpsimd", "memset", kb, [], constant=0.0)
        DMA(fv(F_CS, 2 * T), g["csss"].ap())

        import os
        A1S = int(os.environ.get("A1S", "9"))
        A1C = int(os.environ.get("A1C", "4"))
        for c in range(A1C):
            cs = slice(c * QC, (c + 1) * QC)
            xts = []
            for ft in range(8):
                xt = fv(F_XA + ft * 512, 512)
                DMA(xt, xs[ft * 128:(ft + 1) * 128, cs])
                xts.append(xt)
            if A1S < 1:
                continue
            mu, r = fm_ln_rows(xts, 512, D, 0)
            if A1S < 2:
                continue
            mub = fm_bcast(mu, 0)
            rb = fm_bcast(r, 1)
            ln0 = [bfv(O_S1 + ft * 512, 512) for ft in range(8)]
            tmp = fv(F_ROPE, 512)
            for ft in range(8):
                ln_apply_fm(xts[ft], mub, rb, ln0[ft], tmp)
            if A1S < 3:
                continue
            for of in range(2):
                w = bfv(O_WRING + of * 1024, 1024)
                DMA(w, g["wdkvs"].ap()[of], reads=["wdkvs"])
                for kc in range(8):
                    oap, ok = psB(of * 512, (of + 1) * 512)
                    MM(oap, ok[0], View(w.ap[:, kc * 128:(kc + 1) * 128], w.keys),
                       ln0[kc], kc == 0, kc == 7)
            latf = [fv(F_LAT + of * 512, 512) for of in range(2)]
            for of in range(2):
                EV("vector", "tensor_copy", latf[of], [PV(psB(of * 512, (of + 1) * 512))])
            if A1S < 4:
                continue
            mu2, r2 = fm_ln_rows(latf, 512, LAT, 1536)
            mub2 = fm_bcast(mu2, 0)
            rb2 = fm_bcast(r2, 1)
            for of in range(2):
                t = fv(F_ROPE, 512)
                EV("vector", "tensor_tensor", t, [latf[of], mub2], op=ALU.subtract)
                t2 = fv(F_ROPE + 512, 512)
                EV("vector", "tensor_tensor", t2, [t, rb2], op=ALU.mult)
                latn_t = bfv(O_S2 + of * T, T)
                dstv = View(latn_t.ap[:, cs], latn_t.keys)
                b.op("vector", lambda e, o=dstv.ap, i=t2.ap,
                     g1=cn["latg"][:, of:of + 1], b1=cn["latb"][:, of:of + 1]:
                     e.tensor_scalar(out=o, in0=i, scalar1=g1, scalar2=b1,
                                     op0=ALU.mult, op1=ALU.add),
                     reads=[t2, "c_latg", "c_latb"], writes=[dstv])
            if A1S < 5:
                continue
            kv_list = [("k", "wuks", None)] + ([("v", "wuvs", None)] if A1S >= 6 else [])
            for which, wsname, dsttile in kv_list:
                for of in range(8):
                    par = of % 2
                    if which == "k":
                        wsl = bfv(O_WRING + (of % 2) * 1024, 256)
                        DMA(wsl, g["wuks"].ap()[of], reads=["wuks"])
                        wsw = bfv(O_WRING + 2048 + (of % 2) * 1024, 256)
                        DMA(wsw, g["wukws"].ap()[of], reads=["wukws"])
                        for kc in range(2):
                            latn_t = bfv(O_S2 + kc * T, T)
                            oap, ok = psB(0, 512)
                            MM(oap, ok[0], View(wsl.ap[:, kc * 128:(kc + 1) * 128], wsl.keys),
                               View(latn_t.ap[:, cs], latn_t.keys), kc == 0, kc == 1)
                            oap, ok = psB(512, 1024)
                            MM(oap, ok[0], View(wsw.ap[:, kc * 128:(kc + 1) * 128], wsw.keys),
                               View(latn_t.ap[:, cs], latn_t.keys), kc == 0, kc == 1)
                        kt_ = bfv(O_KBF + of * T, T)
                        rope_evac2(PV(psB(0, 512)), PV(psB(512, 1024)),
                                   View(kt_.ap[:, cs], kt_.keys), cs)
                        continue
                    wsl = bfv(O_WRING + (of % 4) * 1024, 256)
                    DMA(wsl, g[wsname].ap()[of], reads=[wsname])
                    for kc in range(2):
                        latn_t = bfv(O_S2 + kc * T, T)
                        oap, ok = psB(par * 512, (par + 1) * 512)
                        MM(oap, ok[0], View(wsl.ap[:, kc * 128:(kc + 1) * 128], wsl.keys),
                           View(latn_t.ap[:, cs], latn_t.keys), kc == 0, kc == 1)
                    pv = PV(psB(par * 512, (par + 1) * 512))
                    if False:
                        pass
                    else:
                        vb = fv(F_AO + par * 512, 512)
                        EV("scalar", "activation", vb, [pv], func=AF.Copy)
                        for blk in range(4):
                            kt = c * 4 + blk
                            tb = blk % 2
                            oap, ok = psC(tb * 512, tb * 512 + 128)
                            b.op("tensor", lambda e, o=oap,
                                 i=vb.ap[:, blk * 128:(blk + 1) * 128],
                                 idn=cn["identf"][:, :]: e.transpose(o, i, idn),
                                 reads=[vb, "c_identf"], writes=[ok[0]])
                            vtm_t = bfv(O_VTM + kt * 1024, 1024)
                            dv = View(vtm_t.ap[:, of * 128:(of + 1) * 128], vtm_t.keys)
                            EV("vector", "tensor_copy", dv, [PV(psC(tb * 512, tb * 512 + 128))])
            if c == 0:
                tap("ln0", ln0[0].ap, [ln0[0]])
                tap("latn", bfv(O_S2, T).ap[:, 0:512], [bfv(O_S2, T)])

        for qc in range(NQC if 'A2' in phases else 0):
            qs = slice(qc * QC, (qc + 1) * QC)
            xts = []
            for ft in range(8):
                xt = fv(F_XA + ft * 512, 512)
                DMA(xt, xs[ft * 128:(ft + 1) * 128, qs])
                xts.append(xt)
            mu, r = fm_ln_rows(xts, 512, D, 0)
            mub = fm_bcast(mu, 0)
            rb = fm_bcast(r, 1)
            ln0 = [bfv(O_S1 + ft * 512, 512) for ft in range(8)]
            tmp = fv(F_ROPE, 512)
            for ft in range(8):
                ln_apply_fm(xts[ft], mub, rb, ln0[ft], tmp)
            for of in range(8):
                w = bfv(O_WRING + (of % 2) * 1024, 1024)
                DMA(w, g["wqs"].ap()[of], reads=["wqs"])
                ww = bfv(O_WRING + 2048 + (of % 2) * 1024, 1024)
                DMA(ww, g["wqws"].ap()[of], reads=["wqws"])
                for kc in range(8):
                    oap, ok = psB(0, 512)
                    MM(oap, ok[0], View(w.ap[:, kc * 128:(kc + 1) * 128], w.keys),
                       ln0[kc], kc == 0, kc == 7)
                    oap, ok = psB(512, 1024)
                    MM(oap, ok[0], View(ww.ap[:, kc * 128:(kc + 1) * 128], ww.keys),
                       ln0[kc], kc == 0, kc == 7)
                qb = bfv(O_QBF + of * 512, 512)
                rope_evac2(PV(psB(0, 512)), PV(psB(512, 1024)), qb, qs)
            if qc == 0:
                tap("qbf", bfv(O_QBF, 512).ap, [bfv(O_QBF, 512)])

            prevB = None
            for h in range(NH):
                ga = _head_passA(s, qc, h)
                _interleave(prevB, ga)
                prevB = _head_passB(s, qc, h)
            _interleave(prevB)

            _window_tail(s, qc, qs, xs)

    def _head_passA(s, qc, h):
        par_h = h % 2
        SMo = F_SM + par_h * 512
        hp, ho = h // 2, (h % 2) * 64
        qtile = bfv(O_QBF + hp * 512, 512)
        ktile = bfv(O_KBF + hp * T, T)
        nmax = fv(SMo, 4)
        k0v = fv(SMo + 20, 1)
        k0b = fv(SMo + 21, 1)
        offc2 = fv(SMo + 32, 32)
        for ql in range(4):
            qt = qc * 4 + ql
            ext = (qt + 1) * 128
            lhs = View(qtile.ap[ho:ho + 64, ql * 128:(ql + 1) * 128], qtile.keys)
            nch = (ext + 511) // 512
            for cc in range(nch):
                n0, n1 = cc * 512, min(ext, (cc + 1) * 512)
                oap, ok = psA(n0, n1)
                MM(oap, ok[0], lhs, View(ktile.ap[ho:ho + 64, n0:n1], ktile.keys),
                   True, True)
            b.op("vector", lambda e, o=nmax.ap[:, ql:ql + 1], i=ps["A"][:, 0:ext]:
                 e.reduce_max(o, i, axis=AX, negate=True),
                 reads=[f"psA{i}" for i in range(nch)], writes=[nmax])
            yield
        k0rbuf = fv(SMo + 288, 128, rows=(0, 32))
        k0t = fv(SMo + 256, 32)
        b.op("vector", lambda e, o=k0rbuf.ap[0:1, :], i=cn["ones1"][0:1, :],
             sc=nmax.ap[0:1, 0:1]: e.tensor_single_scalar(o, i, sc, op=ALU.mult),
             reads=["c_ones1", nmax], writes=[k0rbuf])
        for j in range(4):
            b.op("vector", lambda e, o=k0t.ap[j * 32:(j + 1) * 32, 0:32],
                 i=k0rbuf.ap[0:32, j * 32:(j + 1) * 32]: e.transpose(o, i),
                 reads=[k0rbuf], writes=[k0t])
        k0v = View(k0t.ap[:, 0:1], k0t.keys)
        b.op("vector", lambda e, o=k0b.ap, i=k0v.ap: e.tensor_scalar_mul(o, i, 0.125),
             reads=[k0v], writes=[k0b])
        b.op("vector", lambda e, o=offc2.ap[:, 0:4], i=nmax.ap, sc=k0v.ap:
             e.tensor_scalar_sub(o, i, sc), reads=[nmax, k0v], writes=[offc2])
        yield
        obuf = fv(SMo + 128, 128, rows=(0, 32))
        for j in range(4):
            b.op("vector", lambda e, o=obuf.ap[0:32, j * 32:(j + 1) * 32],
                 i=offc2.ap[j * 32:(j + 1) * 32, 0:32]: e.transpose(o, i),
                 reads=[offc2], writes=[obuf])
        offsTb = View(obuf.ap[0:4, :], obuf.keys)
        masked = bfv(O_OFFS + par_h * 1024 + 128, 512, rows=(0, 4))
        for j in range(4):
            b.op("gpsimd", lambda e, o=masked.ap[:, j * 128:(j + 1) * 128],
                 i=offsTb.ap, bd=cn["bd4"][:, j * 128:(j + 1) * 128]:
                 e.tensor_tensor(o, i, bd, op=ALU.mult),
                 reads=[offsTb, "c_bd4"], writes=[masked])
        yield

    def _head_passB(s, qc, h):
        par_h = h % 2
        SMo = F_SM + par_h * 512
        hp, ho = h // 2, (h % 2) * 64
        qtile = bfv(O_QBF + hp * 512, 512)
        ktile = bfv(O_KBF + hp * T, T)
        k0b = fv(SMo + 21, 1)
        masked = bfv(O_OFFS + par_h * 1024 + 128, 512, rows=(0, 4))
        nkt = (qc + 1) * 4
        srow = fv(F_ROWS + 2048 + par_h * 512, 512, rows=(0, 1))
        rb64 = fv(F_ROPE + par_h * 512, 512, rows=(0, 64))

        def emit_sT(kt):
            qlo = max(qc * QC, kt * 128)
            n = qc * QC + QC - qlo
            par = kt % 2
            pbap, pbk = psB(par * 512, par * 512 + n)
            MM(pbap, pbk[0],
               View(ktile.ap[ho:ho + 64, kt * 128:(kt + 1) * 128], ktile.keys),
               View(qtile.ap[ho:ho + 64, QC - n:QC], qtile.keys), True, False)
            MM(pbap, pbk[0], C("ones4"),
               View(masked.ap[:, QC - n:QC], masked.keys), False, True)
            return (kt, n, par, pbap, pbk)

        def emit_tail(st):
            kt, n, par, pbap, pbk = st
            pt = bfv(O_S1 + 2048 + par * 512, 512)
            b.op("scalar", lambda e, o=pt.ap[:, 0:n], i=pbap, bi=k0b.ap:
                 e.activation(o, i, AF.Exp, scale=0.125, bias=bi),
                 reads=[pbk[0], k0b], writes=[pt])
            if kt * 128 >= qc * QC:
                b.op("vector", lambda e, o=pt.ap[:, 0:128], t=cn["triQ"][:, :]:
                     e.tensor_tensor(o, o, t, op=ALU.mult),
                     reads=[pt, "c_triQ"], writes=[pt])
            vtm_t = bfv(O_VTM + kt * 1024, 1024)
            avap, avk = psC(512 + QC - n, 512 + QC, rows=(0, 64))
            b.op("tensor", lambda e, o=avap,
                 l=vtm_t.ap[:, hp * 128 + ho:hp * 128 + ho + 64],
                 r=pt.ap[:, 0:n], st=(kt == 0), sp=(kt == nkt - 1):
                 e.matmul(o, l, r, start=st, stop=sp),
                 reads=[vtm_t, pt], writes=["psC1"])
            if NRM >= 1:
                oap, ok = psC(QC - n, QC, rows=(0, 1))
                MM(oap, ok[0], C("onesb"), View(pt.ap[:, 0:n], pt.keys),
                   kt == 0, kt == nkt - 1)

        prev = None
        for kt in range(nkt):
            st = emit_sT(kt)
            if prev is not None:
                emit_tail(prev)
            prev = st
            yield
        emit_tail(prev)
        if NRM >= 2:
            b.op("vector", lambda e, o=srow.ap, i=ps["C"][0:1, 0:QC]:
                 e.tensor_copy(o, i), reads=["psC0"], writes=[srow])
            EV("scalar", "activation", srow, [srow], func=AF.Ln)
            EV("scalar", "activation", srow, [srow], func=AF.Exp, scale=-1.0)
        if NRM >= 3:
            oap, ok = psC(0, QC, rows=(0, 64))
            b.op("tensor", lambda e, o=oap, l=cn["ones1"][0:1, 0:64], r=srow.ap:
                 e.matmul(o, l, r, start=True, stop=True),
                 reads=["c_ones1", srow], writes=["psC0"])
            b.op("vector", lambda e, o=rb64.ap, i=ps["C"][0:64, 0:QC]:
                 e.tensor_copy(o, i), reads=["psC0"], writes=[rb64])
        yield
        afm = bfv(O_S2 + hp * 512, 512)
        if NRM >= 4:
            b.op("vector", lambda e, o=afm.ap[ho:ho + 64, :], i=ps["C"][0:64, 512:1024],
                 rb=rb64.ap: e.scalar_tensor_tensor(
                     out=o, in0=i, scalar=1.0, in1=rb, op0=ALU.mult, op1=ALU.mult),
                 reads=["psC1", rb64], writes=[afm])
        else:
            b.op("vector", lambda e, o=afm.ap[ho:ho + 64, :], i=ps["C"][0:64, 512:1024]:
                 e.tensor_copy(o, i), reads=["psC1"], writes=[afm])
        yield

    def _interleave(*gens):
        live = [g for g in gens if g is not None]
        while live:
            for gi in list(live):
                try:
                    next(gi)
                except StopIteration:
                    live.remove(gi)

    def _window_tail(s, qc, qs, xs):
        aos = []
        for of in range(8):
            w = bfv(O_WRING + (of % 4) * 1024, 1024)
            DMA(w, g["wos"].ap()[of], reads=["wos"])
            par = of % 2
            for kc in range(8):
                afm = bfv(O_S2 + kc * 512, 512)
                oap, ok = psB(par * 512, (par + 1) * 512)
                MM(oap, ok[0], View(w.ap[:, kc * 128:(kc + 1) * 128], w.keys),
                   afm, kc == 0, kc == 7)
            ao = fv(F_AO + of * 512, 512)
            EV("vector", "tensor_copy", ao, [PV(psB(par * 512, (par + 1) * 512))])
            aos.append(ao)
        mu, r = fm_ln_rows(aos, 512, D, 0)
        mub = fm_bcast(mu, 0)
        rb = fm_bcast(r, 1)
        xts = []
        for ft in range(8):
            xt = fv(F_XA + ft * 512, 512)
            DMA(xt, xs[ft * 128:(ft + 1) * 128, qs])
            xts.append(xt)
        for ft in range(8):
            EV("vector", "tensor_tensor", aos[ft], [aos[ft], mub], op=ALU.subtract)
            EV("vector", "tensor_tensor", aos[ft], [aos[ft], rb], op=ALU.mult)
            EV("vector", "tensor_tensor", xts[ft], [xts[ft], aos[ft]], op=ALU.add)
        if qc == 0:
            tap("x1", xts[0].ap, [xts[0]])
        mu1, r1 = fm_ln_rows(xts, 512, D, 0)
        mub1 = fm_bcast(mu1, 0)
        rb1 = fm_bcast(r1, 1)
        for ft in range(8):
            EV("vector", "tensor_tensor", aos[ft], [xts[ft], mub1], op=ALU.subtract)
            lt = bfv(O_S2 + ft * 512, 512)
            EV("vector", "tensor_tensor", lt, [aos[ft], rb1], op=ALU.mult)
            DMA(View(g["ln1s"].ap()[s, ft, :, qs], [f"ln1D{s}_{ft}_{qc}"]), lt)
        stage = fv(F_STAGE, 1024)
        for tt in range(4):
            ttg = qc * 4 + tt
            for ft in range(8):
                par = ft % 2
                oap, ok = psC(par * 512 + 128, par * 512 + 256)
                b.op("tensor", lambda e, o=oap,
                     i=xts[ft].ap[:, tt * 128:(tt + 1) * 128],
                     idn=cn["identf"][:, :]: e.transpose(o, i, idn),
                     reads=[xts[ft], "c_identf"], writes=[ok[0]])
                dv = View(stage.ap[:, ft * 128:(ft + 1) * 128], stage.keys)
                b.op("vector", lambda e, o=dv.ap, i=oap: e.tensor_copy(o, i),
                     reads=[ok[0]], writes=[dv])
            DMA(View(g["x1tm"].ap()[s, ttg], [f"x1tmD{s}_{ttg}"]), stage)

    def emit_ffn(s):
        DMA(fv(F_GAMB, 1024), g["gamb"].ap())
        DMA(fv(F_BETB, 1024), g["betb"].ap())
        accs = [fv(F_ACC + tt * 1024, 1024) for tt in range(16)]
        for tt in range(16):
            EV("gpsimd", "memset", accs[tt], [], constant=0.0)
        ln1t = [bfv(O_LN1 + kc * T, T) for kc in range(8)]
        for kc in range(8):
            for ch in range(NFC):
                DMA(View(ln1t[kc].ap[:, ch * FC:(ch + 1) * FC], ln1t[kc].keys),
                    g["ln1s"].ap()[s, kc, :, ch * FC:(ch + 1) * FC],
                    reads=[f"ln1D{s}_{kc}_{ch * 2}", f"ln1D{s}_{kc}_{ch * 2 + 1}"])
        gb = fv(F_FSM, 8)
        gt = fv(F_FSM + 8, 8)
        grow = fv(F_FSM + 16, 1)
        m0a = fv(F_FSM + 32, 128)
        c1a = fv(F_FSM + 160, 128)
        v0a = fv(F_FSM + 288, 16)
        v1a = fv(F_FSM + 304, 16)
        bnr = fv(F_FSM + 320, 12)
        st2 = fv(F_FSM + 332, 2)
        rstd = fv(F_FSM + 334, 1)
        lnv = fv(F_FSM + 335, 1)
        for tt in range(16):
            for kc in range(8):
                oap, ok = psB(0, 8)
                MM(oap, ok[0],
                   View(ln1t[kc].ap[:, tt * 128:(tt + 1) * 128], ln1t[kc].keys),
                   Cs("gw", (slice(None), slice(kc * 8, (kc + 1) * 8))),
                   kc == 0, kc == 7)
            b.op("vector", lambda e, o=gb.ap, i=ps["B"][:, 0:8],
                 t=cn["gbt"][:, :]: e.tensor_tensor(o, i, t, op=ALU.add),
                 reads=["psB0", "c_gbt"], writes=[gb])
            EV("vector", "reduce_max", grow, [gb], axis=AX)
            EV("vector", "tensor_scalar_sub", gb, [gb, grow])
            EV("scalar", "activation", gb, [gb], func=AF.Exp)
            EV("vector", "reduce_sum", grow, [gb], axis=AX)
            EV("vector", "reciprocal", grow, [grow])
            EV("vector", "tensor_scalar_mul", gb, [gb, grow])
            ov0 = View(v0a.ap[:, tt:tt + 1], v0a.keys)
            EV("vector", "reduce_max", ov0, [gb], axis=AX)
            om0 = View(m0a.ap[:, tt * 8:(tt + 1) * 8], m0a.keys)
            b.op("vector", lambda e, o=om0.ap, i=gb.ap, sc=ov0.ap:
                 e.tensor_single_scalar(o, i, sc, op=ALU.is_ge),
                 reads=[gb, v0a], writes=[m0a])
            b.op("vector", lambda e, o=gt.ap, m=om0.ap, i=gb.ap:
                 e.scalar_tensor_tensor(out=o, in0=m, scalar=NEGBIG, in1=i,
                                        op0=ALU.mult, op1=ALU.add),
                 reads=[m0a, gb], writes=[gt])
            ov1 = View(v1a.ap[:, tt:tt + 1], v1a.keys)
            EV("vector", "reduce_max", ov1, [gt], axis=AX)
            oc1 = View(c1a.ap[:, tt * 8:(tt + 1) * 8], c1a.keys)
            b.op("vector", lambda e, o=oc1.ap, i=gt.ap, sc=ov1.ap:
                 e.tensor_single_scalar(o, i, sc, op=ALU.is_ge),
                 reads=[gt, v1a], writes=[c1a])
            b.op("vector", lambda e, o=oc1.ap, i=oc1.ap, sc=ov1.ap:
                 e.tensor_scalar_mul(o, i, sc), reads=[c1a, v1a], writes=[c1a])
            for src, srck, dsto in [(m0a, "m0", O_M0T), (c1a, "c1", O_C1T)]:
                oap, ok = psB(0, 128, rows=(0, 8))
                b.op("tensor", lambda e, o=oap,
                     i=src.ap[:, tt * 8:(tt + 1) * 8], idn=cn["identf"][:, :]:
                     e.transpose(o, i, idn),
                     reads=[src, "c_identf"], writes=[ok[0]])
                dT = bfv(dsto, 2048, rows=(0, 8))
                b.op("vector", lambda e, o=dT.ap[:, tt * 128:(tt + 1) * 128],
                     i=ps["B"][0:8, 0:128]: e.tensor_copy(o, i),
                     reads=["psB0"], writes=[dT])
        if s == 0:
            tap("gates", m0a.ap, [m0a])

        m0T = bfv(O_M0T, 2048, rows=(0, 8))
        c1T = bfv(O_C1T, 2048, rows=(0, 8))
        w2ts = [bfv(O_W2T + hf * 1024, 1024) for hf in range(16)]
        hts = [bfv(O_HBUF + hf * 1024, 1024) for hf in range(16)]

        yt = fv(F_YT, 1024)

        def tm_rstd(v, pr):
            bnr_p = fv(F_FSM + 336 + pr * 24, 12)
            st2_p = fv(F_FSM + 348 + pr * 24, 2)
            rstd_p = fv(F_FSM + 350 + pr * 24, 1)
            lnv_p = fv(F_FSM + 351 + pr * 24, 1)
            EV("vector", "bn_stats", View(bnr_p.ap[:, 0:6], bnr_p.keys),
               [View(v.ap[:, 0:512], v.keys)])
            EV("vector", "bn_stats", View(bnr_p.ap[:, 6:12], bnr_p.keys),
               [View(v.ap[:, 512:1024], v.keys)])
            EV("vector", "bn_aggr", st2_p, [bnr_p])
            b.op("scalar", lambda e, o=lnv_p.ap, i=st2_p.ap[:, 1:2]:
                 e.activation(o, i, AF.Ln, bias=cn["epsc"][:, 0:1], scale=1.0),
                 reads=[st2_p, "c_epsc"], writes=[lnv_p])
            EV("scalar", "activation", rstd_p, [lnv_p], func=AF.Exp, scale=-0.5)
            return st2_p, rstd_p

        def gen_ychain(ttg):
            pr = ttg % 2
            ytp = yt
            xt = fv(F_X1T + pr * 1024, 1024)
            DMA(xt, g["x1tm"].ap()[s, ttg], reads=[f"x1tmD{s}_{ttg}"])
            st2_p, rstd_p = tm_rstd(xt, pr)
            b.op("vector", lambda e, o=ytp.ap, i=xt.ap, m=st2_p.ap[:, 0:1],
                 rs=rstd_p.ap: e.tensor_scalar(out=o, in0=i, scalar1=m, scalar2=rs,
                                               op0=ALU.subtract, op1=ALU.mult),
                 reads=[xt, st2_p, rstd_p], writes=[ytp])
            EV("vector", "tensor_tensor", ytp, [ytp, accs[ttg]], op=ALU.add)
            yield
            st2_p, rstd_p = tm_rstd(ytp, pr)
            b.op("vector", lambda e, o=ytp.ap, m=st2_p.ap[:, 0:1], rs=rstd_p.ap:
                 e.tensor_scalar(out=o, in0=o, scalar1=m, scalar2=rs,
                                 op0=ALU.subtract, op1=ALU.mult),
                 reads=[ytp, st2_p, rstd_p], writes=[ytp])
            EV("gpsimd", "tensor_tensor", ytp, [ytp, fv(F_GAMB, 1024)], op=ALU.mult)
            EV("gpsimd", "tensor_tensor", ytp, [ytp, fv(F_BETB, 1024)], op=ALU.add)
            yield
            st2_p, rstd_p = tm_rstd(ytp, pr)
            b.op("vector", lambda e, o=ytp.ap, m=st2_p.ap[:, 0:1], rs=rstd_p.ap:
                 e.tensor_scalar(out=o, in0=o, scalar1=m, scalar2=rs,
                                 op0=ALU.subtract, op1=ALU.mult),
                 reads=[ytp, st2_p, rstd_p], writes=[ytp])
            EV("vector", "tensor_tensor", xt, [ytp, xt], op=ALU.add)
            DMA(View(g["out"].ap()[s, ttg * 128:(ttg + 1) * 128, :],
                     [f"outD{s}_{ttg}"]), xt)
            yield

        def gen_block(it, e_, ch, maska):
            for hf in range(4):
                wv = bfv(O_W1R + hf * 1024, 1024)
                DMA(wv, g["w1s"].ap()[e_, hf], reads=[f"w1s{e_}"])
            for hf in range(16):
                if ch == 0 and hf < 8:
                    for j in (2 * hf, 2 * hf + 1):
                        DMA(w2ts[j], g["w2s"].ap()[e_, j * 128:(j + 1) * 128, :],
                            reads=[f"w2s{e_}"], q="scalar")
                wv = bfv(O_W1R + (hf % 4) * 1024, 1024)
                par = hf % 2
                pa0, pk0 = psA(par * 1024, par * 1024 + 512)
                pa1, pk1 = psA(par * 1024 + 512, par * 1024 + 1024)
                for kc in range(8):
                    lhsv = View(wv.ap[:, kc * 128:(kc + 1) * 128], wv.keys)
                    for nh, (pa, pk) in enumerate([(pa0, pk0), (pa1, pk1)]):
                        MM(pa, pk[0], lhsv,
                           View(ln1t[kc].ap[:, ch * FC + nh * 512:
                                            ch * FC + (nh + 1) * 512],
                                ln1t[kc].keys),
                           kc == 0, kc == 7)
                b.op("scalar", lambda e, o=hts[hf].ap,
                     i=ps["A"][:, par * 1024:(par + 1) * 1024],
                     bi=cn["b1r"][:, e_ * 16 + hf:e_ * 16 + hf + 1]:
                     e.activation(o, i, AF.Relu, bias=bi),
                     reads=[pk0[0], pk1[0], "c_b1r"], writes=[hts[hf]])
                if hf + 4 < 16:
                    wv2 = bfv(O_W1R + ((hf + 4) % 4) * 1024, 1024)
                    DMA(wv2, g["w1s"].ap()[e_, hf + 4], reads=[f"w1s{e_}"])
                yield
            for tt in range(8):
                ttg = ch * 8 + tt
                par = tt % 2
                if par == 0:
                    poap = ps["B"]
                    pks = ("psB0", "psB1")
                else:
                    poap = ps["A"][:, 0:1024]
                    pks = ("psA0", "psA1")
                for hf in range(16):
                    lhsv = View(hts[hf].ap[:, tt * 128:(tt + 1) * 128],
                                hts[hf].keys)
                    for nh in range(2):
                        b.op("tensor", lambda e,
                             o=poap[:, nh * 512:(nh + 1) * 512],
                             l=lhsv.ap,
                             r=w2ts[hf].ap[:, nh * 512:(nh + 1) * 512],
                             st=(hf == 0), sp=(hf == 15):
                             e.matmul(o, l, r, start=st, stop=sp),
                             reads=[lhsv, w2ts[hf]], writes=[pks[nh]])
                scol = maska.ap[:, ttg * 8 + e_:ttg * 8 + e_ + 1]
                b.op("vector", lambda e, o=accs[ttg].ap, i=poap[:, 0:1024],
                     sc=scol: e.scalar_tensor_tensor(
                         out=o, in0=i, scalar=sc, in1=o,
                         op0=ALU.mult, op1=ALU.add),
                     reads=[pks[0], pks[1], accs[ttg], maska],
                     writes=[accs[ttg]])
                yield

        def gen_epi(it, ch, maskT):
            for tt in range(8):
                ttg = ch * 8 + tt
                for nh in range(2):
                    oap, ok = psC(nh * 512, (nh + 1) * 512)
                    b.op("tensor", lambda e, o=oap,
                         l=maskT.ap[:, ttg * 128:(ttg + 1) * 128],
                         r=cn["b2"][:, nh * 512:(nh + 1) * 512]:
                         e.matmul(o, l, r, start=True, stop=True),
                         reads=[maskT, "c_b2"], writes=[ok[0]])
                b.op("vector", lambda e, o=accs[ttg].ap, i=ps["C"][:, 0:1024]:
                     e.tensor_tensor(o, o, i, op=ALU.add),
                     reads=["psC0", "psC1", accs[ttg]], writes=[accs[ttg]])
                yield
                if it == 0:
                    for ft in range(8):
                        par = ft % 2
                        oap, ok = psC(par * 512, par * 512 + 128)
                        b.op("tensor", lambda e, o=oap,
                             i=accs[ttg].ap[:, ft * 128:(ft + 1) * 128],
                             idn=cn["identf"][:, :]: e.transpose(o, i, idn),
                             reads=[accs[ttg], "c_identf"], writes=[ok[0]])
                        b.op("vector", lambda e,
                             o=ln1t[ft].ap[:, ttg * 128:(ttg + 1) * 128],
                             i=oap: e.scalar_tensor_tensor(
                                 out=o, in0=i, scalar=0.5,
                                 in1=o, op0=ALU.mult, op1=ALU.add),
                             reads=[ok[0], ln1t[ft]], writes=[ln1t[ft]])
                        if ft == 3:
                            yield
                    b.op("vector", lambda e, o=accs[ttg].ap,
                         sc=v0a.ap[:, ttg:ttg + 1]: e.tensor_scalar_mul(o, o, sc),
                         reads=[accs[ttg], v0a], writes=[accs[ttg]])
                    yield
                else:
                    for _ in gen_ychain(ttg):
                        yield

        for it in range(2):
            maska, maskT = (m0a, m0T) if it == 0 else (c1a, c1T)
            for e_ in range(E):
                if e_ < E - 1:
                    _interleave(gen_block(it, e_, 0, maska))
                    _interleave(gen_block(it, e_, 1, maska))
                else:
                    _interleave(gen_block(it, e_, 0, maska))
                    if it == 0 and s == 0:
                        tap("acc0", accs[0].ap, [accs[0]])
                    _interleave(gen_epi(it, 0, maskT), gen_block(it, e_, 1, maska))
                    _interleave(gen_epi(it, 1, maskT))

    emit_startup()
    for s in range(nseq):
        if 'A1' in phases:
            emit_attention(s)
        if 'FFN' in phases:
            emit_ffn(s)
    b.op("sync", lambda e: e.nop(),
         reads=[f"outD{s}_{t}" for s in range(SEQS) for t in range(16)] + tap_keys)

    b.finalize_and_emit(es)
    nc._keepalive = es
    return nc


def kernel(**inputs):
    in_maps = _host_inputs(inputs)
    nc = build_program()
    res = run_bass_kernel_spmd(nc, in_maps, core_ids=list(range(NCORES)))
    out = np.concatenate([r["out"] for r in res.results], axis=0)
    return np.ascontiguousarray(out.reshape(16, 2048, 1024))
